# version 17 (final)
# baseline (speedup 1.0000x reference)
import math
from contextlib import ExitStack

import numpy as np

import concourse.bass as bass
import concourse.mybir as mybir
from concourse.bass_utils import run_bass_kernel_spmd

F32 = mybir.dt.float32
BF16 = mybir.dt.bfloat16
AF = mybir.ActivationFunctionType
ALU = mybir.AluOpType
AX = mybir.AxisListType

D = 1024
B_TOTAL = 16
T = 2048
N_CORES = 8

FF_HEADS = 8
FF_DH = 64
FF_WINDOW = 512
FF_INNER = 512
FF_HID = 2730
FF_QK_SCALE = 8.0
LA_HEADS = 16
LA_DH = 64
FP4_MID = [0.25, 0.75, 1.25, 1.75, 2.5, 3.5, 5.0]
FP4_DELTA = [0.5, 0.5, 0.5, 0.5, 1.0, 1.0, 2.0]

NEG_BIG = -3.0e38

DEV_T = None
DEV_SLOTS = (1, 2, 3, 4)
DEV_NSEQ = None
DEV_NCORES = None


def _t():
    return DEV_T or T


def _nseq():
    return DEV_NSEQ or (B_TOTAL // (DEV_NCORES or N_CORES))



def _rotary_tables(n):
    dh = FF_DH
    half = np.arange(0, dh, 2, dtype=np.float64)
    inv_freq = 1.0 / 10000.0 ** (half / dh)
    t = np.arange(n, dtype=np.float64)
    freqs = t[:, None] * inv_freq[None, :]
    freqs = np.concatenate([freqs, freqs], axis=-1)
    base = (half + 0.4 * dh) / (1.4 * dh)
    power = (t - n // 2) / float(FF_WINDOW // 2)
    scale = base[None, :] ** power[:, None]
    scale = np.concatenate([scale, scale], axis=-1)
    cos, sin = np.cos(freqs), np.sin(freqs)
    c1, s1 = cos * scale, sin * scale
    c2, s2 = cos / scale, sin / scale
    s1f = s1.copy(); s1f[:, : dh // 2] *= -1.0
    s2f = s2.copy(); s2f[:, : dh // 2] *= -1.0
    out = {}
    for name, arr in (("rc1", c1), ("rs1", s1f), ("rc2", c2), ("rs2", s2f)):
        hi = arr.astype(np.float32)
        lo = (arr - hi.astype(np.float64)).astype(np.float32)
        out[name + "_hi"] = hi
        out[name + "_lo"] = lo
    return out


def _masks():
    i = np.arange(128)[:, None]
    c = np.arange(128)[None, :]
    u = np.where(c > i, NEG_BIG, 0.0).astype(np.float32)
    l = np.where(c < i, NEG_BIG, 0.0).astype(np.float32)
    return u, l



class Sem:
    __slots__ = ("h", "n", "name", "sw")
    _uid = [0]

    def __init__(self, stk, nc, name):
        Sem._uid[0] += 1
        self.name = f"{name}_{Sem._uid[0]}"
        self.h = stk.enter_context(nc.semaphore(self.name))
        self.n = 0
        self.sw = False


class Bld:
    def __init__(self, nc):
        self.nc = nc
        self.stk = ExitStack()
        self.te, self.ve, self.se, self.ge, self.sy = (
            nc.tensor, nc.vector, nc.scalar, nc.gpsimd, nc.sync)
        self.q_sy = Sem(self.stk, nc, "q_sy")
        self.q_se = Sem(self.stk, nc, "q_se")
        self.q_ve = Sem(self.stk, nc, "q_ve")
        self.q_ge = Sem(self.stk, nc, "q_ge")
        self.t_te = Sem(self.stk, nc, "t_te")
        self.t_ve = Sem(self.stk, nc, "t_ve")
        self.t_se = Sem(self.stk, nc, "t_se")
        self.t_ge = Sem(self.stk, nc, "t_ge")
        self._all = [self.q_sy, self.q_se, self.q_ve, self.q_ge,
                     self.t_te, self.t_ve, self.t_se, self.t_ge]
        self._waited = {}
        self._pool_hw = []
        self._pool_sw = []
        self.bank_marks = [dict() for _ in range(8)]
        self._engs = [self.te, self.ve, self.se, self.ge, self.sy]

    def wait(self, eng, sem, val=None):
        val = sem.n if val is None else val
        if val <= 0:
            return
        key = (id(eng), sem.name)
        if self._waited.get(key, -1) >= val:
            return
        eng.wait_ge(sem.h, val)
        self._waited[key] = val

    def inc(self, instr, sem, k=1):
        instr.then_inc(sem.h, k)
        sem.n += k
        return sem.n

    def v(self, i):
        return self.inc(i, self.t_ve)

    def s(self, i):
        return self.inc(i, self.t_se)

    def g(self, i):
        return self.inc(i, self.t_ge)

    def t(self, i):
        return self.inc(i, self.t_te)

    def chan(self, stk, name, sw=False):
        pool = self._pool_sw if sw else self._pool_hw
        if pool:
            return pool.pop()
        s = Sem(self.stk, self.nc, name)
        s.sw = sw
        self._all.append(s)
        return s

    def rmchan(self, *sems):
        for s in sems:
            (self._pool_sw if s.sw else self._pool_hw).append(s)

    def bmark(self, i, sem, val):
        d = self.bank_marks[i]
        if d.get(sem.name, (None, -1))[1] < val:
            d[sem.name] = (sem, val)

    def bwait(self, i):
        for sem, val in list(self.bank_marks[i].values()):
            self.wait(self.te, sem, val)

    def barrier(self):
        bar = Sem(self.stk, self.nc, "bar")
        for e in self._engs:
            for q in self._all:
                self.wait(e, q)
            e.nop().then_inc(bar.h, 1)
        for e in self._engs:
            e.wait_ge(bar.h, 5)
        for e in self._engs:
            for q in self._all:
                self._waited[(id(e), q.name)] = q.n



def split_hilo(b, dst_h, dst_l, src_ap):
    b.wait(b.ve, b.t_ve)
    b.v(b.ve.tensor_copy(dst_h, src_ap))
    b.wait(b.ve, b.t_ve)
    b.v(b.ve.tensor_tensor(out=dst_l, in0=src_ap, in1=dst_h, op=ALU.subtract))


def tp128s(b, sem, dst_ap, src_ap):
    b.inc(b.sy.dma_start_transpose(dst_ap, src_ap), sem, 16)


def mm3(b, out_ap, lh, ll, rh, rl, start, stop):
    b.te.matmul(out_ap, lh, rh, start=start, stop=False)
    b.te.matmul(out_ap, lh, rl, start=False, stop=False)
    return b.te.matmul(out_ap, ll, rh, start=False, stop=stop)


def load_w_hilo(b, stk, wf_stage, wh, wl, hbm_ap, n_chunks, chunk_cols,
                dst_col0=0):
    s_wf = b.chan(stk, "wf")
    s_wh = b.chan(stk, "whc", sw=True)
    last_split = 0
    src3 = hbm_ap.rearrange("(ci p) n -> p ci n", p=128)
    for ci in range(n_chunks):
        par = (ci % 2) * chunk_cols
        b.wait(b.se, b.t_ve)
        b.wait(b.ge, b.t_ve)
        b.inc(b.se.dma_start(out=wf_stage[:, par:par + chunk_cols],
                             in_=src3[:, ci]), s_wf, 16)
        b.inc(b.ge.dma_start(
            out=wh[:, dst_col0 + ci * chunk_cols: dst_col0 + (ci + 1) * chunk_cols],
            in_=src3[:, ci]), s_wh, 16)
        b.wait(b.ve, s_wf)
        b.wait(b.ve, s_wh)
        last_split = b.v(b.ve.tensor_tensor(
            out=wl[:, dst_col0 + ci * chunk_cols: dst_col0 + (ci + 1) * chunk_cols],
            in0=wf_stage[:, par:par + chunk_cols],
            in1=wh[:, dst_col0 + ci * chunk_cols: dst_col0 + (ci + 1) * chunk_cols],
            op=ALU.subtract))
    return [(s_wh, s_wh.n), (b.t_ve, last_split)]


def bias_rep_load(b, sem, dst, hbm_1d, width):
    b.inc(b.se.dma_start(out=dst,
                         in_=bass.AP(hbm_1d.tensor, hbm_1d.offset,
                                     [[0, 128], [1, width]])), sem, 16)


def layernorm_tile(b, out_ap, src_ap, scr_sq, scr_s, width=D):
    s_sum, s_negm, s_ss, s_sig, s_rsig = (scr_s[:, i:i + 1] for i in range(5))
    b.wait(b.ve, b.t_ve)
    b.v(b.ve.reduce_sum(s_sum, src_ap, axis=AX.X))
    b.wait(b.ve, b.t_ve)
    b.v(b.ve.tensor_scalar_mul(s_negm, s_sum, -1.0 / width))
    b.wait(b.se, b.t_ve)
    b.wait(b.se, b.t_se)
    b.s(b.se.activation(out=scr_sq, in_=src_ap, func=AF.Square,
                        bias=s_negm, scale=1.0, accum_out=s_ss))
    b.wait(b.se, b.t_se)
    b.s(b.se.activation(out=s_sig, in_=s_ss, func=AF.Sqrt,
                        bias=1e-5, scale=1.0 / width))
    b.wait(b.ve, b.t_se)
    b.v(b.ve.reciprocal(s_rsig, s_sig))
    b.wait(b.ve, b.t_ve)
    b.v(b.ve.tensor_scalar(out=out_ap, in0=src_ap,
                           scalar1=s_negm, scalar2=s_rsig,
                           op0=ALU.add, op1=ALU.mult))


def layernorm_quad(b, out_ap, src_ap, scr_xc, scr_sq, scr_s, ng=4):
    sums, negm, rst = scr_s[:, 0:ng], scr_s[:, ng:2 * ng], scr_s[:, 2 * ng:3 * ng]
    g3 = lambda t: t.rearrange("p (g c) -> p g c", c=D)
    b.wait(b.ve, b.t_ve)
    b.v(b.ve.reduce_sum(sums, g3(src_ap), axis=AX.X))
    b.wait(b.ve, b.t_ve)
    b.v(b.ve.tensor_scalar_mul(negm, sums, -1.0 / D))
    nb = negm.rearrange("p g -> p g ()").broadcast_to([128, ng, D])
    b.wait(b.ve, b.t_ve)
    b.v(b.ve.tensor_tensor(out=g3(scr_xc), in0=g3(src_ap), in1=nb, op=ALU.add))
    b.wait(b.se, b.t_ve)
    b.wait(b.se, b.t_se)
    b.s(b.se.activation(out=scr_sq, in_=scr_xc, func=AF.Square,
                        bias=0.0, scale=1.0))
    b.wait(b.ve, b.t_se)
    b.v(b.ve.reduce_sum(sums, g3(scr_sq), axis=AX.X))
    b.wait(b.se, b.t_ve)
    b.s(b.se.activation(out=negm, in_=sums, func=AF.Sqrt,
                        bias=1e-5, scale=1.0 / D))
    b.wait(b.ve, b.t_se)
    b.v(b.ve.reciprocal(rst, negm))
    rb = rst.rearrange("p g -> p g ()").broadcast_to([128, ng, D])
    b.wait(b.ve, b.t_ve)
    b.v(b.ve.tensor_tensor(out=g3(out_ap), in0=g3(scr_xc), in1=rb, op=ALU.mult))


def quant_tile(b, C, out_ap, src_ap, scr, width):
    nb = width // 16
    xn, absx, acc, tmpb = scr["xn"], scr["absx"], scr["acc"], scr["tmpb"]
    bmax, scale, hsc = scr["bmax"], scr["scale"], scr["hsc"]
    src3 = src_ap.rearrange("p (g s) -> p g s", s=16)
    b.wait(b.ve, b.t_ve)
    b.v(b.ve.reduce_max(bmax[:, :nb], src3, axis=AX.X,
                        apply_absolute_value=True))
    b.wait(b.ve, b.t_ve)
    b.v(b.ve.tensor_scalar(out=scale[:, :nb], in0=bmax[:, :nb],
                           scalar1=C["rsix"], scalar2=1e-12,
                           op0=ALU.mult, op1=ALU.max))
    b.wait(b.ve, b.t_ve)
    b.v(b.ve.reciprocal(bmax[:, :nb], scale[:, :nb]))
    rs_b = bmax[:, :nb].rearrange("p g -> p g ()").broadcast_to([128, nb, 16])
    b.wait(b.ve, b.t_ve)
    b.v(b.ve.tensor_tensor(out=xn[:, :width], in0=src_ap, in1=rs_b,
                           op=ALU.mult))
    b.wait(b.se, b.t_ve)
    b.wait(b.se, b.t_se)
    b.s(b.se.activation(out=absx[:, :width], in_=xn[:, :width], func=AF.Abs,
                        bias=0.0, scale=1.0))
    b.s(b.se.activation(out=tmpb[:, :width], in_=xn[:, :width], func=AF.Sign,
                        bias=0.0, scale=1.0))
    for j, (mid, wj) in enumerate(zip(FP4_MID, (1.0, 1.0, 1.0, 1.0, 2.0, 2.0, 4.0))):
        b.wait(b.ve, b.t_ve)
        b.wait(b.ve, b.t_se)
        if j == 0:
            b.v(b.ve.tensor_scalar(out=acc[:, :width], in0=absx[:, :width],
                                   scalar1=mid, scalar2=None, op0=ALU.is_gt))
        elif wj == 1.0:
            b.v(b.ve.scalar_tensor_tensor(out=acc[:, :width],
                                          in0=absx[:, :width], scalar=mid,
                                          in1=acc[:, :width],
                                          op0=ALU.is_gt, op1=ALU.add))
        else:
            b.v(b.ve.tensor_scalar(out=xn[:, :width], in0=absx[:, :width],
                                   scalar1=mid, scalar2=None, op0=ALU.is_gt))
            b.wait(b.ve, b.t_ve)
            b.v(b.ve.scalar_tensor_tensor(out=acc[:, :width],
                                          in0=xn[:, :width], scalar=wj,
                                          in1=acc[:, :width],
                                          op0=ALU.mult, op1=ALU.add))
    b.wait(b.ve, b.t_ve)
    b.v(b.ve.tensor_scalar_mul(hsc[:, :nb], scale[:, :nb], 0.5))
    hs_b = hsc[:, :nb].rearrange("p g -> p g ()").broadcast_to([128, nb, 16])
    b.wait(b.ve, b.t_ve)
    b.v(b.ve.tensor_tensor(out=acc[:, :width], in0=acc[:, :width],
                           in1=hs_b, op=ALU.mult))
    b.wait(b.ve, b.t_ve)
    b.v(b.ve.tensor_tensor(out=out_ap, in0=acc[:, :width],
                           in1=tmpb[:, :width], op=ALU.mult))


def tp_pe_split(b, C, bank, col, dst_h, dst_l, src_f32_ap):
    ps = C["PS"][bank]
    sl = ps[:, col * 128:(col + 1) * 128]
    b.wait(b.te, b.q_sy)
    b.t(b.te.transpose(sl, src_f32_ap, C["ident"][:]))
    b.wait(b.se, b.t_te)
    b.bmark(bank, b.t_se,
            b.s(b.se.activation(out=dst_h, in_=sl, func=AF.Identity,
                                bias=0.0, scale=1.0)))
    b.wait(b.ve, b.t_se)
    b.bmark(bank, b.t_ve,
            b.v(b.ve.scalar_tensor_tensor(out=dst_l, in0=sl, scalar=0.0,
                                          in1=dst_h, op0=ALU.add,
                                          op1=ALU.subtract)))


def plane(scr, u):
    return scr.rearrange("(u p) c -> p u c", p=128)[:, u]


def rows3(hbm, u, p=128):
    return hbm.rearrange("(u p) d -> p u d", p=p)[:, u]



def slot1(b, C, src, dst):
    nc, W, PS = C["nc"], C["W"], C["PS"]
    t_len, nt = C["t_len"], C["nt"]
    ng = nt // 4
    scr_qk = C["scr_qk"]
    scr_vt = C["scr_vt"]
    maskU = C["maskU"]

    with ExitStack() as pa:
        def sba(name, cols, dt=F32):
            return pa.enter_context(nc.sbuf_tensor(f"{name}A{Sem._uid[0]}", [128, cols], dt))
        wf = sba("wf", 2 * D)
        wh = {m: sba(f"w{m}h", 8 * D, BF16) for m in "qkv"}
        wl = {m: sba(f"w{m}l", 8 * D, BF16) for m in "qkv"}
        bqkv = sba("bqkv", 3 * 8)
        s_bias = b.chan(pa, "bias")
        wsems = []
        for mi, m in enumerate("qkv"):
            wsems += list(load_w_hilo(b, pa, wf, wh[m], wl[m], W["l0a_w" + m], 8, D))
            b.inc(b.se.dma_start(
                out=bqkv[:, mi * 8:(mi + 1) * 8],
                in_=W["l0a_b" + m].rearrange("(cc p) -> p cc", p=128),
                allow_slow_non_contiguous=True), s_bias, 16)

        xg = sba("xg", 4 * D)
        xfh = sba("xfh", 8 * 512, BF16)
        xfl = sba("xfl", 8 * 512, BF16)
        prh = sba("prh", 2 * 8 * 512, BF16)
        prl = sba("prl", 2 * 8 * 512, BF16)
        vtgh = sba("vtgh", 4 * D, BF16)
        vtgl = sba("vtgl", 4 * D, BF16)
        s_xg = b.chan(pa, "xg")
        s_spq = [b.chan(pa, "spqa", sw=True), b.chan(pa, "spqb", sw=True)]
        s_spv = b.chan(pa, "spv", sw=True)
        bv_rep = sba("bv_rep", D)
        bias_rep_load(b, s_bias, bv_rep[:], W["l0a_bv"], D)
        vtf = sba("vtf", 512)

        for g in range(ng):
            b.wait(b.se, b.t_te)
            b.inc(b.se.dma_start(
                out=xg.rearrange("p (tt d) -> p tt d", d=D),
                in_=src.rearrange("(tt p) d -> p tt d", p=128)[:, 4 * g:4 * g + 4]),
                s_xg, 16)
            b.wait(b.te, s_xg)
            for tt in range(4):
                for c in range(8):
                    bank = 6 + ((tt * 8 + c) % 2)
                    b.bwait(bank)
                    tp_pe_split(b, C, bank, 0,
                                xfh[:, c * 512 + tt * 128: c * 512 + tt * 128 + 128],
                                xfl[:, c * 512 + tt * 128: c * 512 + tt * 128 + 128],
                                xg[:, tt * D + c * 128: tt * D + c * 128 + 128])
            b.wait(b.te, b.t_se)
            b.wait(b.te, b.t_ve)
            for sem, val in wsems:
                b.wait(b.te, sem, val)
            for mi, m in enumerate("qk"):
                mo = mi * 8 * 512
                for cc in range(8):
                    ps = PS[cc % 2]
                    b.bwait(cc % 2)
                    for ci in range(8):
                        i = mm3(b, ps[:, :],
                                wh[m][:, ci * D + cc * 128: ci * D + cc * 128 + 128],
                                wl[m][:, ci * D + cc * 128: ci * D + cc * 128 + 128],
                                xfh[:, ci * 512:(ci + 1) * 512],
                                xfl[:, ci * 512:(ci + 1) * 512],
                                start=(ci == 0), stop=(ci == 7))
                    b.t(i)
                    oh = prh[:, mo + cc * 512: mo + (cc + 1) * 512]
                    ol = prl[:, mo + cc * 512: mo + (cc + 1) * 512]
                    b.wait(b.se, b.t_te)
                    b.wait(b.se, s_bias)
                    b.wait(b.se, s_spq[mi])
                    b.bmark(cc % 2, b.t_se,
                            b.s(b.se.activation(out=oh, in_=ps[:, :], func=AF.Identity,
                                                bias=bqkv[:, mi * 8 + cc: mi * 8 + cc + 1],
                                                scale=1.0)))
                    b.wait(b.ve, b.t_te)
                    b.wait(b.ve, b.t_se)
                    b.wait(b.ve, s_spq[mi])
                    b.bmark(cc % 2, b.t_ve,
                            b.v(b.ve.scalar_tensor_tensor(
                                out=ol, in0=ps[:, :],
                                scalar=bqkv[:, mi * 8 + cc: mi * 8 + cc + 1],
                                in1=oh, op0=ALU.add, op1=ALU.subtract)))
                base = (0 if m == "q" else 2 * ng)
                b.wait(b.ge, b.t_ve)
                b.wait(b.ge, b.t_se)
                b.inc(b.ge.dma_start(out=plane(scr_qk, base + 2 * g),
                                     in_=prh[:, mo:mo + 8 * 512]), s_spq[mi], 16)
                b.inc(b.ge.dma_start(out=plane(scr_qk, base + 2 * g + 1),
                                     in_=prl[:, mo:mo + 8 * 512]), s_spq[mi], 16)
            for tt in range(4):
                for half in range(2):
                    bk = 2 + (tt * 2 + half) % 2
                    b.bwait(bk)
                    ps = PS[bk]
                    for ci in range(8):
                        i = mm3(b, ps[:, :],
                                xfh[:, ci * 512 + tt * 128: ci * 512 + tt * 128 + 128],
                                xfl[:, ci * 512 + tt * 128: ci * 512 + tt * 128 + 128],
                                wh["v"][:, ci * D + half * 512: ci * D + half * 512 + 512],
                                wl["v"][:, ci * D + half * 512: ci * D + half * 512 + 512],
                                start=(ci == 0), stop=(ci == 7))
                    b.t(i)
                    b.wait(b.ve, b.t_te)
                    b.wait(b.ve, b.t_ve)
                    b.wait(b.ve, s_bias)
                    cnt = b.v(b.ve.tensor_tensor(
                        out=vtf[:], in0=ps[:, :],
                        in1=bv_rep[:, half * 512:(half + 1) * 512], op=ALU.add))
                    b.bmark(bk, b.t_ve, cnt)
                    b.wait(b.ve, s_spv)
                    split_hilo(b, vtgh[:, tt * D + half * 512: tt * D + (half + 1) * 512],
                               vtgl[:, tt * D + half * 512: tt * D + (half + 1) * 512],
                               vtf[:])
            b.wait(b.ge, b.t_ve)
            for tt in range(4):
                tok = 4 * g + tt
                b.inc(b.ge.dma_start(out=plane(scr_vt, 2 * tok),
                                     in_=vtgh[:, tt * D:(tt + 1) * D]),
                      s_spv, 16)
                b.inc(b.ge.dma_start(out=plane(scr_vt, 2 * tok + 1),
                                     in_=vtgl[:, tt * D:(tt + 1) * D]),
                      s_spv, 16)
        b.barrier()
        b.rmchan(s_bias, s_xg, s_spv, *s_spq)

    with ExitStack() as pb:
        def sb(name, cols, dt=F32):
            return pb.enter_context(nc.sbuf_tensor(f"{name}B{Sem._uid[0]}", [128, cols], dt))
        wf = sb("wf", 2 * D)
        woh = sb("woh", 8 * D, BF16)
        wol = sb("wol", 8 * D, BF16)
        wsems = list(load_w_hilo(b, pb, wf, woh, wol, W["l0a_wo"], 8, D))
        bo_rep = sb("bo_rep", D)
        s_bias = b.chan(pb, "bias")
        bias_rep_load(b, s_bias, bo_rep[:], W["l0a_bo"], D)

        qgh = sb("qgh", 8 * 512, BF16)
        qgl = sb("qgl", 8 * 512, BF16)
        kbh = sb("kbh", 2 * 8 * 512, BF16)
        kbl = sb("kbl", 2 * 8 * 512, BF16)
        vbh = sb("vbh", 2 * D, BF16)
        vbl = sb("vbl", 2 * D, BF16)
        s_sb = sb("s_sb", t_len)
        p_f32 = sb("p_f32", t_len)
        pth = sb("pth", nt * 128, BF16)
        ptl = sb("ptl", nt * 128, BF16)
        ogh = sb("ogh", 8 * 512, BF16)
        ogl = sb("ogl", 8 * 512, BF16)
        stats = sb("stats", 8)
        xr = sb("xr", D)
        xpb = sb("xpb", D)
        resid = sb("resid", D)
        outt = sb("outt", D)
        s_qg = b.chan(pb, "qg")
        s_kb = [b.chan(pb, "kba"), b.chan(pb, "kbb")]
        s_vb = [b.chan(pb, "vba"), b.chan(pb, "vbb")]
        s_xr = b.chan(pb, "xr")
        kb_marks = [0, 0]
        vb_marks = [0, 0]
        s_out = b.chan(pb, "sout", sw=True)

        for g in range(ng):
            b.wait(b.se, b.t_te)
            b.inc(b.se.dma_start(out=qgh[:], in_=plane(scr_qk, 2 * g)), s_qg, 16)
            b.inc(b.se.dma_start(out=qgl[:], in_=plane(scr_qk, 2 * g + 1)), s_qg, 16)
            for iq in range(4):
                i = 4 * g + iq
                L = (i + 1) * 128
                nkg = (L + 511) // 512
                for kg in range(nkg):
                    w_keys = min(512, L - kg * 512)
                    kpar = (kg % 2) * 8 * 512
                    b.wait(b.se, b.t_te, kb_marks[kg % 2])
                    b.inc(b.se.dma_start(out=kbh[:, kpar:kpar + 8 * 512],
                                         in_=plane(scr_qk, 2 * ng + 2 * kg)), s_kb[kg % 2], 16)
                    b.inc(b.se.dma_start(out=kbl[:, kpar:kpar + 8 * 512],
                                         in_=plane(scr_qk, 2 * ng + 2 * kg + 1)), s_kb[kg % 2], 16)
                    ps = PS[kg % 2]
                    b.wait(b.te, s_kb[kg % 2])
                    b.wait(b.te, s_qg)
                    b.wait(b.te, b.t_ve)
                    for ci in range(8):
                        ii = mm3(b, ps[:, :w_keys],
                                 qgh[:, ci * 512 + iq * 128: ci * 512 + iq * 128 + 128],
                                 qgl[:, ci * 512 + iq * 128: ci * 512 + iq * 128 + 128],
                                 kbh[:, kpar + ci * 512: kpar + ci * 512 + w_keys],
                                 kbl[:, kpar + ci * 512: kpar + ci * 512 + w_keys],
                                 start=(ci == 0), stop=(ci == 7))
                    kb_marks[kg % 2] = b.t(ii)
                    b.wait(b.ve, b.t_te)
                    b.v(b.ve.tensor_scalar_mul(
                        s_sb[:, kg * 512: kg * 512 + w_keys], ps[:, :w_keys],
                        1.0 / 32.0))
                b.wait(b.ve, b.t_ve)
                b.v(b.ve.tensor_tensor(out=s_sb[:, L - 128:L],
                                       in0=s_sb[:, L - 128:L],
                                       in1=maskU[:], op=ALU.add))
                b.wait(b.ve, b.t_ve)
                b.v(b.ve.reduce_max(stats[:, 0:1], s_sb[:, :L], axis=AX.X,
                                    negate=True))
                b.wait(b.se, b.t_ve)
                b.wait(b.se, b.t_se)
                b.wait(b.se, b.t_te)
                b.s(b.se.activation(out=p_f32[:, :L], in_=s_sb[:, :L],
                                    func=AF.Exp, bias=stats[:, 0:1], scale=1.0,
                                    accum_out=stats[:, 1:2]))
                b.wait(b.ve, b.t_se)
                b.wait(b.ve, b.t_te)
                b.v(b.ve.reciprocal(stats[:, 2:3], stats[:, 1:2]))
                b.wait(b.ve, b.t_ve)
                b.v(b.ve.tensor_scalar(out=p_f32[:, :L], in0=p_f32[:, :L],
                                       scalar1=stats[:, 2:3], scalar2=None,
                                       op0=ALU.mult))
                pn_mark = b.t_ve.n
                for sc in range(i + 1):
                    bank = 2 + sc % 2
                    b.bwait(bank)
                    b.wait(b.te, b.t_ve, pn_mark)
                    tp_pe_split(b, C, bank, 0,
                                pth[:, sc * 128:(sc + 1) * 128],
                                ptl[:, sc * 128:(sc + 1) * 128],
                                p_f32[:, sc * 128:(sc + 1) * 128])
                pt_mark_se = b.t_se.n
                pt_mark_ve = b.t_ve.n
                for sc in range(i + 1):
                    par = (sc % 2) * D
                    b.wait(b.se, b.t_te, vb_marks[sc % 2])
                    b.inc(b.se.dma_start(out=vbh[:, par:par + D],
                                         in_=plane(scr_vt, 2 * sc)), s_vb[sc % 2], 16)
                    b.inc(b.se.dma_start(out=vbl[:, par:par + D],
                                         in_=plane(scr_vt, 2 * sc + 1)), s_vb[sc % 2], 16)
                    b.wait(b.te, s_vb[sc % 2])
                    b.wait(b.te, b.t_se, pt_mark_se)
                    b.wait(b.te, b.t_ve, pt_mark_ve)
                    if sc == 0:
                        b.bwait(4)
                        b.bwait(5)
                    for half in range(2):
                        ii = mm3(b, PS[4 + half][:, :],
                                 pth[:, sc * 128:(sc + 1) * 128],
                                 ptl[:, sc * 128:(sc + 1) * 128],
                                 vbh[:, par + half * 512: par + (half + 1) * 512],
                                 vbl[:, par + half * 512: par + (half + 1) * 512],
                                 start=(sc == 0), stop=(sc == i))
                    vb_marks[sc % 2] = b.t(ii)
                b.wait(b.se, b.t_te)
                o_t = p_f32
                for half in range(2):
                    b.bmark(4 + half, b.t_se,
                            b.s(b.se.activation(out=o_t[:, half * 512:(half + 1) * 512],
                                                in_=PS[4 + half][:, :], func=AF.Identity,
                                                bias=0.0, scale=1.0)))
                ot_mark = b.t_se.n
                for c in range(8):
                    bank = 2 + c % 2
                    b.bwait(bank)
                    b.wait(b.te, b.t_se, ot_mark)
                    tp_pe_split(b, C, bank, 0,
                                ogh[:, c * 512 + iq * 128: c * 512 + iq * 128 + 128],
                                ogl[:, c * 512 + iq * 128: c * 512 + iq * 128 + 128],
                                o_t[:, c * 128:(c + 1) * 128])
            og_mark_se = b.t_se.n
            og_mark_ve = b.t_ve.n
            for tt in range(4):
                tok = 4 * g + tt
                b.wait(b.se, b.t_ve)
                b.inc(b.se.dma_start(out=xr[:], in_=rows3(src, tok)), s_xr, 16)
                b.wait(b.te, b.t_se, og_mark_se)
                b.wait(b.te, b.t_ve, og_mark_ve)
                for sem, val in wsems:
                    b.wait(b.te, sem, val)
                b.bwait(6)
                b.bwait(7)
                for half in range(2):
                    ps = PS[6 + half]
                    for c in range(8):
                        ii = mm3(b, ps[:, :],
                                 ogh[:, c * 512 + tt * 128: c * 512 + tt * 128 + 128],
                                 ogl[:, c * 512 + tt * 128: c * 512 + tt * 128 + 128],
                                 woh[:, c * D + half * 512: c * D + (half + 1) * 512],
                                 wol[:, c * D + half * 512: c * D + (half + 1) * 512],
                                 start=(c == 0), stop=(c == 7))
                    b.t(ii)
                b.wait(b.ve, s_xr)
                b.wait(b.ve, s_bias)
                b.wait(b.ve, b.t_ve)
                b.v(b.ve.tensor_tensor(out=xpb[:], in0=xr[:], in1=bo_rep[:],
                                       op=ALU.add))
                b.wait(b.ve, b.t_te)
                b.wait(b.ve, b.t_ve)
                for half in range(2):
                    b.bmark(6 + half, b.t_ve,
                            b.v(b.ve.tensor_tensor(
                                out=resid[:, half * 512:(half + 1) * 512],
                                in0=PS[6 + half][:, :],
                                in1=xpb[:, half * 512:(half + 1) * 512], op=ALU.add)))
                b.wait(b.ve, s_out)
                layernorm_tile(b, outt[:], resid[:], xpb[:], stats[:, 3:8])
                b.wait(b.ge, b.t_ve)
                b.inc(b.ge.dma_start(out=rows3(dst, tok), in_=outt[:]), s_out, 16)
        b.barrier()
        b.rmchan(s_bias, s_qg, s_xr, s_out, *s_kb, *s_vb)



def load_shifted(b, sem, eng, dst, src, t0, ntiles):
    d3 = dst.rearrange("p (tt d) -> p tt d", d=D)
    base = src.offset
    t = src.tensor
    b.inc(eng.dma_start(
        out=d3[:, :, 0:512],
        in_=bass.AP(t, base + t0 * D, [[D, 128], [128 * D, ntiles], [1, 512]])),
        sem, 16)
    if t0 > 0:
        b.inc(eng.dma_start(
            out=d3[:, :, 512:1024],
            in_=bass.AP(t, base + (t0 - 1) * D + 512,
                        [[D, 128], [128 * D, ntiles], [1, 512]])), sem, 16)
    else:
        b.v(b.ve.memset(d3[0:1, 0, 512:1024], 0.0))
        b.inc(eng.dma_start(
            out=d3[1:128, 0, 512:1024],
            in_=bass.AP(t, base + 512, [[D, 127], [1, 512]])), sem, 16)
        if ntiles > 1:
            b.inc(eng.dma_start(
                out=d3[:, 1:ntiles, 512:1024],
                in_=bass.AP(t, base + 127 * D + 512,
                            [[D, 128], [128 * D, ntiles - 1], [1, 512]])), sem, 16)


def slot2_wprep(b, C):
    nc, W = C["nc"], C["W"]
    with ExitStack() as ph:
        def sb(name, cols, dt=F32):
            return ph.enter_context(nc.sbuf_tensor(f"{name}W2{Sem._uid[0]}", [128, cols], dt))
        stage = sb("stage", 2 * 5460)
        sh = sb("sh", 2 * 5460, BF16)
        sl = sb("sl", 2 * 5460, BF16)
        s_ld = [b.chan(ph, "w2lda"), b.chan(ph, "w2ldb", sw=True)]
        s_st = [b.chan(ph, "w2sta"), b.chan(ph, "w2stb")]
        s_sl2 = [b.chan(ph, "w2sla", sw=True), b.chan(ph, "w2slb")]
        lm = [0, 0]
        cn = [0]
        jobs = []
        for l in range(2):
            jobs.append((W["ff_w1"].rearrange("l (c p) n -> l c p n", p=128)[l],
                         C["scr_w1h"], C["scr_w1l"], l * 1024, 8, 5460, 128))
            jobs.append((W["ff_wqkv"].rearrange("l (c p) n -> l c p n", p=128)[l],
                         C["scr_wqh"], C["scr_wql"], l * 1024, 8, 1536, 128))
            jobs.append((W["ff_wo"].rearrange("l (c p) n -> l c p n", p=128)[l],
                         C["scr_woh"], C["scr_wol"], l * 512, 4, 1024, 128))
        for l in range(2):
            jobs.append((W["ff_w2"].rearrange("l m n -> l m n")[l],
                         C["scr_w2h"], C["scr_w2l"], l * 2730, 22, 1024, None))
        for (hbm, sch, scl_, row0, nch, wid, chunked) in jobs:
            for cji in range(nch):
                rows = 128 if (chunked is not None or cji < 21) else 42
                r0 = cji * 128
                p_ = cn[0] % 2
                off = p_ * 5460
                ld_e = (b.se, b.ge)[p_]
                lo_e = (b.ge, b.se)[p_]
                b.wait(ld_e, b.t_ve, lm[p_])
                b.wait(ld_e, s_st[p_])
                b.wait(ld_e, s_sl2[p_])
                if chunked is not None:
                    b.inc(ld_e.dma_start(out=stage[:rows, off:off + wid], in_=hbm[cji]),
                          s_ld[p_], 16)
                else:
                    b.inc(ld_e.dma_start(out=stage[:rows, off:off + wid],
                                         in_=hbm[r0:r0 + rows, :]), s_ld[p_], 16)
                b.wait(b.ve, s_ld[p_])
                b.wait(b.ve, s_st[p_])
                b.wait(b.ve, s_sl2[p_])
                split_hilo(b, sh[:rows, off:off + wid], sl[:rows, off:off + wid],
                           stage[:rows, off:off + wid])
                lm[p_] = b.t_ve.n
                b.wait(b.sy, b.t_ve, lm[p_])
                b.wait(lo_e, b.t_ve, lm[p_])
                for (scr, buf, eng, sem) in ((sch, sh, b.sy, s_st[p_]),
                                             (scl_, sl, lo_e, s_sl2[p_])):
                    b.inc(eng.dma_start(
                        out=scr.rearrange("(u p) n -> p u n", p=128)[:rows, (row0 // 128) + cji]
                        if chunked is not None else scr[row0 + r0: row0 + r0 + rows, :],
                        in_=buf[:rows, off:off + wid]), sem, 16)
                cn[0] += 1
        b.barrier()
        b.rmchan(*s_ld, *s_st, *s_sl2)


def slot2_attn(b, C, l, src, dst):
    nc, W, PS = C["nc"], C["W"], C["PS"]
    t_len, nt = C["t_len"], C["nt"]
    nw = t_len // 512
    maskU, maskL = C["maskU"], C["maskL"]
    with ExitStack() as pa:
        def sb(name, cols, dt=F32):
            return pa.enter_context(nc.sbuf_tensor(f"{name}2A{Sem._uid[0]}", [128, cols], dt))
        rot = {}
        s_rt = b.chan(pa, "rt2")
        histage = sb("histage", 4 * 64)
        for nm in ("rc1", "rs1", "rc2", "rs2"):
            rot[nm] = sb(nm, 4 * 64)

        def load_rot(w):
            for nm in ("rc1", "rs1", "rc2", "rs2"):
                b.wait(b.se, b.t_ve)
                b.inc(b.se.dma_start(
                    out=histage[:, :],
                    in_=W[nm + "_hi"].rearrange("(tt p) c -> p tt c", p=128)[:, 4 * w:4 * w + 4]),
                    s_rt, 16)
                b.inc(b.se.dma_start(
                    out=rot[nm][:, :],
                    in_=W[nm + "_lo"].rearrange("(tt p) c -> p tt c", p=128)[:, 4 * w:4 * w + 4]),
                    s_rt, 16)
                b.wait(b.ve, s_rt)
                b.wait(b.ve, b.t_ve)
                b.v(b.ve.tensor_tensor(out=rot[nm][:, :], in0=histage[:, :],
                                       in1=rot[nm][:, :], op=ALU.add))
        qs_rep = sb("qs_rep", 64)
        ks_rep = sb("ks_rep", 64)
        s_sc = b.chan(pa, "sc2")
        b.inc(b.se.dma_start(out=qs_rep[:],
                             in_=bass.AP(W["ff_qs"].tensor, W["ff_qs"].offset + l * 64,
                                         [[0, 128], [1, 64]])), s_sc, 16)
        b.inc(b.se.dma_start(out=ks_rep[:],
                             in_=bass.AP(W["ff_ks"].tensor, W["ff_ks"].offset + l * 64,
                                         [[0, 128], [1, 64]])), s_sc, 16)
        wkvh = sb("wkvh", 2 * 8 * 512, BF16)
        wkvl = sb("wkvl", 2 * 8 * 512, BF16)
        woh = sb("woh", 4 * D, BF16)
        wol = sb("wol", 4 * D, BF16)
        s_wk = [b.chan(pa, "wk2a"), b.chan(pa, "wk2b")]
        s_wo = b.chan(pa, "wo2")
        wk_marks = [0, 0]
        wqh3 = C["scr_wqh"].rearrange("(u p) n -> p u n", p=128)
        wql3 = C["scr_wql"].rearrange("(u p) n -> p u n", p=128)
        b.inc(b.se.dma_start(out=woh.rearrange("p (c q) -> p c q", q=D),
                             in_=C["scr_woh"].rearrange("(u p) n -> p u n", p=128)[:, 4 * l:4 * l + 4]),
              s_wo, 16)
        b.inc(b.se.dma_start(out=wol.rearrange("p (c q) -> p c q", q=D),
                             in_=C["scr_wol"].rearrange("(u p) n -> p u n", p=128)[:, 4 * l:4 * l + 4]),
              s_wo, 16)
        wsems = [(s_wo, s_wo.n)]

        hwin = sb("hwin", 2 * D)
        hfh = sb("hfh", 8 * 512, BF16)
        hfl = sb("hfl", 8 * 512, BF16)
        qkvG = sb("qkvG", 4 * 1536)
        sqs = sb("sqs", 2048)
        ss8 = sb("ss8", 96)
        qn = sb("qn", 2048)
        qwfh = sb("qwfh", 4 * 512, BF16)
        qwfl = sb("qwfl", 4 * 512, BF16)
        kwfh = sb("kwfh", 2 * 4 * 512, BF16)
        kwfl = sb("kwfl", 2 * 4 * 512, BF16)
        vwh = sb("vwh", 2 * 4 * 512, BF16)
        vwl = sb("vwl", 2 * 4 * 512, BF16)
        s_sb2 = sb("s_sb2", 2 * 1280)
        p_f32 = sb("p_f32", 2 * 1280)
        o_t1 = sb("o_t1", 128)
        pth = sb("pth", 2 * 640, BF16)
        ptl = sb("ptl", 2 * 640, BF16)
        o_fmh = sb("o_fmh", 4 * 512, BF16)
        o_fml = sb("o_fml", 4 * 512, BF16)
        stats = sb("stats", 16)
        xr = sb("xr", D)
        s_hw = b.chan(pa, "hw2")
        s_xr = b.chan(pa, "xr2")
        s_out = b.chan(pa, "out2", sw=True)

        for w in range(nw):
            cur = w % 2
            load_rot(w)
            for sub in range(2):
                b.wait(b.se, b.t_te)
                b.wait(b.se, b.t_ve)
                load_shifted(b, s_hw, b.se, hwin, src, w * 512 + sub * 256, 2)
                b.wait(b.te, s_hw)
                for t2 in range(2):
                    tt = sub * 2 + t2
                    for c in range(8):
                        bank = 6 + ((t2 * 8 + c) % 2)
                        b.bwait(bank)
                        tp_pe_split(b, C, bank, 0,
                                    hfh[:, c * 512 + tt * 128: c * 512 + tt * 128 + 128],
                                    hfl[:, c * 512 + tt * 128: c * 512 + tt * 128 + 128],
                                    hwin[:, t2 * D + c * 128: t2 * D + c * 128 + 128])
            h_mark_se = b.t_se.n
            h_mark_ve = b.t_ve.n
            for cc in range(3):
                wkp = (cc % 2) * 8 * 512
                b.wait(b.se, b.t_te, wk_marks[cc % 2])
                b.inc(b.se.dma_start(
                    out=wkvh[:, wkp:wkp + 8 * 512].rearrange("p (c q) -> p c q", q=512),
                    in_=wqh3[:, 8 * l:8 * l + 8, cc * 512:(cc + 1) * 512]), s_wk[cc % 2], 16)
                b.inc(b.se.dma_start(
                    out=wkvl[:, wkp:wkp + 8 * 512].rearrange("p (c q) -> p c q", q=512),
                    in_=wql3[:, 8 * l:8 * l + 8, cc * 512:(cc + 1) * 512]), s_wk[cc % 2], 16)
                for tt in range(4):
                    ps = PS[tt % 2]
                    b.wait(b.te, b.t_se, h_mark_se)
                    b.wait(b.te, b.t_ve, h_mark_ve)
                    b.wait(b.te, s_wk[cc % 2])
                    b.bwait(tt % 2)
                    for ci in range(8):
                        ii = mm3(b, ps[:, :],
                                 hfh[:, ci * 512 + tt * 128: ci * 512 + tt * 128 + 128],
                                 hfl[:, ci * 512 + tt * 128: ci * 512 + tt * 128 + 128],
                                 wkvh[:, wkp + ci * 512: wkp + (ci + 1) * 512],
                                 wkvl[:, wkp + ci * 512: wkp + (ci + 1) * 512],
                                 start=(ci == 0), stop=(ci == 7))
                    wk_marks[cc % 2] = b.t(ii)
                    b.wait(b.ve, b.t_te)
                    b.wait(b.ve, b.t_ve)
                    b.bmark(tt % 2, b.t_ve,
                            b.v(b.ve.tensor_copy(
                                qkvG[:, cc * 2048 + tt * 512: cc * 2048 + tt * 512 + 512],
                                ps[:, :])))
            for (part, srep, ctab, stab) in ((0, qs_rep, "rc1", "rs1"),
                                             (1, ks_rep, "rc2", "rs2")):
                reg = qkvG[:, part * 2048:(part + 1) * 2048]
                b.wait(b.se, b.t_ve)
                b.wait(b.se, b.t_se)
                b.s(b.se.activation(out=sqs[:], in_=reg, func=AF.Square,
                                    bias=0.0, scale=1.0))
                b.wait(b.ve, b.t_se)
                b.v(b.ve.reduce_sum(ss8[:, 0:32],
                                    sqs.rearrange("p (h c) -> p h c", c=64),
                                    axis=AX.X))
                b.wait(b.se, b.t_ve)
                b.s(b.se.activation(out=ss8[:, 32:64], in_=ss8[:, 0:32],
                                    func=AF.Sqrt, bias=0.0, scale=1.0))
                b.wait(b.ve, b.t_se)
                b.v(b.ve.tensor_scalar_max(ss8[:, 0:32], ss8[:, 32:64], 1e-12))
                b.wait(b.ve, b.t_ve)
                b.v(b.ve.reciprocal(ss8[:, 64:96], ss8[:, 0:32]))
                nrm_b = ss8[:, 64:96].rearrange("p h -> p h ()").broadcast_to([128, 32, 64])
                b.wait(b.ve, b.t_ve)
                b.wait(b.ve, b.t_te)
                b.v(b.ve.tensor_tensor(out=qn.rearrange("p (h c) -> p h c", c=64),
                                       in0=reg.rearrange("p (h c) -> p h c", c=64),
                                       in1=nrm_b, op=ALU.mult))
                qs_b = srep.rearrange("p c -> p () c").broadcast_to([128, 32, 64])
                b.wait(b.ve, b.t_ve)
                b.wait(b.ve, s_sc)
                b.v(b.ve.tensor_tensor(out=qn.rearrange("p (h c) -> p h c", c=64),
                                       in0=qn.rearrange("p (h c) -> p h c", c=64),
                                       in1=qs_b, op=ALU.mult))
                c_b = rot[ctab].rearrange("p (g c) -> p g () c", c=64).broadcast_to(
                    [128, 4, 8, 64])
                b.wait(b.ve, b.t_ve)
                b.v(b.ve.tensor_tensor(
                    out=sqs.rearrange("p (g h c) -> p g h c", h=8, c=64),
                    in0=qn.rearrange("p (g h c) -> p g h c", h=8, c=64),
                    in1=c_b, op=ALU.mult))
                for hf in range(2):
                    s_b = bass.AP(rot[stab], hf * 32,
                                  [[256, 128], [64, 4], [0, 8], [1, 32]])
                    shuf = bass.AP(qn, (1 - hf) * 32,
                                   [[2048, 128], [512, 4], [64, 8], [1, 32]])
                    dst4 = bass.AP(hwin, hf * 32,
                                   [[2048, 128], [512, 4], [64, 8], [1, 32]])
                    b.wait(b.ve, b.t_ve)
                    b.v(b.ve.tensor_tensor(out=dst4, in0=shuf, in1=s_b,
                                           op=ALU.mult))
                b.wait(b.ve, b.t_ve)
                b.v(b.ve.tensor_tensor(out=qn[:], in0=sqs[:], in1=hwin[:, :2048],
                                       op=ALU.add))
                tr_mark = b.t_ve.n
                b.wait(b.te, b.t_ve, tr_mark)
                for tt in range(4):
                    for c in range(4):
                        if part == 0:
                            dh_ = qwfh[:, c * 512 + tt * 128: c * 512 + tt * 128 + 128]
                            dl_ = qwfl[:, c * 512 + tt * 128: c * 512 + tt * 128 + 128]
                        else:
                            base_ = cur * 4 * 512
                            dh_ = kwfh[:, base_ + c * 512 + tt * 128: base_ + c * 512 + tt * 128 + 128]
                            dl_ = kwfl[:, base_ + c * 512 + tt * 128: base_ + c * 512 + tt * 128 + 128]
                        bank = 4 + (tt * 4 + c) % 2
                        b.bwait(bank)
                        tp_pe_split(b, C, bank, 0, dh_, dl_,
                                    qn[:, tt * 512 + c * 128: tt * 512 + (c + 1) * 128])
            b.wait(b.ve, b.t_te)
            split_hilo(b, vwh[:, cur * 2048: cur * 2048 + 2048],
                       vwl[:, cur * 2048: cur * 2048 + 2048],
                       qkvG[:, 4096:6144])
            vw_mark = b.t_ve.n
            qk_mark_se = b.t_se.n
            qk_mark_ve = b.t_ve.n
            pn_marks = [0, 0]

            def s2_stage1(n):
                pr, iq = divmod(n, 4)
                buf = n % 2
                wA = (512 - iq * 128) if w > 0 else 0
                wB = iq * 128 + 128
                L = wA + wB
                for hh in range(2):
                    hrow = slice(64 * hh, 64 * hh + 64)
                    bnkA, bnkB = (2, 3) if hh == 0 else (6, 7)
                    b.wait(b.te, b.t_se, qk_mark_se)
                    b.wait(b.te, b.t_ve, qk_mark_ve)
                    b.bwait(bnkA)
                    b.bwait(bnkB)
                    qap_h = qwfh[hrow, pr * 512 + iq * 128: pr * 512 + iq * 128 + 128]
                    qap_l = qwfl[hrow, pr * 512 + iq * 128: pr * 512 + iq * 128 + 128]
                    if wA:
                        prev = (1 - cur) * 2048
                        mm3(b, PS[bnkA][:, :wA],
                            qap_h, qap_l,
                            kwfh[hrow, prev + pr * 512 + iq * 128: prev + pr * 512 + 512],
                            kwfl[hrow, prev + pr * 512 + iq * 128: prev + pr * 512 + 512],
                            start=True, stop=True)
                    ii = mm3(b, PS[bnkB][:, :wB],
                             qap_h, qap_l,
                             kwfh[hrow, cur * 2048 + pr * 512: cur * 2048 + pr * 512 + wB],
                             kwfl[hrow, cur * 2048 + pr * 512: cur * 2048 + pr * 512 + wB],
                             start=True, stop=True)
                    b.t(ii)
                    sreg = s_sb2[:, buf * 1280 + hh * 640: buf * 1280 + hh * 640 + L]
                    b.wait(b.ve, b.t_te)
                    b.wait(b.ve, b.t_se)
                    b.wait(b.ve, b.t_ve)
                    if wA:
                        b.bmark(bnkA, b.t_ve,
                                b.v(b.ve.tensor_scalar_mul(sreg[:, :wA], PS[bnkA][:, :wA], 8.0)))
                    b.bmark(bnkB, b.t_ve,
                            b.v(b.ve.tensor_scalar_mul(sreg[:, wA:L], PS[bnkB][:, :wB], 8.0)))
                    b.wait(b.ve, b.t_ve)
                    if wA:
                        b.v(b.ve.tensor_tensor(out=sreg[:, 0:128], in0=sreg[:, 0:128],
                                               in1=maskL[:], op=ALU.add))
                        b.wait(b.ve, b.t_ve)
                    b.v(b.ve.tensor_tensor(out=sreg[:, L - 128:L], in0=sreg[:, L - 128:L],
                                           in1=maskU[:], op=ALU.add))
                    b.wait(b.ve, b.t_ve)
                    st = stats[:, buf * 4 + hh * 2: buf * 4 + hh * 2 + 2]
                    b.v(b.ve.reduce_max(st[:, 0:1], sreg, axis=AX.X, negate=True))
                    b.wait(b.se, b.t_ve)
                    b.wait(b.se, b.t_se)
                    b.wait(b.se, b.t_te)
                    preg = p_f32[:, buf * 1280 + hh * 640: buf * 1280 + hh * 640 + L]
                    b.s(b.se.activation(out=preg, in_=sreg, func=AF.Exp,
                                        bias=st[:, 0:1], scale=1.0,
                                        accum_out=st[:, 1:2]))
                    b.wait(b.ve, b.t_se)
                    b.v(b.ve.reciprocal(stats[:, 8 + buf * 2 + hh: 9 + buf * 2 + hh],
                                        st[:, 1:2]))
                    b.wait(b.ve, b.t_ve)
                    b.v(b.ve.tensor_scalar(out=preg, in0=preg,
                                           scalar1=stats[:, 8 + buf * 2 + hh: 9 + buf * 2 + hh],
                                           scalar2=None, op0=ALU.mult))
                pn_marks[buf] = b.t_ve.n

            def s2_stage2(n):
                pr, iq = divmod(n, 4)
                buf = n % 2
                wA = (512 - iq * 128) if w > 0 else 0
                L = wA + iq * 128 + 128
                nsc = L // 128
                for hh in range(2):
                    for sc in range(nsc):
                        bank = sc % 2
                        b.bwait(bank)
                        b.wait(b.te, b.t_ve, pn_marks[buf])
                        tp_pe_split(b, C, bank, 0,
                                    pth[:, hh * 640 + sc * 128: hh * 640 + sc * 128 + 128],
                                    ptl[:, hh * 640 + sc * 128: hh * 640 + sc * 128 + 128],
                                    p_f32[:, buf * 1280 + hh * 640 + sc * 128:
                                          buf * 1280 + hh * 640 + sc * 128 + 128])
                pt_se = b.t_se.n
                pt_ve = b.t_ve.n
                ob = PS[4 + n % 2]
                b.wait(b.te, b.t_se, pt_se)
                b.wait(b.te, b.t_ve, pt_ve)
                b.bwait(4 + n % 2)
                first = True
                for sc in range(nsc):
                    if sc < wA // 128:
                        vwin, vtile = 1 - cur, iq + sc
                    else:
                        vwin, vtile = cur, sc - wA // 128
                    vbase = vwin * 2048 + vtile * 512
                    for hh in range(2):
                        h = 2 * pr + hh
                        b.te.matmul(
                            ob[:, 64 * hh:64 * hh + 64],
                            pth[:, hh * 640 + sc * 128: hh * 640 + sc * 128 + 128],
                            vwh[:, vbase + h * 64: vbase + h * 64 + 64],
                            start=first, stop=False)
                        first = False
                        b.te.matmul(
                            ob[:, 64 * hh:64 * hh + 64],
                            pth[:, hh * 640 + sc * 128: hh * 640 + sc * 128 + 128],
                            vwl[:, vbase + h * 64: vbase + h * 64 + 64],
                            start=False, stop=False)
                        ii = b.te.matmul(
                            ob[:, 64 * hh:64 * hh + 64],
                            ptl[:, hh * 640 + sc * 128: hh * 640 + sc * 128 + 128],
                            vwh[:, vbase + h * 64: vbase + h * 64 + 64],
                            start=False,
                            stop=(sc == nsc - 1 and hh == 1))
                b.t(ii)
                b.wait(b.se, b.t_te)
                b.bmark(4 + n % 2, b.t_se,
                        b.s(b.se.activation(out=o_t1[:, 0:128], in_=ob[:, 0:128],
                                            func=AF.Identity, bias=0.0, scale=1.0)))
                ot_mark = b.t_se.n
                obank = n % 2
                b.bwait(obank)
                b.wait(b.te, b.t_se, ot_mark)
                tp_pe_split(b, C, obank, 0,
                            o_fmh[:, pr * 512 + iq * 128: pr * 512 + iq * 128 + 128],
                            o_fml[:, pr * 512 + iq * 128: pr * 512 + iq * 128 + 128],
                            o_t1[:, 0:128])

            for n in range(16):
                s2_stage1(n)
                if n > 0:
                    s2_stage2(n - 1)
            s2_stage2(15)
            for tt in range(4):
                tok = w * 4 + tt
                b.wait(b.se, b.t_ve)
                b.wait(b.se, s_out)
                b.inc(b.se.dma_start(out=xr[:], in_=rows3(src, tok)), s_xr, 16)
                b.bwait(6)
                b.bwait(7)
                b.wait(b.te, b.t_se)
                b.wait(b.te, b.t_ve)
                for sem, val in wsems:
                    b.wait(b.te, sem, val)
                for half in range(2):
                    ps = PS[6 + half]
                    for c in range(4):
                        ii = mm3(b, ps[:, :],
                                 o_fmh[:, c * 512 + tt * 128: c * 512 + tt * 128 + 128],
                                 o_fml[:, c * 512 + tt * 128: c * 512 + tt * 128 + 128],
                                 woh[:, c * D + half * 512: c * D + (half + 1) * 512],
                                 wol[:, c * D + half * 512: c * D + (half + 1) * 512],
                                 start=(c == 0), stop=(c == 3))
                    b.t(ii)
                b.wait(b.ve, s_xr)
                b.wait(b.ve, b.t_te)
                b.wait(b.ve, b.t_ve)
                b.wait(b.ve, s_out)
                for half in range(2):
                    b.bmark(6 + half, b.t_ve,
                            b.v(b.ve.tensor_tensor(
                                out=xr[:, half * 512:(half + 1) * 512],
                                in0=PS[6 + half][:, :],
                                in1=xr[:, half * 512:(half + 1) * 512], op=ALU.add)))
                b.wait(b.ge, b.t_ve)
                b.inc(b.ge.dma_start(out=rows3(dst, tok), in_=xr[:]), s_out, 16)
        b.barrier()
        b.rmchan(s_rt, s_sc, s_hw, s_xr, s_out, s_wo, *s_wk)


def slot2_ffn(b, C, l, src, dst):
    nc, W, PS = C["nc"], C["W"], C["PS"]
    t_len, nt = C["t_len"], C["nt"]
    GRP = min(1024, t_len)
    TPG = GRP // 128
    n_grp = t_len // GRP
    JCH = [(j * 128, min(128, 2730 - j * 128)) for j in range(22)]
    with ExitStack() as ph:
        def sb(name, cols, dt=F32):
            return ph.enter_context(nc.sbuf_tensor(f"{name}2F{Sem._uid[0]}", [128, cols], dt))
        lg_rep = sb("lg_rep", D)
        lb_rep = sb("lb_rep", D)
        b2_rep = sb("b2_rep", D)
        b1a = sb("b1a", 22)
        b1g = sb("b1g", 22)
        s_cst = b.chan(ph, "cst2f")
        for (dstv, name, off) in ((lg_rep, "ff_lg", l * D), (lb_rep, "ff_lb", l * D),
                                  (b2_rep, "ff_b2", l * D)):
            b.inc(b.se.dma_start(out=dstv[:],
                                 in_=bass.AP(W[name].tensor, W[name].offset + off,
                                             [[0, 128], [1, D]])), s_cst, 16)
        bt = W["ff_b1"].tensor
        boff = W["ff_b1"].offset + l * 5460
        for (dstv, o2) in ((b1a, 0), (b1g, 2730)):
            b.inc(b.se.dma_start(out=dstv[:, 0:21],
                                 in_=bass.AP(bt, boff + o2, [[1, 128], [128, 21]]),
                                 allow_slow_non_contiguous=True), s_cst, 16)
            b.inc(b.se.dma_start(out=dstv[0:42, 21:22],
                                 in_=bass.AP(bt, boff + o2 + 2688, [[1, 42], [1, 1]]),
                                 allow_slow_non_contiguous=True), s_cst, 16)

        h2 = sb("h2", TPG * D)
        h2f_h = sb("h2fh", 8 * GRP, BF16)
        h2f_l = sb("h2fl", 8 * GRP, BF16)
        w1ah = sb("w1ah", 2 * 8 * 128, BF16)
        w1al = sb("w1al", 2 * 8 * 128, BF16)
        w1gh = sb("w1gh", 2 * 8 * 128, BF16)
        w1gl = sb("w1gl", 2 * 8 * 128, BF16)
        w2h_ = sb("w2h_", 2 * 512, BF16)
        w2l_ = sb("w2l_", 2 * 512, BF16)
        gel = sb("gel", 2 * 512)
        h3c = sb("h3c", 2 * 512)
        h3h = sb("h3h", 22 * GRP, BF16)
        h3l = sb("h3l", 22 * GRP, BF16)
        xab = sb("xab", D)
        outt = sb("outt", D)
        stats = sb("stats", 24)
        s_h2 = b.chan(ph, "h2f")
        s_w1 = [b.chan(ph, "w12fa"), b.chan(ph, "w12fb")]
        s_w2 = [b.chan(ph, "w22fa"), b.chan(ph, "w22fb")]
        s_xab = b.chan(ph, "xab2f")
        s_out = b.chan(ph, "out2f", sw=True)

        w1h3 = C["scr_w1h"].rearrange("(u p) n -> p u n", p=128)
        w1l3 = C["scr_w1l"].rearrange("(u p) n -> p u n", p=128)

        for g in range(n_grp):
            t0g = g * GRP
            b.wait(b.se, b.t_ve)
            b.wait(b.se, b.t_te)
            load_shifted(b, s_h2, b.se, h2, src, t0g, TPG)
            g3 = lambda t: t.rearrange("p (g c) -> p g c", c=D)
            b.wait(b.ve, s_h2)
            b.wait(b.ve, b.t_ve)
            b.v(b.ve.reduce_sum(stats[:, 0:TPG], g3(h2[:]), axis=AX.X))
            b.wait(b.ve, b.t_ve)
            b.v(b.ve.tensor_scalar_mul(stats[:, TPG:2 * TPG], stats[:, 0:TPG],
                                       -1.0 / D))
            nb_ = stats[:, TPG:2 * TPG].rearrange("p g -> p g ()").broadcast_to(
                [128, TPG, D])
            b.wait(b.ve, b.t_ve)
            b.wait(b.ve, b.t_te)
            b.v(b.ve.tensor_tensor(out=g3(h2[:]), in0=g3(h2[:]), in1=nb_,
                                   op=ALU.add))
            for tt in range(TPG):
                b.wait(b.se, b.t_ve)
                b.wait(b.se, b.t_se)
                b.s(b.se.activation(out=xab[:], in_=h2[:, tt * D:(tt + 1) * D],
                                    func=AF.Square, bias=0.0, scale=1.0,
                                    accum_out=stats[:, 2 * TPG + tt: 2 * TPG + tt + 1]))
            b.wait(b.se, b.t_se)
            b.s(b.se.activation(out=stats[:, 0:TPG], in_=stats[:, 2 * TPG:3 * TPG],
                                func=AF.Sqrt, bias=1e-5, scale=1.0 / D))
            b.wait(b.ve, b.t_se)
            b.v(b.ve.reciprocal(stats[:, TPG:2 * TPG], stats[:, 0:TPG]))
            rb_ = stats[:, TPG:2 * TPG].rearrange("p g -> p g ()").broadcast_to(
                [128, TPG, D])
            b.wait(b.ve, b.t_ve)
            b.v(b.ve.tensor_tensor(out=g3(h2[:]), in0=g3(h2[:]), in1=rb_,
                                   op=ALU.mult))
            lgb_ = lg_rep.rearrange("p c -> p () c").broadcast_to([128, TPG, D])
            lbb_ = lb_rep.rearrange("p c -> p () c").broadcast_to([128, TPG, D])
            b.wait(b.ve, b.t_ve)
            b.wait(b.ve, s_cst)
            b.v(b.ve.tensor_tensor(out=g3(h2[:]), in0=g3(h2[:]), in1=lgb_,
                                   op=ALU.mult))
            b.wait(b.ve, b.t_ve)
            b.v(b.ve.tensor_tensor(out=g3(h2[:]), in0=g3(h2[:]), in1=lbb_,
                                   op=ALU.add))
            h2_mark = b.t_ve.n
            for tt in range(TPG):
                reg = h2[:, tt * D:(tt + 1) * D]
                for c in range(8):
                    bank = 2 + (tt * 8 + c) % 2
                    b.bwait(bank)
                    b.wait(b.te, b.t_ve, h2_mark)
                    tp_pe_split(b, C, bank, 0,
                                h2f_h[:, c * GRP + tt * 128: c * GRP + tt * 128 + 128],
                                h2f_l[:, c * GRP + tt * 128: c * GRP + tt * 128 + 128],
                                reg[:, c * 128:(c + 1) * 128])
            h2f_mark_se = b.t_se.n
            h2f_mark_ve = b.t_ve.n
            w1_marks = [0, 0]
            for jj, (j0, jw) in enumerate(JCH):
                w1p = (jj % 2) * 8 * 128
                b.wait(b.se, b.t_te, w1_marks[jj % 2])
                b.inc(b.se.dma_start(
                    out=w1ah[:, w1p:w1p + 8 * jw].rearrange("p (c q) -> p c q", q=jw),
                    in_=w1h3[:, 8 * l:8 * l + 8, j0:j0 + jw]), s_w1[jj % 2], 16)
                b.inc(b.se.dma_start(
                    out=w1al[:, w1p:w1p + 8 * jw].rearrange("p (c q) -> p c q", q=jw),
                    in_=w1l3[:, 8 * l:8 * l + 8, j0:j0 + jw]), s_w1[jj % 2], 16)
                b.inc(b.se.dma_start(
                    out=w1gh[:, w1p:w1p + 8 * jw].rearrange("p (c q) -> p c q", q=jw),
                    in_=w1h3[:, 8 * l:8 * l + 8, 2730 + j0:2730 + j0 + jw]), s_w1[jj % 2], 16)
                b.inc(b.se.dma_start(
                    out=w1gl[:, w1p:w1p + 8 * jw].rearrange("p (c q) -> p c q", q=jw),
                    in_=w1l3[:, 8 * l:8 * l + 8, 2730 + j0:2730 + j0 + jw]), s_w1[jj % 2], 16)
                pa_, pg_ = (0, 1) if jj % 2 == 0 else (2, 3)
                par2 = (jj % 2) * 512
                for tg in range(GRP // 512):
                    b.wait(b.te, s_w1[jj % 2])
                    b.wait(b.te, b.t_se, h2f_mark_se)
                    b.wait(b.te, b.t_ve, h2f_mark_ve)
                    b.bwait(pa_)
                    b.bwait(pg_)
                    for ci in range(8):
                        mm3(b, PS[pa_][:jw, :],
                            w1ah[:, w1p + ci * jw: w1p + (ci + 1) * jw],
                            w1al[:, w1p + ci * jw: w1p + (ci + 1) * jw],
                            h2f_h[:, ci * GRP + tg * 512: ci * GRP + (tg + 1) * 512],
                            h2f_l[:, ci * GRP + tg * 512: ci * GRP + (tg + 1) * 512],
                            start=(ci == 0), stop=(ci == 7))
                        ii = mm3(b, PS[pg_][:jw, :],
                                 w1gh[:, w1p + ci * jw: w1p + (ci + 1) * jw],
                                 w1gl[:, w1p + ci * jw: w1p + (ci + 1) * jw],
                                 h2f_h[:, ci * GRP + tg * 512: ci * GRP + (tg + 1) * 512],
                                 h2f_l[:, ci * GRP + tg * 512: ci * GRP + (tg + 1) * 512],
                                 start=(ci == 0), stop=(ci == 7))
                    w1_marks[jj % 2] = b.t(ii)
                    b.wait(b.se, b.t_te)
                    b.wait(b.se, s_cst)
                    b.wait(b.se, b.t_ve)
                    b.bmark(pg_, b.t_se,
                            b.s(b.se.activation(out=gel[:jw, par2:par2 + 512],
                                                in_=PS[pg_][:jw, :],
                                                func=AF.Gelu, bias=b1g[:jw, jj:jj + 1],
                                                scale=1.0)))
                    b.wait(b.ve, b.t_se)
                    b.wait(b.ve, b.t_te)
                    b.bmark(pa_, b.t_ve,
                            b.v(b.ve.scalar_tensor_tensor(
                                out=h3c[:jw, par2:par2 + 512], in0=PS[pa_][:jw, :],
                                scalar=b1a[:jw, jj:jj + 1],
                                in1=gel[:jw, par2:par2 + 512], op0=ALU.add, op1=ALU.mult)))
                    b.wait(b.ve, b.t_ve)
                    b.wait(b.ve, b.t_te)
                    b.v(b.ve.tensor_copy(h3h[:jw, jj * GRP + tg * 512: jj * GRP + (tg + 1) * 512],
                                         h3c[:jw, par2:par2 + 512]))
                    b.wait(b.ve, b.t_ve)
                    b.v(b.ve.tensor_tensor(out=h3l[:jw, jj * GRP + tg * 512: jj * GRP + (tg + 1) * 512],
                                           in0=h3c[:jw, par2:par2 + 512],
                                           in1=h3h[:jw, jj * GRP + tg * 512: jj * GRP + (tg + 1) * 512],
                                           op=ALU.subtract))
            h3_mark = b.t_ve.n
            for half in range(2):
                w2_marks = [0, 0]
                for jj, (j0, jw) in enumerate(JCH):
                    w2p = (jj % 2) * 512
                    b.wait(b.se, b.t_te, w2_marks[jj % 2])
                    b.inc(b.se.dma_start(
                        out=w2h_[:jw, w2p:w2p + 512],
                        in_=C["scr_w2h"][l * 2730 + j0: l * 2730 + j0 + jw,
                                         half * 512:(half + 1) * 512]), s_w2[jj % 2], 16)
                    b.inc(b.se.dma_start(
                        out=w2l_[:jw, w2p:w2p + 512],
                        in_=C["scr_w2l"][l * 2730 + j0: l * 2730 + j0 + jw,
                                         half * 512:(half + 1) * 512]), s_w2[jj % 2], 16)
                    b.wait(b.te, s_w2[jj % 2])
                    b.wait(b.te, b.t_ve, h3_mark)
                    if jj == 0:
                        for tt2 in range(TPG):
                            b.bwait(tt2)
                    for tt in range(TPG):
                        ii = mm3(b, PS[tt][:, :],
                                 h3h[:jw, jj * GRP + tt * 128: jj * GRP + tt * 128 + 128],
                                 h3l[:jw, jj * GRP + tt * 128: jj * GRP + tt * 128 + 128],
                                 w2h_[:jw, w2p:w2p + 512],
                                 w2l_[:jw, w2p:w2p + 512],
                                 start=(jj == 0), stop=(jj == 21))
                    w2_marks[jj % 2] = b.t(ii)
                for tt in range(TPG):
                    tok = t0g // 128 + tt
                    b.wait(b.se, b.t_ve)
                    b.inc(b.se.dma_start(out=xab[:, half * 512:(half + 1) * 512],
                                         in_=bass.AP(src.tensor,
                                                     src.offset + (tok * 128) * D + half * 512,
                                                     [[D, 128], [1, 512]])), s_xab, 16)
                    b.wait(b.ve, s_xab)
                    b.wait(b.ve, b.t_te)
                    b.wait(b.ve, b.t_ve)
                    b.wait(b.ve, s_cst)
                    b.v(b.ve.tensor_tensor(out=xab[:, half * 512:(half + 1) * 512],
                                           in0=xab[:, half * 512:(half + 1) * 512],
                                           in1=b2_rep[:, half * 512:(half + 1) * 512],
                                           op=ALU.add))
                    b.wait(b.ve, b.t_ve)
                    b.wait(b.ve, s_out)
                    b.bmark(tt, b.t_ve,
                            b.v(b.ve.tensor_tensor(
                                out=outt[:, half * 512:(half + 1) * 512],
                                in0=PS[tt][:, :],
                                in1=xab[:, half * 512:(half + 1) * 512], op=ALU.add)))
                    b.wait(b.ge, b.t_ve)
                    b.inc(b.ge.dma_start(
                        out=bass.AP(dst.tensor, dst.offset + (tok * 128) * D + half * 512,
                                    [[D, 128], [1, 512]]),
                        in_=outt[:, half * 512:(half + 1) * 512]), s_out, 16)
        b.barrier()
        b.rmchan(s_cst, s_h2, s_xab, s_out, *s_w1, *s_w2)


def slot2(b, C, src, dst):
    nc, W = C["nc"], C["W"]
    t_len, nt = C["t_len"], C["nt"]
    if not C.get("w12_done"):
        slot2_wprep(b, C)
        C["w12_done"] = True
    xa, y = C["scr2_xa"], C["scr2_y"]
    cur = src
    for l in range(2):
        slot2_attn(b, C, l, cur, xa)
        slot2_ffn(b, C, l, xa, y)
        cur = y
    with ExitStack() as ph:
        def sb(name, cols, dt=F32):
            return ph.enter_context(nc.sbuf_tensor(f"{name}2L{Sem._uid[0]}", [128, cols], dt))
        x0 = sb("x0", 2 * 4 * D)
        y2 = sb("y2", 2 * 4 * D)
        sq = sb("sq", 4 * D)
        outt = sb("outt", 4 * D)
        stats = sb("stats", 16)
        s_ld = [b.chan(ph, "ld2La"), b.chan(ph, "ld2Lb")]
        s_out = b.chan(ph, "out2L", sw=True)
        lm4 = [0, 0]
        for g4 in range(nt // 4):
            p_ = g4 % 2
            off = p_ * 4 * D
            b.wait(b.se, b.t_ve, lm4[p_])
            b.inc(b.se.dma_start(
                out=x0[:, off:off + 4 * D].rearrange("p (tt d) -> p tt d", d=D),
                in_=src.rearrange("(tt p) d -> p tt d", p=128)[:, 4 * g4:4 * g4 + 4]),
                s_ld[p_], 16)
            b.inc(b.se.dma_start(
                out=y2[:, off:off + 4 * D].rearrange("p (tt d) -> p tt d", d=D),
                in_=y.rearrange("(tt p) d -> p tt d", p=128)[:, 4 * g4:4 * g4 + 4]),
                s_ld[p_], 16)
            b.wait(b.ve, s_ld[p_])
            b.wait(b.ve, b.t_ve)
            lm4[p_] = b.v(b.ve.tensor_tensor(out=y2[:, off:off + 4 * D],
                                             in0=x0[:, off:off + 4 * D],
                                             in1=y2[:, off:off + 4 * D], op=ALU.add))
            b.wait(b.ve, s_out)
            layernorm_quad(b, outt[:], y2[:, off:off + 4 * D], x0[:, off:off + 4 * D],
                           sq[:], stats[:, 0:12])
            lm4[p_] = b.t_ve.n
            b.wait(b.ge, b.t_ve)
            b.inc(b.ge.dma_start(
                out=dst.rearrange("(tt p) d -> p tt d", p=128)[:, 4 * g4:4 * g4 + 4],
                in_=outt.rearrange("p (tt d) -> p tt d", d=D)), s_out, 16)
        b.barrier()
        b.rmchan(s_out, *s_ld)



def slot3(b, C, src, dst):
    nc, W, PS = C["nc"], C["W"], C["PS"]
    t_len, nt = C["t_len"], C["nt"]
    ng = nt // 4
    scr_k = C["scr_vt"]
    scr_v3 = C["scr_v3"]

    with ExitStack() as pa:
        def sba(name, cols, dt=F32):
            return pa.enter_context(nc.sbuf_tensor(f"{name}3A{Sem._uid[0]}", [128, cols], dt))
        wf = sba("wf", 2 * D)
        wkh = sba("wkh", 8 * D, BF16); wkl = sba("wkl", 8 * D, BF16)
        wvh = sba("wvh", 8 * D, BF16); wvl = sba("wvl", 8 * D, BF16)
        wsems = list(load_w_hilo(b, pa, wf, wkh, wkl, W["l1a_wk"], 8, D))
        wsems += list(load_w_hilo(b, pa, wf, wvh, wvl, W["l1a_wv"], 8, D))
        bk_rep = sba("bk_rep", D)
        bv_rep = sba("bv_rep", D)
        s_bias = b.chan(pa, "bias3")
        bias_rep_load(b, s_bias, bk_rep[:], W["l1a_bk"], D)
        bias_rep_load(b, s_bias, bv_rep[:], W["l1a_bv"], D)

        xg = sba("xg", 4 * D)
        xfh = sba("xfh", 8 * 512, BF16)
        xfl = sba("xfl", 8 * 512, BF16)
        kt1 = sba("kt1", 512)
        kt2 = sba("kt2", 512)
        kte = sba("kte", 512)
        ktileh = sba("ktileh", D, BF16)
        ktilel = sba("ktilel", D, BF16)
        vt1 = sba("vt1", 512)
        vtileh = sba("vtileh", 8 * 130, BF16)
        vtilel = sba("vtilel", 8 * 130, BF16)
        s_xg = b.chan(pa, "xg3")
        s_spk = b.chan(pa, "spk3", sw=True)
        s_spv = b.chan(pa, "spv3", sw=True)

        vth4 = vtileh.rearrange("p (pr hh c) -> p pr hh c", hh=2, c=65)
        vtl4 = vtilel.rearrange("p (pr hh c) -> p pr hh c", hh=2, c=65)
        b.v(b.ve.memset(vth4[:, :, :, 64:65], 1.0))
        b.v(b.ve.memset(vtl4[:, :, :, 64:65], 0.0))

        for g in range(ng):
            b.wait(b.se, b.t_te)
            b.inc(b.se.dma_start(
                out=xg.rearrange("p (tt d) -> p tt d", d=D),
                in_=src.rearrange("(tt p) d -> p tt d", p=128)[:, 4 * g:4 * g + 4]),
                s_xg, 16)
            b.wait(b.te, s_xg)
            for tt in range(4):
                for c in range(8):
                    bank = 6 + ((tt * 8 + c) % 2)
                    b.bwait(bank)
                    tp_pe_split(b, C, bank, 0,
                                xfh[:, c * 512 + tt * 128: c * 512 + tt * 128 + 128],
                                xfl[:, c * 512 + tt * 128: c * 512 + tt * 128 + 128],
                                xg[:, tt * D + c * 128: tt * D + c * 128 + 128])
            b.wait(b.te, b.t_se)
            b.wait(b.te, b.t_ve)
            for tt in range(4):
                tok = 4 * g + tt
                for m in "kv":
                    for half in range(2):
                        ps = PS[half]
                        for sem, val in wsems:
                            b.wait(b.te, sem, val)
                        b.bwait(half)
                        wmh = wkh if m == "k" else wvh
                        wml = wkl if m == "k" else wvl
                        for ci in range(8):
                            ii = mm3(b, ps[:, :],
                                     xfh[:, ci * 512 + tt * 128: ci * 512 + tt * 128 + 128],
                                     xfl[:, ci * 512 + tt * 128: ci * 512 + tt * 128 + 128],
                                     wmh[:, ci * D + half * 512: ci * D + (half + 1) * 512],
                                     wml[:, ci * D + half * 512: ci * D + (half + 1) * 512],
                                     start=(ci == 0), stop=(ci == 7))
                        b.t(ii)
                    if m == "k":
                        b.wait(b.ve, b.t_te)
                        b.wait(b.ve, s_bias)
                        b.wait(b.ve, s_spk)
                        for half in range(2):
                            ps = PS[half]
                            b.wait(b.ve, b.t_ve)
                            b.bmark(half, b.t_ve,
                                    b.v(b.ve.tensor_tensor(out=kt1[:], in0=ps[:, :],
                                                           in1=bk_rep[:, half * 512:(half + 1) * 512],
                                                           op=ALU.add)))
                            b.wait(b.ve, b.t_ve)
                            b.v(b.ve.tensor_scalar_min(kt2[:], kt1[:], 0.0))
                            b.wait(b.se, b.t_ve)
                            b.wait(b.se, b.t_se)
                            b.s(b.se.activation(out=kte[:], in_=kt2[:], func=AF.Exp,
                                                bias=0.0, scale=1.0))
                            b.wait(b.ve, b.t_se)
                            b.v(b.ve.scalar_tensor_tensor(
                                out=kt2[:], in0=kt1[:], scalar=0.0,
                                in1=kte[:], op0=ALU.max, op1=ALU.add))
                            split_hilo(b, ktileh[:, half * 512:(half + 1) * 512],
                                       ktilel[:, half * 512:(half + 1) * 512], kt2[:])
                        b.wait(b.ge, b.t_ve)
                        b.inc(b.ge.dma_start(out=plane(scr_k, 2 * tok),
                                             in_=ktileh[:]), s_spk, 16)
                        b.inc(b.ge.dma_start(out=plane(scr_k, 2 * tok + 1),
                                             in_=ktilel[:]), s_spk, 16)
                    else:
                        b.wait(b.ve, b.t_te)
                        b.wait(b.ve, s_bias)
                        b.wait(b.ve, s_spv)
                        for half in range(2):
                            ps = PS[half]
                            b.wait(b.ve, b.t_ve)
                            b.bmark(half, b.t_ve,
                                    b.v(b.ve.tensor_tensor(out=vt1[:], in0=ps[:, :],
                                                           in1=bv_rep[:, half * 512:(half + 1) * 512],
                                                           op=ALU.add)))
                            dsth = vth4[:, 4 * half:4 * half + 4, :, 0:64]
                            dstl = vtl4[:, 4 * half:4 * half + 4, :, 0:64]
                            b.wait(b.ve, b.t_ve)
                            b.v(b.ve.tensor_copy(dsth, vt1.rearrange(
                                "p (pr hh c) -> p pr hh c", hh=2, c=64)))
                            b.wait(b.ve, b.t_ve)
                            b.v(b.ve.tensor_tensor(
                                out=dstl, in0=vt1.rearrange(
                                    "p (pr hh c) -> p pr hh c", hh=2, c=64),
                                in1=dsth, op=ALU.subtract))
                        b.wait(b.ge, b.t_ve)
                        b.inc(b.ge.dma_start(out=plane(scr_v3, 2 * tok),
                                             in_=vtileh[:]), s_spv, 16)
                        b.inc(b.ge.dma_start(out=plane(scr_v3, 2 * tok + 1),
                                             in_=vtilel[:]), s_spv, 16)
        b.barrier()
        b.rmchan(s_bias, s_xg, s_spk, s_spv)

    with ExitStack() as pb:
        def sb(name, cols, dt=F32):
            return pb.enter_context(nc.sbuf_tensor(f"{name}3B{Sem._uid[0]}", [128, cols], dt))
        kv2h = sb("kv2h", 8 * 128, BF16)
        kv2l = sb("kv2l", 8 * 128, BF16)
        ksumh = sb("ksumh", 8 * 16, BF16)
        ksuml = sb("ksuml", 8 * 16, BF16)
        b.v(b.ve.memset(kv2h[:], 0.0))
        b.v(b.ve.memset(kv2l[:], 0.0))
        b.v(b.ve.memset(ksumh[:], 0.0))
        b.v(b.ve.memset(ksuml[:], 0.0))
        with ExitStack() as p1:
            def sb1(name, cols, dt=F32):
                return p1.enter_context(nc.sbuf_tensor(f"{name}31{Sem._uid[0]}", [128, cols], dt))
            ktbh = sb1("ktbh", 2 * D, BF16)
            ktbl = sb1("ktbl", 2 * D, BF16)
            vtbh = sb1("vtbh", 2 * 1040, BF16)
            vtbl = sb1("vtbl", 2 * 1040, BF16)
            s_kt = [b.chan(p1, "kt31a"), b.chan(p1, "kt31b")]
            kt_marks = [0, 0]
            for tok in range(nt):
                par = tok % 2
                b.wait(b.se, b.t_te, kt_marks[par])
                b.inc(b.se.dma_start(out=ktbh[:, par * D:(par + 1) * D],
                                     in_=plane(scr_k, 2 * tok)), s_kt[par], 16)
                b.inc(b.se.dma_start(out=ktbl[:, par * D:(par + 1) * D],
                                     in_=plane(scr_k, 2 * tok + 1)), s_kt[par], 16)
                b.inc(b.se.dma_start(out=vtbh[:, par * 1040:(par + 1) * 1040],
                                     in_=plane(scr_v3, 2 * tok)), s_kt[par], 16)
                b.inc(b.se.dma_start(out=vtbl[:, par * 1040:(par + 1) * 1040],
                                     in_=plane(scr_v3, 2 * tok + 1)), s_kt[par], 16)
                b.wait(b.te, s_kt[par])
                if tok == 0:
                    for p2 in range(8):
                        b.bwait(p2)
                for p in range(8):
                    ps = PS[p]
                    off = 0
                    ii = mm3(b, ps[:, off:off + 130],
                             ktbh[:, par * D + 128 * p: par * D + 128 * p + 128],
                             ktbl[:, par * D + 128 * p: par * D + 128 * p + 128],
                             vtbh[:, par * 1040 + 130 * p: par * 1040 + 130 * p + 130],
                             vtbl[:, par * 1040 + 130 * p: par * 1040 + 130 * p + 130],
                             start=(tok == 0), stop=(tok == nt - 1))
                kt_marks[par] = b.t(ii)
            for p in range(8):
                ps = PS[p]
                off = 0
                for (rows, csrc, cdst) in (
                        (slice(0, 64), slice(off, off + 64), slice(128 * p, 128 * p + 64)),
                        (slice(64, 128), slice(off + 65, off + 129),
                         slice(128 * p + 64, 128 * p + 128))):
                    b.wait(b.se, b.t_te)
                    b.wait(b.se, b.t_ve)
                    b.bmark(p, b.t_se,
                            b.s(b.se.activation(out=kv2h[rows, cdst], in_=ps[rows, csrc],
                                                func=AF.Identity, bias=0.0, scale=1.0)))
                    b.wait(b.ve, b.t_se)
                    b.bmark(p, b.t_ve,
                            b.v(b.ve.scalar_tensor_tensor(
                                out=kv2l[rows, cdst], in0=ps[rows, csrc], scalar=0.0,
                                in1=kv2h[rows, cdst], op0=ALU.add, op1=ALU.subtract)))
                for (rows, csrc, jj) in ((slice(0, 64), off + 64, 0),
                                         (slice(64, 128), off + 129, 1)):
                    cdst = slice(16 * p + 2 * p + jj, 16 * p + 2 * p + jj + 1)
                    b.wait(b.se, b.t_te)
                    b.wait(b.se, b.t_ve)
                    b.bmark(p, b.t_se,
                            b.s(b.se.activation(out=ksumh[rows, cdst],
                                                in_=ps[rows, csrc:csrc + 1],
                                                func=AF.Identity, bias=0.0, scale=1.0)))
                    b.wait(b.ve, b.t_se)
                    b.bmark(p, b.t_ve,
                            b.v(b.ve.scalar_tensor_tensor(
                                out=ksuml[rows, cdst], in0=ps[rows, csrc:csrc + 1],
                                scalar=0.0, in1=ksumh[rows, cdst],
                                op0=ALU.add, op1=ALU.subtract)))
            b.barrier()
            b.rmchan(*s_kt)

        wf = sb("wf", 2 * D)
        wqh = sb("wqh", 8 * D, BF16); wql = sb("wql", 8 * D, BF16)
        woh = sb("woh", 8 * D, BF16); wol = sb("wol", 8 * D, BF16)
        wsems = list(load_w_hilo(b, pb, wf, wqh, wql, W["l1a_wq"], 8, D))
        wsems += list(load_w_hilo(b, pb, wf, woh, wol, W["l1a_wo"], 8, D))
        bq8 = sb("bq8", 8)
        bo_rep = sb("bo_rep", D)
        s_bias = b.chan(pb, "bias3b")
        b.inc(b.se.dma_start(
            out=bq8[:], in_=W["l1a_bq"].rearrange("(cc p) -> p cc", p=128),
            allow_slow_non_contiguous=True), s_bias, 16)
        bias_rep_load(b, s_bias, bo_rep[:], W["l1a_bo"], D)

        xg = sb("xg", 4 * D)
        xfh = sb("xfh", 8 * 512, BF16)
        xfl = sb("xfl", 8 * 512, BF16)
        fmq = xg
        t_a = sb("t_a", 512)
        t_b = sb("t_b", 512)
        fmqh = sb("fmqh", 8 * 512, BF16)
        fmql = sb("fmql", 8 * 512, BF16)
        z_sb = sb("z_sb", 512)
        z_b = sb("z_b", 512)
        qzc = sb("qzc", 512)
        qzh = sb("qzh", 8 * 512, BF16)
        qzl = sb("qzl", 8 * 512, BF16)
        sch = sb("sch", 8 * 512, BF16)
        scl = sb("scl", 8 * 512, BF16)
        xr = sb("xr", D)
        xpb = sb("xpb", D)
        resid = sb("resid", D)
        outt = sb("outt", D)
        stats = sb("stats", 8)
        s_xg = b.chan(pb, "xg3b")
        s_zb = b.chan(pb, "zb3")
        s_zw = b.chan(pb, "zw3", sw=True)
        s_xr = b.chan(pb, "xr3")
        s_out = b.chan(pb, "out3", sw=True)

        for g in range(ng):
            b.wait(b.se, b.t_te)
            b.wait(b.se, b.t_ve)
            b.inc(b.se.dma_start(
                out=xg.rearrange("p (tt d) -> p tt d", d=D),
                in_=src.rearrange("(tt p) d -> p tt d", p=128)[:, 4 * g:4 * g + 4]),
                s_xg, 16)
            b.wait(b.te, s_xg)
            for tt in range(4):
                for c in range(8):
                    bank = 6 + ((tt * 8 + c) % 2)
                    b.bwait(bank)
                    tp_pe_split(b, C, bank, 0,
                                xfh[:, c * 512 + tt * 128: c * 512 + tt * 128 + 128],
                                xfl[:, c * 512 + tt * 128: c * 512 + tt * 128 + 128],
                                xg[:, tt * D + c * 128: tt * D + c * 128 + 128])
            b.wait(b.te, b.t_se)
            b.wait(b.te, b.t_ve)
            for cc in range(8):
                ps = PS[cc % 2]
                for sem, val in wsems:
                    b.wait(b.te, sem, val)
                b.bwait(cc % 2)
                for ci in range(8):
                    ii = mm3(b, ps[:, :],
                             wqh[:, ci * D + cc * 128: ci * D + cc * 128 + 128],
                             wql[:, ci * D + cc * 128: ci * D + cc * 128 + 128],
                             xfh[:, ci * 512:(ci + 1) * 512],
                             xfl[:, ci * 512:(ci + 1) * 512],
                             start=(ci == 0), stop=(ci == 7))
                b.t(ii)
                b.wait(b.ve, b.t_te)
                b.wait(b.ve, s_bias)
                b.wait(b.ve, b.t_ve)
                b.v(b.ve.tensor_scalar(out=t_a[:], in0=ps[:, :],
                                       scalar1=bq8[:, cc:cc + 1], scalar2=0.0,
                                       op0=ALU.add, op1=ALU.min))
                b.wait(b.se, b.t_ve)
                b.wait(b.se, b.t_se)
                b.s(b.se.activation(out=t_b[:], in_=t_a[:], func=AF.Exp,
                                    bias=0.0, scale=1.0))
                b.wait(b.ve, b.t_se)
                b.bmark(cc % 2, b.t_ve,
                        b.v(b.ve.tensor_scalar(out=t_a[:], in0=ps[:, :],
                                               scalar1=bq8[:, cc:cc + 1], scalar2=0.0,
                                               op0=ALU.add, op1=ALU.max)))
                b.wait(b.ve, b.t_ve)
                b.v(b.ve.tensor_tensor(out=fmq[:, cc * 512:(cc + 1) * 512],
                                       in0=t_a[:], in1=t_b[:], op=ALU.add))
            split_hilo(b, fmqh[:], fmql[:], fmq[:])
            fmq_mark = b.t_ve.n
            b.wait(b.te, b.t_ve, fmq_mark)
            b.bwait(2)
            for c in range(8):
                ii = mm3(b, PS[2][0:16, :], ksumh[:, 16 * c:16 * c + 16],
                         ksuml[:, 16 * c:16 * c + 16],
                         fmqh[:, c * 512:(c + 1) * 512],
                         fmql[:, c * 512:(c + 1) * 512],
                         start=(c == 0), stop=(c == 7))
            b.t(ii)
            b.wait(b.ve, b.t_te)
            b.wait(b.ve, s_zb)
            b.bmark(2, b.t_ve,
                    b.v(b.ve.tensor_scalar_add(z_sb[0:16, :], PS[2][0:16, :], 1e-6)))
            b.wait(b.ve, b.t_ve)
            b.wait(b.ve, s_zw)
            b.v(b.ve.reciprocal(qzc[0:16, :], z_sb[0:16, :]))
            b.wait(b.ge, b.t_ve)
            b.wait(b.ge, s_zb)
            b.inc(b.ge.dma_start(out=C["scr_z"][:], in_=qzc[0:16, :]), s_zw, 16)
            zt = C["scr_z"].tensor
            for c in range(8):
                b.wait(b.se, s_zw)
                b.wait(b.se, b.t_ve)
                b.inc(b.se.dma_start(
                    out=z_b[0:64, :],
                    in_=bass.AP(zt, 2 * c * 512, [[0, 64], [1, 512]])), s_zb, 16)
                b.inc(b.se.dma_start(
                    out=z_b[64:128, :],
                    in_=bass.AP(zt, (2 * c + 1) * 512, [[0, 64], [1, 512]])), s_zb, 16)
                b.wait(b.ve, s_zb)
                b.wait(b.ve, s_zw)
                b.wait(b.ve, b.t_ve)
                b.v(b.ve.tensor_tensor(out=qzc[:], in0=fmq[:, c * 512:(c + 1) * 512],
                                       in1=z_b[:], op=ALU.mult))
                b.wait(b.ve, b.t_ve)
                b.wait(b.ve, b.t_te)
                b.v(b.ve.tensor_copy(qzh[:, c * 512:(c + 1) * 512], qzc[:]))
                b.wait(b.ve, b.t_ve)
                b.v(b.ve.tensor_tensor(out=qzl[:, c * 512:(c + 1) * 512],
                                       in0=qzc[:],
                                       in1=qzh[:, c * 512:(c + 1) * 512],
                                       op=ALU.subtract))
            qz_mark = b.t_ve.n
            for p in range(8):
                ps = PS[3 + p % 2]
                b.wait(b.te, b.t_ve, qz_mark)
                b.bwait(3 + p % 2)
                ii = mm3(b, ps[:, :], kv2h[:, 128 * p:128 * p + 128],
                         kv2l[:, 128 * p:128 * p + 128],
                         qzh[:, p * 512:(p + 1) * 512],
                         qzl[:, p * 512:(p + 1) * 512],
                         start=True, stop=True)
                b.t(ii)
                b.wait(b.se, b.t_te)
                b.wait(b.se, b.t_se)
                b.bmark(3 + p % 2, b.t_se,
                        b.s(b.se.activation(out=sch[:, p * 512:(p + 1) * 512],
                                            in_=ps[:, :], func=AF.Identity,
                                            bias=0.0, scale=1.0)))
                b.wait(b.ve, b.t_se)
                b.wait(b.ve, b.t_te)
                b.bmark(3 + p % 2, b.t_ve,
                        b.v(b.ve.scalar_tensor_tensor(
                            out=scl[:, p * 512:(p + 1) * 512], in0=ps[:, :], scalar=0.0,
                            in1=sch[:, p * 512:(p + 1) * 512],
                            op0=ALU.add, op1=ALU.subtract)))
            sc_mark_ve = b.t_ve.n
            sc_mark_se = b.t_se.n
            for tt in range(4):
                tok = 4 * g + tt
                b.wait(b.se, b.t_ve)
                b.inc(b.se.dma_start(out=xr[:], in_=rows3(src, tok)), s_xr, 16)
                b.wait(b.te, b.t_ve, sc_mark_ve)
                b.wait(b.te, b.t_se, sc_mark_se)
                b.bwait(6)
                b.bwait(7)
                for half in range(2):
                    ps = PS[6 + half]
                    for c in range(8):
                        ii = mm3(b, ps[:, :],
                                 sch[:, c * 512 + tt * 128: c * 512 + tt * 128 + 128],
                                 scl[:, c * 512 + tt * 128: c * 512 + tt * 128 + 128],
                                 woh[:, c * D + half * 512: c * D + (half + 1) * 512],
                                 wol[:, c * D + half * 512: c * D + (half + 1) * 512],
                                 start=(c == 0), stop=(c == 7))
                    b.t(ii)
                b.wait(b.ve, s_xr)
                b.wait(b.ve, s_bias)
                b.wait(b.ve, b.t_ve)
                b.v(b.ve.tensor_tensor(out=xpb[:], in0=xr[:], in1=bo_rep[:],
                                       op=ALU.add))
                b.wait(b.ve, b.t_te)
                b.wait(b.ve, b.t_ve)
                for half in range(2):
                    b.bmark(6 + half, b.t_ve,
                            b.v(b.ve.tensor_tensor(
                                out=resid[:, half * 512:(half + 1) * 512],
                                in0=PS[6 + half][:, :],
                                in1=xpb[:, half * 512:(half + 1) * 512], op=ALU.add)))
                b.wait(b.ve, s_out)
                layernorm_tile(b, outt[:], resid[:], xpb[:], stats[:, 0:5])
                b.wait(b.ge, b.t_ve)
                b.inc(b.ge.dma_start(out=rows3(dst, tok), in_=outt[:]), s_out, 16)
        b.barrier()
        b.rmchan(s_bias, s_xg, s_zb, s_zw, s_xr, s_out)



def _quant_scr(sb, width):
    return dict(xn=sb("q_xn", width), absx=sb("q_absx", width),
                acc=sb("q_acc", width), tmpb=sb("q_tmpb", width),
                bmax=sb("q_bmax", width // 16), scale=sb("q_scale", width // 16),
                hsc=sb("q_hsc", width // 16))


def slot4_wprep(b, C):
    nc, W = C["nc"], C["W"]
    with ExitStack() as ph:
        def sb(name, cols, dt=F32):
            return ph.enter_context(nc.sbuf_tensor(f"{name}P{Sem._uid[0]}", [128, cols], dt))
        wchunk = sb("wchunk", 2 * 2816)
        wq = sb("wq", 2816)
        tpsh = sb("tpsh", 22 * 128, BF16)
        tpsl = sb("tpsl", 22 * 128, BF16)
        scr = _quant_scr(sb, 2816)
        s_ld = [b.chan(ph, "wp_lda"), b.chan(ph, "wp_ldb")]
        s_st = b.chan(ph, "wp_st", sw=True)
        ld_marks = [0, 0]

        jobs = [("fp4_wg", C["scr_wgT"], 22, 1024, 8),
                ("fp4_wu", C["scr_wuT"], 22, 1024, 8),
                ("fp4_wd", C["scr_wdT"], 8, 2816, 22)]
        jn = 0
        for wname, scrT, nrow, wid, nkc in jobs:
            hbm = W[wname]
            for j in range(nrow):
                par = (jn % 2) * 2816
                b.wait(b.se, b.t_ve, ld_marks[jn % 2])
                b.inc(b.se.dma_start(
                    out=wchunk[:, par:par + wid],
                    in_=hbm.rearrange("(j p) k -> p j k", p=128)[:, j]),
                    s_ld[jn % 2], 16)
                b.wait(b.ve, s_ld[jn % 2])
                b.wait(b.ve, b.t_te)
                quant_tile(b, C, wq[:, :wid], wchunk[:, par:par + wid], scr, wid)
                ld_marks[jn % 2] = b.t_ve.n
                wq_mark = b.t_ve.n
                b.wait(b.se, s_st)
                b.wait(b.ve, s_st)
                for c in range(nkc):
                    bank = c % 4
                    b.bwait(bank)
                    b.wait(b.te, b.t_ve, wq_mark)
                    tp_pe_split(b, C, bank, 0,
                                tpsh[:, c * 128:(c + 1) * 128],
                                tpsl[:, c * 128:(c + 1) * 128],
                                wq[:, c * 128:(c + 1) * 128])
                b.wait(b.ge, b.t_se)
                b.wait(b.ge, b.t_ve)
                v3 = scrT.rearrange("(u p) n -> p u n", p=128)
                b.inc(b.ge.dma_start(
                    out=v3[:, 0:nkc, j * 128:(j + 1) * 128],
                    in_=tpsh[:, :nkc * 128].rearrange("p (c q) -> p c q", q=128)),
                    s_st, 16)
                b.inc(b.ge.dma_start(
                    out=v3[:, nkc:2 * nkc, j * 128:(j + 1) * 128],
                    in_=tpsl[:, :nkc * 128].rearrange("p (c q) -> p c q", q=128)),
                    s_st, 16)
                jn += 1
        b.barrier()
        b.rmchan(s_st, *s_ld)


def slot4(b, C, src, dst):
    nc, W, PS = C["nc"], C["W"], C["PS"]
    t_len, nt = C["t_len"], C["nt"]
    if not C.get("wq_done"):
        slot4_wprep(b, C)
        C["wq_done"] = True

    GRP = min(512, t_len)
    TPG = GRP // 128
    n_grp = t_len // GRP
    NSPANS = [512] * 5 + [256]

    with ExitStack() as ph:
        def sb(name, cols, dt=F32):
            return ph.enter_context(nc.sbuf_tensor(f"{name}F{Sem._uid[0]}", [128, cols], dt))
        xg = sb("xg", TPG * D)
        xqt = sb("xqt", D)
        xq_fmh = sb("xqfh", 8 * GRP, BF16)
        xq_fml = sb("xqfl", 8 * GRP, BF16)
        scr = _quant_scr(sb, D)
        wgbh = sb("wgbh", 2 * 8 * 512, BF16)
        wgbl = sb("wgbl", 2 * 8 * 512, BF16)
        wubh = sb("wubh", 2 * 8 * 512, BF16)
        wubl = sb("wubl", 2 * 8 * 512, BF16)
        sg = sb("sg", 2 * 512)
        hbuf = sb("hbuf", 512)
        hqc = sb("hqc", 2 * 512)
        hq_fmh = sb("hqfh", 22 * GRP, BF16)
        hq_fml = sb("hqfl", 22 * GRP, BF16)
        wdbh = sb("wdbh", 2 * 512, BF16)
        wdbl = sb("wdbl", 2 * 512, BF16)
        stats = sb("stats", 16)
        outt = sb("outt", TPG * D)
        s_xg = b.chan(ph, "xg4")
        s_wg = [b.chan(ph, "wg4a"), b.chan(ph, "wg4b")]
        s_wd = [b.chan(ph, "wd4a"), b.chan(ph, "wd4b")]
        s_out = b.chan(ph, "out4", sw=True)

        wgT3 = C["scr_wgT"].rearrange("(u p) n -> p u n", p=128)
        wuT3 = C["scr_wuT"].rearrange("(u p) n -> p u n", p=128)
        wdT3 = C["scr_wdT"].rearrange("(u p) n -> p u n", p=128)

        for grp in range(n_grp):
            t0 = grp * TPG
            b.wait(b.se, b.t_ve)
            b.wait(b.se, b.t_se)
            b.inc(b.se.dma_start(
                out=xg.rearrange("p (tt d) -> p tt d", d=D),
                in_=src.rearrange("(tt p) d -> p tt d", p=128)[:, t0:t0 + TPG]),
                s_xg, 16)
            b.wait(b.ve, s_xg)
            for tt in range(TPG):
                b.wait(b.ve, b.t_ve)
                b.wait(b.ve, b.t_te)
                quant_tile(b, C, xqt[:], xg[:, tt * D:(tt + 1) * D], scr, D)
                xq_mark = b.t_ve.n
                for c in range(8):
                    bank = 2 + c % 2
                    b.bwait(bank)
                    b.wait(b.te, b.t_ve, xq_mark)
                    tp_pe_split(b, C, bank, 0,
                                xq_fmh[:, c * GRP + tt * 128: c * GRP + tt * 128 + 128],
                                xq_fml[:, c * GRP + tt * 128: c * GRP + tt * 128 + 128],
                                xqt[:, c * 128:(c + 1) * 128])
            xqf_mark_se = b.t_se.n
            xqf_mark_ve = b.t_ve.n
            kk = 0
            mm_marks = [0, 0]
            sg_marks = [0, 0]
            hq_marks = [0, 0]
            hqtp_marks = [0, 0]
            wg_marks = [0, 0]
            for nch, span in enumerate(NSPANS):
                n0 = nch * 512
                wpar = (nch % 2) * 8 * 512
                b.wait(b.se, b.t_te, wg_marks[nch % 2])
                for (buf_h, buf_l, w3) in ((wgbh, wgbl, wgT3), (wubh, wubl, wuT3)):
                    b.inc(b.se.dma_start(
                        out=buf_h[:, wpar:wpar + 8 * span].rearrange("p (c q) -> p c q", q=span),
                        in_=w3[:, 0:8, n0:n0 + span]), s_wg[nch % 2], 16)
                    b.inc(b.se.dma_start(
                        out=buf_l[:, wpar:wpar + 8 * span].rearrange("p (c q) -> p c q", q=span),
                        in_=w3[:, 8:16, n0:n0 + span]), s_wg[nch % 2], 16)
                def s4_mm(tt):
                    pa_, pg_ = (0, 1) if tt % 2 == 0 else (4, 5)
                    b.wait(b.te, s_wg[nch % 2])
                    b.wait(b.te, b.t_se, xqf_mark_se)
                    b.wait(b.te, b.t_ve, xqf_mark_ve)
                    b.bwait(pa_)
                    b.bwait(pg_)
                    for ci in range(8):
                        mm3(b, PS[pa_][:, :span],
                            xq_fmh[:, ci * GRP + tt * 128: ci * GRP + tt * 128 + 128],
                            xq_fml[:, ci * GRP + tt * 128: ci * GRP + tt * 128 + 128],
                            wgbh[:, wpar + ci * span: wpar + (ci + 1) * span],
                            wgbl[:, wpar + ci * span: wpar + (ci + 1) * span],
                            start=(ci == 0), stop=(ci == 7))
                        ii = mm3(b, PS[pg_][:, :span],
                                 xq_fmh[:, ci * GRP + tt * 128: ci * GRP + tt * 128 + 128],
                                 xq_fml[:, ci * GRP + tt * 128: ci * GRP + tt * 128 + 128],
                                 wubh[:, wpar + ci * span: wpar + (ci + 1) * span],
                                 wubl[:, wpar + ci * span: wpar + (ci + 1) * span],
                                 start=(ci == 0), stop=(ci == 7))
                    mm_marks[tt % 2] = b.t(ii)
                    wg_marks[nch % 2] = mm_marks[tt % 2]

                def s4_quant_dve(tt):
                    pa_, pg_ = (0, 1) if tt % 2 == 0 else (4, 5)
                    par4 = (tt % 2) * 512
                    b.wait(b.se, b.t_te, mm_marks[tt % 2])
                    b.wait(b.se, b.t_ve, sg_marks[tt % 2])
                    b.bmark(pa_, b.t_se,
                            b.s(b.se.activation(out=sg[:, par4:par4 + span], in_=PS[pa_][:, :span],
                                                func=AF.Silu, bias=0.0, scale=1.0)))
                    b.wait(b.ve, b.t_se)
                    b.wait(b.ve, b.t_ve)
                    sg_marks[tt % 2] = b.v(b.ve.tensor_tensor(
                        out=hbuf[:, :span], in0=sg[:, par4:par4 + span],
                        in1=PS[pg_][:, :span], op=ALU.mult))
                    b.bmark(pg_, b.t_ve, sg_marks[tt % 2])
                    b.wait(b.ve, b.t_te, hqtp_marks[tt % 2])
                    quant_tile(b, C, hqc[:, par4:par4 + span], hbuf[:, :span],
                               scr, span)
                    hq_marks[tt % 2] = b.t_ve.n

                def s4_quant_tp(tt):
                    par4 = (tt % 2) * 512
                    b.wait(b.te, b.t_ve, hq_marks[tt % 2])
                    for c in range(span // 128):
                        bank = 6 + c % 2
                        b.bwait(bank)
                        tp_pe_split(b, C, bank, 0,
                                    hq_fmh[:, (kk + c) * GRP + tt * 128:
                                           (kk + c) * GRP + tt * 128 + 128],
                                    hq_fml[:, (kk + c) * GRP + tt * 128:
                                           (kk + c) * GRP + tt * 128 + 128],
                                    hqc[:, par4 + c * 128: par4 + (c + 1) * 128])
                    hqtp_marks[tt % 2] = b.t_te.n

                for tt in range(TPG):
                    s4_mm(tt)
                    if tt > 0:
                        s4_quant_dve(tt - 1)
                    if tt > 1:
                        s4_quant_tp(tt - 2)
                s4_quant_dve(TPG - 1)
                s4_quant_tp(TPG - 2)
                s4_quant_tp(TPG - 1)
                kk += span // 128
            hqf_se, hqf_ve = b.t_se.n, b.t_ve.n
            wd_marks = [0, 0]
            for co in range(2):
                for kch in range(22):
                    par = (kch % 2) * 512
                    b.wait(b.se, b.t_te, wd_marks[kch % 2])
                    b.inc(b.se.dma_start(out=wdbh[:, par:par + 512],
                                         in_=wdT3[:, kch, co * 512:(co + 1) * 512]),
                          s_wd[kch % 2], 16)
                    b.inc(b.se.dma_start(out=wdbl[:, par:par + 512],
                                         in_=wdT3[:, 22 + kch, co * 512:(co + 1) * 512]),
                          s_wd[kch % 2], 16)
                    b.wait(b.te, s_wd[kch % 2])
                    b.wait(b.te, b.t_se, hqf_se)
                    b.wait(b.te, b.t_ve, hqf_ve)
                    if kch == 0:
                        for tt2 in range(TPG):
                            b.bwait(4 + tt2)
                    for tt in range(TPG):
                        ii = mm3(b, PS[4 + tt][:, :],
                                 hq_fmh[:, kch * GRP + tt * 128: kch * GRP + tt * 128 + 128],
                                 hq_fml[:, kch * GRP + tt * 128: kch * GRP + tt * 128 + 128],
                                 wdbh[:, par:par + 512], wdbl[:, par:par + 512],
                                 start=(kch == 0), stop=(kch == 21))
                    wd_marks[kch % 2] = b.t(ii)
                for tt in range(TPG):
                    b.wait(b.ve, b.t_te)
                    b.wait(b.ve, s_xg)
                    b.bmark(4 + tt, b.t_ve,
                            b.v(b.ve.tensor_tensor(
                                out=xg[:, tt * D + co * 512: tt * D + co * 512 + 512],
                                in0=PS[4 + tt][:, :],
                                in1=xg[:, tt * D + co * 512: tt * D + co * 512 + 512],
                                op=ALU.add)))
            b.wait(b.ve, s_out)
            layernorm_quad(b, outt[:], xg[:], outt[:], xg[:],
                           stats[:, 0:12], ng=TPG)
            b.wait(b.ge, b.t_ve)
            b.inc(b.ge.dma_start(
                out=dst.rearrange("(tt p) d -> p tt d", p=128)[:, t0:t0 + TPG],
                in_=outt.rearrange("p (tt d) -> p tt d", d=D)), s_out, 16)
        b.barrier()
        b.rmchan(s_xg, s_out, *s_wg, *s_wd)



SLOT_FNS = {}


def build_program():
    t_len = _t()
    n_seq = _nseq()
    nt = t_len // 128

    nc = bass.Bass("TRN2", target_bir_lowering=False)

    x_in = nc.dram_tensor("x_in", [n_seq * t_len, D], F32, kind="ExternalInput").ap()
    y_out = nc.dram_tensor("y_out", [n_seq * t_len, D], F32, kind="ExternalOutput").ap()

    W = {}

    def w_in(name, shape):
        W[name] = nc.dram_tensor(name, shape, F32, kind="ExternalInput").ap()

    for m in "qkvo":
        w_in("l0a_w" + m, [D, D]); w_in("l0a_b" + m, [D])
        w_in("l1a_w" + m, [D, D]); w_in("l1a_b" + m, [D])
    w_in("ff_wqkv", [2, D, 3 * FF_INNER])
    w_in("ff_qs", [2, FF_DH]); w_in("ff_ks", [2, FF_DH])
    w_in("ff_wo", [2, FF_INNER, D])
    w_in("ff_lg", [2, D]); w_in("ff_lb", [2, D])
    w_in("ff_w1", [2, D, 2 * FF_HID]); w_in("ff_b1", [2, 2 * FF_HID])
    w_in("ff_w2", [2, FF_HID, D]); w_in("ff_b2", [2, D])
    w_in("fp4_wg", [2816, D]); w_in("fp4_wu", [2816, D]); w_in("fp4_wd", [D, 2816])
    w_in("mask_u", [128, 128]); w_in("mask_l", [128, 128]); w_in("ident", [128, 128])
    for nm in ("rc1", "rs1", "rc2", "rs2"):
        w_in(nm + "_hi", [t_len, FF_DH]); w_in(nm + "_lo", [t_len, FF_DH])

    ng = nt // 4
    scr_qk = nc.dram_tensor("scr_qk", [4 * ng * 128, 8 * 512], BF16, kind="Internal").ap()
    scr_vt = nc.dram_tensor("scr_vt", [2 * nt * 128, D], BF16, kind="Internal").ap()
    scr_a = nc.dram_tensor("scr_a", [t_len, D], F32, kind="Internal").ap()
    scr2_xa = nc.dram_tensor("scr2_xa", [t_len, D], F32, kind="Internal").ap()
    scr2_y = nc.dram_tensor("scr2_y", [t_len, D], F32, kind="Internal").ap()
    scr_w1h = nc.dram_tensor("scr_w1h", [2 * 1024, 5460], BF16, kind="Internal").ap()
    scr_wqh = nc.dram_tensor("scr_wqh", [2 * 1024, 1536], BF16, kind="Internal").ap()
    scr_wql = nc.dram_tensor("scr_wql", [2 * 1024, 1536], BF16, kind="Internal").ap()
    scr_woh = nc.dram_tensor("scr_woh", [2 * 512, 1024], BF16, kind="Internal").ap()
    scr_wol = nc.dram_tensor("scr_wol", [2 * 512, 1024], BF16, kind="Internal").ap()
    scr_w1l = nc.dram_tensor("scr_w1l", [2 * 1024, 5460], BF16, kind="Internal").ap()
    scr_w2h = nc.dram_tensor("scr_w2h", [2 * 2730, 1024], BF16, kind="Internal").ap()
    scr_w2l = nc.dram_tensor("scr_w2l", [2 * 2730, 1024], BF16, kind="Internal").ap()
    scr_v3 = nc.dram_tensor("scr_v3", [2 * nt * 128, 1040], BF16, kind="Internal").ap()
    scr_z = nc.dram_tensor("scr_z", [16, 512], F32, kind="Internal").ap()
    scr_wgT = nc.dram_tensor("scr_wgT", [16 * 128, 2816], BF16, kind="Internal").ap()
    scr_wuT = nc.dram_tensor("scr_wuT", [16 * 128, 2816], BF16, kind="Internal").ap()
    scr_wdT = nc.dram_tensor("scr_wdT", [44 * 128, 1024], BF16, kind="Internal").ap()
    scr_b2 = nc.dram_tensor("scr_b", [t_len, D], F32, kind="Internal").ap()

    b = Bld(nc)
    stk = b.stk

    PS = [stk.enter_context(nc.psum_tensor(f"ps{i}", [128, 512], F32))
          for i in range(8)]
    cpool = stk.enter_context(nc.sbuf_tensor("cpool", [128, 8], F32))
    for j, v in enumerate((1e-5, 1.0 / 1024.0, 1e-6, 1e-12)):
        b.v(b.ve.memset(cpool[:, j:j + 1], v))
        nc.const_aps.register(F32, v, cpool[:, j:j + 1])
    b.v(b.ve.memset(cpool[:, 6:7], 6.0))
    b.wait(b.ve, b.t_ve)
    b.v(b.ve.reciprocal(cpool[:, 7:8], cpool[:, 6:7]))
    maskU = stk.enter_context(nc.sbuf_tensor("maskU", [128, 128], F32))
    maskL = stk.enter_context(nc.sbuf_tensor("maskL", [128, 128], F32))
    ident = stk.enter_context(nc.sbuf_tensor("ident_sb", [128, 128], F32))
    b.inc(b.sy.dma_start(out=maskU[:], in_=W["mask_u"][:]), b.q_sy, 16)
    b.inc(b.sy.dma_start(out=maskL[:], in_=W["mask_l"][:]), b.q_sy, 16)
    b.inc(b.sy.dma_start(out=ident[:], in_=W["ident"][:]), b.q_sy, 16)

    C = dict(nc=nc, W=W, t_len=t_len, nt=nt, PS=PS, maskU=maskU, maskL=maskL,
             ident=ident,
             rsix=cpool[:, 7:8],
             scr_qk=scr_qk, scr_vt=scr_vt,
             scr_wgT=scr_wgT, scr_wuT=scr_wuT, scr_wdT=scr_wdT, scr_v3=scr_v3, scr_z=scr_z,
             scr2_xa=scr2_xa, scr2_y=scr2_y, scr_w1h=scr_w1h, scr_w1l=scr_w1l,
             scr_wqh=scr_wqh, scr_wql=scr_wql, scr_woh=scr_woh, scr_wol=scr_wol,
             scr_w2h=scr_w2h, scr_w2l=scr_w2l)

    slot_fns = {1: slot1, 2: slot2, 3: slot3, 4: slot4}
    slot_fns.update(SLOT_FNS)
    for s in range(n_seq):
        src = x_in.rearrange("(s2 t) d -> s2 t d", s2=n_seq)[s] if n_seq > 1 else x_in
        y_rows = y_out.rearrange("(s2 t) d -> s2 t d", s2=n_seq)[s] if n_seq > 1 else y_out
        active = [k for k in (1, 2, 3, 4) if k in DEV_SLOTS and k in slot_fns]
        cur = src
        for j, sl in enumerate(active):
            nxt = y_rows if j == len(active) - 1 else (scr_a if j % 2 == 0 else scr_b2)
            slot_fns[sl](b, C, cur, nxt)
            cur = nxt
        b.barrier()

    b.ge.wait_ge(b.q_ge.h, b.q_ge.n)
    return nc



def _prep_inputs(inputs, t_len, n_seq, n_cores):
    rot = _rotary_tables(t_len)
    mu, ml = _masks()
    base = {
        "ff_wqkv": np.ascontiguousarray(inputs["L0_ffn_wqkv"], np.float32),
        "ff_qs": np.ascontiguousarray(inputs["L0_ffn_q_scale"], np.float32),
        "ff_ks": np.ascontiguousarray(inputs["L0_ffn_k_scale"], np.float32),
        "ff_wo": np.ascontiguousarray(inputs["L0_ffn_wo"], np.float32),
        "ff_lg": np.ascontiguousarray(inputs["L0_ffn_ln_g"], np.float32),
        "ff_lb": np.ascontiguousarray(inputs["L0_ffn_ln_b"], np.float32),
        "ff_w1": np.ascontiguousarray(inputs["L0_ffn_w1"], np.float32),
        "ff_b1": np.ascontiguousarray(inputs["L0_ffn_b1"], np.float32),
        "ff_w2": np.ascontiguousarray(inputs["L0_ffn_w2"], np.float32),
        "ff_b2": np.ascontiguousarray(inputs["L0_ffn_b2"], np.float32),
        "fp4_wg": np.ascontiguousarray(inputs["L1_ffn_w_gate"], np.float32),
        "fp4_wu": np.ascontiguousarray(inputs["L1_ffn_w_up"], np.float32),
        "fp4_wd": np.ascontiguousarray(inputs["L1_ffn_w_down"], np.float32),
        "mask_u": mu, "mask_l": ml,
        "ident": np.eye(128, dtype=np.float32),
    }
    for m in "qkvo":
        base["l0a_w" + m] = np.ascontiguousarray(inputs["L0_attn_W" + m], np.float32)
        base["l0a_b" + m] = np.ascontiguousarray(inputs["L0_attn_b" + m], np.float32)
        base["l1a_w" + m] = np.ascontiguousarray(inputs["L1_attn_w" + m], np.float32)
        base["l1a_b" + m] = np.ascontiguousarray(inputs["L1_attn_b" + m], np.float32)
    base.update(rot)

    x = np.asarray(inputs["L0_attn_x"], dtype=np.float32)
    in_maps = []
    for c in range(n_cores):
        m = dict(base)
        xs = x[c * n_seq:(c + 1) * n_seq, :t_len, :]
        m["x_in"] = np.ascontiguousarray(xs.reshape(n_seq * t_len, D))
        in_maps.append(m)
    return in_maps


def kernel(**inputs):
    t_len = _t()
    n_seq = _nseq()
    n_cores = DEV_NCORES or N_CORES
    nc = build_program()
    in_maps = _prep_inputs(inputs, t_len, n_seq, n_cores)
    res = run_bass_kernel_spmd(nc, in_maps, core_ids=list(range(n_cores)))
    outs = [r["y_out"].reshape(n_seq, t_len, D) for r in res.results]
    return np.concatenate(outs, axis=0)


# version 18
# speedup vs baseline: 1.0204x; 1.0204x over previous
import math
from contextlib import ExitStack

import numpy as np

import concourse.bass as bass
import concourse.mybir as mybir
from concourse.bass_utils import run_bass_kernel_spmd

F32 = mybir.dt.float32
BF16 = mybir.dt.bfloat16
AF = mybir.ActivationFunctionType
ALU = mybir.AluOpType
AX = mybir.AxisListType

D = 1024
B_TOTAL = 16
T = 2048
N_CORES = 8

FF_HEADS = 8
FF_DH = 64
FF_WINDOW = 512
FF_INNER = 512
FF_HID = 2730
FF_QK_SCALE = 8.0
LA_HEADS = 16
LA_DH = 64
FP4_MID = [0.25, 0.75, 1.25, 1.75, 2.5, 3.5, 5.0]
FP4_DELTA = [0.5, 0.5, 0.5, 0.5, 1.0, 1.0, 2.0]

NEG_BIG = -3.0e38

DEV_T = None
DEV_SLOTS = (1, 2, 3, 4)
DEV_NSEQ = None
DEV_NCORES = None


def _t():
    return DEV_T or T


def _nseq():
    return DEV_NSEQ or (B_TOTAL // (DEV_NCORES or N_CORES))



def _rotary_tables(n):
    dh = FF_DH
    half = np.arange(0, dh, 2, dtype=np.float64)
    inv_freq = 1.0 / 10000.0 ** (half / dh)
    t = np.arange(n, dtype=np.float64)
    freqs = t[:, None] * inv_freq[None, :]
    freqs = np.concatenate([freqs, freqs], axis=-1)
    base = (half + 0.4 * dh) / (1.4 * dh)
    power = (t - n // 2) / float(FF_WINDOW // 2)
    scale = base[None, :] ** power[:, None]
    scale = np.concatenate([scale, scale], axis=-1)
    cos, sin = np.cos(freqs), np.sin(freqs)
    c1, s1 = cos * scale, sin * scale
    c2, s2 = cos / scale, sin / scale
    s1f = s1.copy(); s1f[:, : dh // 2] *= -1.0
    s2f = s2.copy(); s2f[:, : dh // 2] *= -1.0
    out = {}
    for name, arr in (("rc1", c1), ("rs1", s1f), ("rc2", c2), ("rs2", s2f)):
        hi = arr.astype(np.float32)
        lo = (arr - hi.astype(np.float64)).astype(np.float32)
        out[name + "_hi"] = hi
        out[name + "_lo"] = lo
    return out


def _masks():
    i = np.arange(128)[:, None]
    c = np.arange(128)[None, :]
    u = np.where(c > i, NEG_BIG, 0.0).astype(np.float32)
    l = np.where(c < i, NEG_BIG, 0.0).astype(np.float32)
    return u, l



class Sem:
    __slots__ = ("h", "n", "name", "sw")
    _uid = [0]

    def __init__(self, stk, nc, name):
        Sem._uid[0] += 1
        self.name = f"{name}_{Sem._uid[0]}"
        self.h = stk.enter_context(nc.semaphore(self.name))
        self.n = 0
        self.sw = False


class Bld:
    def __init__(self, nc):
        self.nc = nc
        self.stk = ExitStack()
        self.te, self.ve, self.se, self.ge, self.sy = (
            nc.tensor, nc.vector, nc.scalar, nc.gpsimd, nc.sync)
        self.q_sy = Sem(self.stk, nc, "q_sy")
        self.q_se = Sem(self.stk, nc, "q_se")
        self.q_ve = Sem(self.stk, nc, "q_ve")
        self.q_ge = Sem(self.stk, nc, "q_ge")
        self.t_te = Sem(self.stk, nc, "t_te")
        self.t_ve = Sem(self.stk, nc, "t_ve")
        self.t_se = Sem(self.stk, nc, "t_se")
        self.t_ge = Sem(self.stk, nc, "t_ge")
        self._all = [self.q_sy, self.q_se, self.q_ve, self.q_ge,
                     self.t_te, self.t_ve, self.t_se, self.t_ge]
        self._waited = {}
        self._pool_hw = []
        self._pool_sw = []
        self.bank_marks = [dict() for _ in range(8)]
        self._engs = [self.te, self.ve, self.se, self.ge, self.sy]

    def wait(self, eng, sem, val=None):
        val = sem.n if val is None else val
        if val <= 0:
            return
        key = (id(eng), sem.name)
        if self._waited.get(key, -1) >= val:
            return
        eng.wait_ge(sem.h, val)
        self._waited[key] = val

    def inc(self, instr, sem, k=1):
        instr.then_inc(sem.h, k)
        sem.n += k
        return sem.n

    def v(self, i):
        return self.inc(i, self.t_ve)

    def s(self, i):
        return self.inc(i, self.t_se)

    def g(self, i):
        return self.inc(i, self.t_ge)

    def t(self, i):
        return self.inc(i, self.t_te)

    def chan(self, stk, name, sw=False):
        pool = self._pool_sw if sw else self._pool_hw
        if pool:
            return pool.pop()
        s = Sem(self.stk, self.nc, name)
        s.sw = sw
        self._all.append(s)
        return s

    def rmchan(self, *sems):
        for s in sems:
            (self._pool_sw if s.sw else self._pool_hw).append(s)

    def bmark(self, i, sem, val):
        d = self.bank_marks[i]
        if d.get(sem.name, (None, -1))[1] < val:
            d[sem.name] = (sem, val)

    def bwait(self, i):
        for sem, val in list(self.bank_marks[i].values()):
            self.wait(self.te, sem, val)

    def barrier(self):
        bar = Sem(self.stk, self.nc, "bar")
        for e in self._engs:
            for q in self._all:
                self.wait(e, q)
            e.nop().then_inc(bar.h, 1)
        for e in self._engs:
            e.wait_ge(bar.h, 5)
        for e in self._engs:
            for q in self._all:
                self._waited[(id(e), q.name)] = q.n



def split_hilo(b, dst_h, dst_l, src_ap):
    b.wait(b.ve, b.t_ve)
    b.v(b.ve.tensor_copy(dst_h, src_ap))
    b.wait(b.ve, b.t_ve)
    b.v(b.ve.tensor_tensor(out=dst_l, in0=src_ap, in1=dst_h, op=ALU.subtract))


def tp128s(b, sem, dst_ap, src_ap):
    b.inc(b.sy.dma_start_transpose(dst_ap, src_ap), sem, 16)


def mm3(b, out_ap, lh, ll, rh, rl, start, stop):
    b.te.matmul(out_ap, lh, rh, start=start, stop=False)
    b.te.matmul(out_ap, lh, rl, start=False, stop=False)
    return b.te.matmul(out_ap, ll, rh, start=False, stop=stop)


def load_w_hilo(b, stk, wf_stage, wh, wl, hbm_ap, n_chunks, chunk_cols,
                dst_col0=0):
    s_wf = b.chan(stk, "wf")
    s_wh = b.chan(stk, "whc", sw=True)
    last_split = 0
    src3 = hbm_ap.rearrange("(ci p) n -> p ci n", p=128)
    for ci in range(n_chunks):
        par = (ci % 2) * chunk_cols
        b.wait(b.se, b.t_ve)
        b.wait(b.ge, b.t_ve)
        b.inc(b.se.dma_start(out=wf_stage[:, par:par + chunk_cols],
                             in_=src3[:, ci]), s_wf, 16)
        b.inc(b.ge.dma_start(
            out=wh[:, dst_col0 + ci * chunk_cols: dst_col0 + (ci + 1) * chunk_cols],
            in_=src3[:, ci]), s_wh, 16)
        b.wait(b.ve, s_wf)
        b.wait(b.ve, s_wh)
        last_split = b.v(b.ve.tensor_tensor(
            out=wl[:, dst_col0 + ci * chunk_cols: dst_col0 + (ci + 1) * chunk_cols],
            in0=wf_stage[:, par:par + chunk_cols],
            in1=wh[:, dst_col0 + ci * chunk_cols: dst_col0 + (ci + 1) * chunk_cols],
            op=ALU.subtract))
    return [(s_wh, s_wh.n), (b.t_ve, last_split)]


def bias_rep_load(b, sem, dst, hbm_1d, width):
    b.inc(b.se.dma_start(out=dst,
                         in_=bass.AP(hbm_1d.tensor, hbm_1d.offset,
                                     [[0, 128], [1, width]])), sem, 16)


def layernorm_tile(b, out_ap, src_ap, scr_sq, scr_s, width=D):
    s_sum, s_negm, s_ss, s_sig, s_rsig = (scr_s[:, i:i + 1] for i in range(5))
    b.wait(b.ve, b.t_ve)
    b.v(b.ve.reduce_sum(s_sum, src_ap, axis=AX.X))
    b.wait(b.ve, b.t_ve)
    b.v(b.ve.tensor_scalar_mul(s_negm, s_sum, -1.0 / width))
    b.wait(b.se, b.t_ve)
    b.wait(b.se, b.t_se)
    b.s(b.se.activation(out=scr_sq, in_=src_ap, func=AF.Square,
                        bias=s_negm, scale=1.0, accum_out=s_ss))
    b.wait(b.se, b.t_se)
    b.s(b.se.activation(out=s_sig, in_=s_ss, func=AF.Sqrt,
                        bias=1e-5, scale=1.0 / width))
    b.wait(b.ve, b.t_se)
    b.v(b.ve.reciprocal(s_rsig, s_sig))
    b.wait(b.ve, b.t_ve)
    b.v(b.ve.tensor_scalar(out=out_ap, in0=src_ap,
                           scalar1=s_negm, scalar2=s_rsig,
                           op0=ALU.add, op1=ALU.mult))


def layernorm_quad(b, out_ap, src_ap, scr_xc, scr_sq, scr_s, ng=4):
    sums, negm, rst = scr_s[:, 0:ng], scr_s[:, ng:2 * ng], scr_s[:, 2 * ng:3 * ng]
    g3 = lambda t: t.rearrange("p (g c) -> p g c", c=D)
    b.wait(b.ve, b.t_ve)
    b.v(b.ve.reduce_sum(sums, g3(src_ap), axis=AX.X))
    b.wait(b.ve, b.t_ve)
    b.v(b.ve.tensor_scalar_mul(negm, sums, -1.0 / D))
    nb = negm.rearrange("p g -> p g ()").broadcast_to([128, ng, D])
    b.wait(b.ve, b.t_ve)
    b.v(b.ve.tensor_tensor(out=g3(scr_xc), in0=g3(src_ap), in1=nb, op=ALU.add))
    b.wait(b.se, b.t_ve)
    b.wait(b.se, b.t_se)
    b.s(b.se.activation(out=scr_sq, in_=scr_xc, func=AF.Square,
                        bias=0.0, scale=1.0))
    b.wait(b.ve, b.t_se)
    b.v(b.ve.reduce_sum(sums, g3(scr_sq), axis=AX.X))
    b.wait(b.se, b.t_ve)
    b.s(b.se.activation(out=negm, in_=sums, func=AF.Sqrt,
                        bias=1e-5, scale=1.0 / D))
    b.wait(b.ve, b.t_se)
    b.v(b.ve.reciprocal(rst, negm))
    rb = rst.rearrange("p g -> p g ()").broadcast_to([128, ng, D])
    b.wait(b.ve, b.t_ve)
    b.v(b.ve.tensor_tensor(out=g3(out_ap), in0=g3(scr_xc), in1=rb, op=ALU.mult))


def quant_tile(b, C, out_ap, src_ap, scr, width):
    nb = width // 16
    xn, absx, acc, tmpb = scr["xn"], scr["absx"], scr["acc"], scr["tmpb"]
    bmax, scale, hsc = scr["bmax"], scr["scale"], scr["hsc"]
    src3 = src_ap.rearrange("p (g s) -> p g s", s=16)
    b.wait(b.ve, b.t_ve)
    b.v(b.ve.reduce_max(bmax[:, :nb], src3, axis=AX.X,
                        apply_absolute_value=True))
    b.wait(b.ve, b.t_ve)
    b.v(b.ve.tensor_scalar(out=scale[:, :nb], in0=bmax[:, :nb],
                           scalar1=C["rsix"], scalar2=1e-12,
                           op0=ALU.mult, op1=ALU.max))
    b.wait(b.ve, b.t_ve)
    b.v(b.ve.reciprocal(bmax[:, :nb], scale[:, :nb]))
    rs_b = bmax[:, :nb].rearrange("p g -> p g ()").broadcast_to([128, nb, 16])
    b.wait(b.ve, b.t_ve)
    b.v(b.ve.tensor_tensor(out=xn[:, :width], in0=src_ap, in1=rs_b,
                           op=ALU.mult))
    b.wait(b.se, b.t_ve)
    b.wait(b.se, b.t_se)
    b.s(b.se.activation(out=absx[:, :width], in_=xn[:, :width], func=AF.Abs,
                        bias=0.0, scale=1.0))
    b.s(b.se.activation(out=tmpb[:, :width], in_=xn[:, :width], func=AF.Sign,
                        bias=0.0, scale=1.0))
    for j, (mid, wj) in enumerate(zip(FP4_MID, (1.0, 1.0, 1.0, 1.0, 2.0, 2.0, 4.0))):
        b.wait(b.ve, b.t_ve)
        b.wait(b.ve, b.t_se)
        if j == 0:
            b.v(b.ve.tensor_scalar(out=acc[:, :width], in0=absx[:, :width],
                                   scalar1=mid, scalar2=None, op0=ALU.is_gt))
        elif wj == 1.0:
            b.v(b.ve.scalar_tensor_tensor(out=acc[:, :width],
                                          in0=absx[:, :width], scalar=mid,
                                          in1=acc[:, :width],
                                          op0=ALU.is_gt, op1=ALU.add))
        else:
            b.v(b.ve.tensor_scalar(out=xn[:, :width], in0=absx[:, :width],
                                   scalar1=mid, scalar2=None, op0=ALU.is_gt))
            b.wait(b.ve, b.t_ve)
            b.v(b.ve.scalar_tensor_tensor(out=acc[:, :width],
                                          in0=xn[:, :width], scalar=wj,
                                          in1=acc[:, :width],
                                          op0=ALU.mult, op1=ALU.add))
    b.wait(b.ve, b.t_ve)
    b.v(b.ve.tensor_scalar_mul(hsc[:, :nb], scale[:, :nb], 0.5))
    hs_b = hsc[:, :nb].rearrange("p g -> p g ()").broadcast_to([128, nb, 16])
    b.wait(b.ve, b.t_ve)
    b.v(b.ve.tensor_tensor(out=acc[:, :width], in0=acc[:, :width],
                           in1=hs_b, op=ALU.mult))
    b.wait(b.ve, b.t_ve)
    b.v(b.ve.tensor_tensor(out=out_ap, in0=acc[:, :width],
                           in1=tmpb[:, :width], op=ALU.mult))


def tp_pe_split(b, C, bank, col, dst_h, dst_l, src_f32_ap):
    ps = C["PS"][bank]
    sl = ps[:, col * 128:(col + 1) * 128]
    b.wait(b.te, b.q_sy)
    b.t(b.te.transpose(sl, src_f32_ap, C["ident"][:]))
    b.wait(b.se, b.t_te)
    b.bmark(bank, b.t_se,
            b.s(b.se.activation(out=dst_h, in_=sl, func=AF.Identity,
                                bias=0.0, scale=1.0)))
    b.wait(b.ve, b.t_se)
    b.bmark(bank, b.t_ve,
            b.v(b.ve.scalar_tensor_tensor(out=dst_l, in0=sl, scalar=0.0,
                                          in1=dst_h, op0=ALU.add,
                                          op1=ALU.subtract)))


def plane(scr, u):
    return scr.rearrange("(u p) c -> p u c", p=128)[:, u]


def rows3(hbm, u, p=128):
    return hbm.rearrange("(u p) d -> p u d", p=p)[:, u]



def slot1(b, C, src, dst):
    nc, W, PS = C["nc"], C["W"], C["PS"]
    t_len, nt = C["t_len"], C["nt"]
    ng = nt // 4
    scr_qk = C["scr_qk"]
    scr_vt = C["scr_vt"]
    maskU = C["maskU"]

    with ExitStack() as pa:
        def sba(name, cols, dt=F32):
            return pa.enter_context(nc.sbuf_tensor(f"{name}A{Sem._uid[0]}", [128, cols], dt))
        wf = sba("wf", 2 * D)
        wh = {m: sba(f"w{m}h", 8 * D, BF16) for m in "qkv"}
        wl = {m: sba(f"w{m}l", 8 * D, BF16) for m in "qkv"}
        bqkv = sba("bqkv", 3 * 8)
        s_bias = b.chan(pa, "bias")
        wsems = []
        for mi, m in enumerate("qkv"):
            wsems += list(load_w_hilo(b, pa, wf, wh[m], wl[m], W["l0a_w" + m], 8, D))
            b.inc(b.se.dma_start(
                out=bqkv[:, mi * 8:(mi + 1) * 8],
                in_=W["l0a_b" + m].rearrange("(cc p) -> p cc", p=128),
                allow_slow_non_contiguous=True), s_bias, 16)

        xg = sba("xg", 4 * D)
        xfh = sba("xfh", 8 * 512, BF16)
        xfl = sba("xfl", 8 * 512, BF16)
        prh = sba("prh", 2 * 8 * 512, BF16)
        prl = sba("prl", 2 * 8 * 512, BF16)
        vtgh = sba("vtgh", 4 * D, BF16)
        vtgl = sba("vtgl", 4 * D, BF16)
        s_xg = b.chan(pa, "xg")
        s_spq = [b.chan(pa, "spqa", sw=True), b.chan(pa, "spqb", sw=True)]
        s_spv = b.chan(pa, "spv", sw=True)
        bv_rep = sba("bv_rep", D)
        bias_rep_load(b, s_bias, bv_rep[:], W["l0a_bv"], D)
        vtf = sba("vtf", 512)

        for g in range(ng):
            b.wait(b.se, b.t_te)
            b.inc(b.se.dma_start(
                out=xg.rearrange("p (tt d) -> p tt d", d=D),
                in_=src.rearrange("(tt p) d -> p tt d", p=128)[:, 4 * g:4 * g + 4]),
                s_xg, 16)
            b.wait(b.te, s_xg)
            for tt in range(4):
                for c in range(8):
                    bank = 6 + ((tt * 8 + c) % 2)
                    b.bwait(bank)
                    tp_pe_split(b, C, bank, 0,
                                xfh[:, c * 512 + tt * 128: c * 512 + tt * 128 + 128],
                                xfl[:, c * 512 + tt * 128: c * 512 + tt * 128 + 128],
                                xg[:, tt * D + c * 128: tt * D + c * 128 + 128])
            b.wait(b.te, b.t_se)
            b.wait(b.te, b.t_ve)
            for sem, val in wsems:
                b.wait(b.te, sem, val)
            for mi, m in enumerate("qk"):
                mo = mi * 8 * 512
                for cc in range(8):
                    ps = PS[cc % 2]
                    b.bwait(cc % 2)
                    for ci in range(8):
                        i = mm3(b, ps[:, :],
                                wh[m][:, ci * D + cc * 128: ci * D + cc * 128 + 128],
                                wl[m][:, ci * D + cc * 128: ci * D + cc * 128 + 128],
                                xfh[:, ci * 512:(ci + 1) * 512],
                                xfl[:, ci * 512:(ci + 1) * 512],
                                start=(ci == 0), stop=(ci == 7))
                    b.t(i)
                    oh = prh[:, mo + cc * 512: mo + (cc + 1) * 512]
                    ol = prl[:, mo + cc * 512: mo + (cc + 1) * 512]
                    b.wait(b.se, b.t_te)
                    b.wait(b.se, s_bias)
                    b.wait(b.se, s_spq[mi])
                    b.bmark(cc % 2, b.t_se,
                            b.s(b.se.activation(out=oh, in_=ps[:, :], func=AF.Identity,
                                                bias=bqkv[:, mi * 8 + cc: mi * 8 + cc + 1],
                                                scale=1.0)))
                    b.wait(b.ve, b.t_te)
                    b.wait(b.ve, b.t_se)
                    b.wait(b.ve, s_spq[mi])
                    b.bmark(cc % 2, b.t_ve,
                            b.v(b.ve.scalar_tensor_tensor(
                                out=ol, in0=ps[:, :],
                                scalar=bqkv[:, mi * 8 + cc: mi * 8 + cc + 1],
                                in1=oh, op0=ALU.add, op1=ALU.subtract)))
                base = (0 if m == "q" else 2 * ng)
                b.wait(b.ge, b.t_ve)
                b.wait(b.ge, b.t_se)
                b.inc(b.ge.dma_start(out=plane(scr_qk, base + 2 * g),
                                     in_=prh[:, mo:mo + 8 * 512]), s_spq[mi], 16)
                b.inc(b.ge.dma_start(out=plane(scr_qk, base + 2 * g + 1),
                                     in_=prl[:, mo:mo + 8 * 512]), s_spq[mi], 16)
            for tt in range(4):
                for half in range(2):
                    bk = 2 + (tt * 2 + half) % 2
                    b.bwait(bk)
                    ps = PS[bk]
                    for ci in range(8):
                        i = mm3(b, ps[:, :],
                                xfh[:, ci * 512 + tt * 128: ci * 512 + tt * 128 + 128],
                                xfl[:, ci * 512 + tt * 128: ci * 512 + tt * 128 + 128],
                                wh["v"][:, ci * D + half * 512: ci * D + half * 512 + 512],
                                wl["v"][:, ci * D + half * 512: ci * D + half * 512 + 512],
                                start=(ci == 0), stop=(ci == 7))
                    b.t(i)
                    b.wait(b.ve, b.t_te)
                    b.wait(b.ve, b.t_ve)
                    b.wait(b.ve, s_bias)
                    cnt = b.v(b.ve.tensor_tensor(
                        out=vtf[:], in0=ps[:, :],
                        in1=bv_rep[:, half * 512:(half + 1) * 512], op=ALU.add))
                    b.bmark(bk, b.t_ve, cnt)
                    b.wait(b.ve, s_spv)
                    split_hilo(b, vtgh[:, tt * D + half * 512: tt * D + (half + 1) * 512],
                               vtgl[:, tt * D + half * 512: tt * D + (half + 1) * 512],
                               vtf[:])
            b.wait(b.ge, b.t_ve)
            for tt in range(4):
                tok = 4 * g + tt
                b.inc(b.ge.dma_start(out=plane(scr_vt, 2 * tok),
                                     in_=vtgh[:, tt * D:(tt + 1) * D]),
                      s_spv, 16)
                b.inc(b.ge.dma_start(out=plane(scr_vt, 2 * tok + 1),
                                     in_=vtgl[:, tt * D:(tt + 1) * D]),
                      s_spv, 16)
        b.barrier()
        b.rmchan(s_bias, s_xg, s_spv, *s_spq)

    with ExitStack() as pb:
        def sb(name, cols, dt=F32):
            return pb.enter_context(nc.sbuf_tensor(f"{name}B{Sem._uid[0]}", [128, cols], dt))
        wf = sb("wf", 2 * D)
        woh = sb("woh", 8 * D, BF16)
        wol = sb("wol", 8 * D, BF16)
        wsems = list(load_w_hilo(b, pb, wf, woh, wol, W["l0a_wo"], 8, D))
        bo_rep = sb("bo_rep", D)
        s_bias = b.chan(pb, "bias")
        bias_rep_load(b, s_bias, bo_rep[:], W["l0a_bo"], D)

        qgh = sb("qgh", 8 * 512, BF16)
        qgl = sb("qgl", 8 * 512, BF16)
        kbh = sb("kbh", 2 * 8 * 512, BF16)
        kbl = sb("kbl", 2 * 8 * 512, BF16)
        vbh = sb("vbh", 2 * D, BF16)
        vbl = sb("vbl", 2 * D, BF16)
        s_sb = sb("s_sb", t_len)
        p_f32 = sb("p_f32", t_len)
        pth = sb("pth", nt * 128, BF16)
        ptl = sb("ptl", nt * 128, BF16)
        ogh = sb("ogh", 8 * 512, BF16)
        ogl = sb("ogl", 8 * 512, BF16)
        stats = sb("stats", 8)
        xr = sb("xr", D)
        xpb = sb("xpb", D)
        resid = sb("resid", D)
        outt = sb("outt", D)
        s_qg = b.chan(pb, "qg")
        s_kb = [b.chan(pb, "kba"), b.chan(pb, "kbb")]
        s_vb = [b.chan(pb, "vba"), b.chan(pb, "vbb")]
        s_xr = b.chan(pb, "xr")
        kb_marks = [0, 0]
        vb_marks = [0, 0]
        s_out = b.chan(pb, "sout", sw=True)

        for g in range(ng):
            b.wait(b.se, b.t_te)
            b.inc(b.se.dma_start(out=qgh[:], in_=plane(scr_qk, 2 * g)), s_qg, 16)
            b.inc(b.se.dma_start(out=qgl[:], in_=plane(scr_qk, 2 * g + 1)), s_qg, 16)
            for iq in range(4):
                i = 4 * g + iq
                L = (i + 1) * 128
                nkg = (L + 511) // 512
                for kg in range(nkg):
                    w_keys = min(512, L - kg * 512)
                    kpar = (kg % 2) * 8 * 512
                    b.wait(b.se, b.t_te, kb_marks[kg % 2])
                    b.inc(b.se.dma_start(out=kbh[:, kpar:kpar + 8 * 512],
                                         in_=plane(scr_qk, 2 * ng + 2 * kg)), s_kb[kg % 2], 16)
                    b.inc(b.se.dma_start(out=kbl[:, kpar:kpar + 8 * 512],
                                         in_=plane(scr_qk, 2 * ng + 2 * kg + 1)), s_kb[kg % 2], 16)
                    ps = PS[kg % 2]
                    b.wait(b.te, s_kb[kg % 2])
                    b.wait(b.te, s_qg)
                    b.wait(b.te, b.t_ve)
                    for ci in range(8):
                        ii = mm3(b, ps[:, :w_keys],
                                 qgh[:, ci * 512 + iq * 128: ci * 512 + iq * 128 + 128],
                                 qgl[:, ci * 512 + iq * 128: ci * 512 + iq * 128 + 128],
                                 kbh[:, kpar + ci * 512: kpar + ci * 512 + w_keys],
                                 kbl[:, kpar + ci * 512: kpar + ci * 512 + w_keys],
                                 start=(ci == 0), stop=(ci == 7))
                    kb_marks[kg % 2] = b.t(ii)
                    b.wait(b.ve, b.t_te)
                    b.v(b.ve.tensor_scalar_mul(
                        s_sb[:, kg * 512: kg * 512 + w_keys], ps[:, :w_keys],
                        1.0 / 32.0))
                b.wait(b.ve, b.t_ve)
                b.v(b.ve.tensor_tensor(out=s_sb[:, L - 128:L],
                                       in0=s_sb[:, L - 128:L],
                                       in1=maskU[:], op=ALU.add))
                b.wait(b.ve, b.t_ve)
                b.v(b.ve.reduce_max(stats[:, 0:1], s_sb[:, :L], axis=AX.X,
                                    negate=True))
                b.wait(b.se, b.t_ve)
                b.wait(b.se, b.t_se)
                b.wait(b.se, b.t_te)
                b.s(b.se.activation(out=p_f32[:, :L], in_=s_sb[:, :L],
                                    func=AF.Exp, bias=stats[:, 0:1], scale=1.0,
                                    accum_out=stats[:, 1:2]))
                b.wait(b.ve, b.t_se)
                b.wait(b.ve, b.t_te)
                b.v(b.ve.reciprocal(stats[:, 2:3], stats[:, 1:2]))
                b.wait(b.ve, b.t_ve)
                b.v(b.ve.tensor_scalar(out=p_f32[:, :L], in0=p_f32[:, :L],
                                       scalar1=stats[:, 2:3], scalar2=None,
                                       op0=ALU.mult))
                pn_mark = b.t_ve.n
                for sc in range(i + 1):
                    bank = 2 + sc % 2
                    b.bwait(bank)
                    b.wait(b.te, b.t_ve, pn_mark)
                    tp_pe_split(b, C, bank, 0,
                                pth[:, sc * 128:(sc + 1) * 128],
                                ptl[:, sc * 128:(sc + 1) * 128],
                                p_f32[:, sc * 128:(sc + 1) * 128])
                pt_mark_se = b.t_se.n
                pt_mark_ve = b.t_ve.n
                for sc in range(i + 1):
                    par = (sc % 2) * D
                    b.wait(b.se, b.t_te, vb_marks[sc % 2])
                    b.inc(b.se.dma_start(out=vbh[:, par:par + D],
                                         in_=plane(scr_vt, 2 * sc)), s_vb[sc % 2], 16)
                    b.inc(b.se.dma_start(out=vbl[:, par:par + D],
                                         in_=plane(scr_vt, 2 * sc + 1)), s_vb[sc % 2], 16)
                    b.wait(b.te, s_vb[sc % 2])
                    b.wait(b.te, b.t_se, pt_mark_se)
                    b.wait(b.te, b.t_ve, pt_mark_ve)
                    if sc == 0:
                        b.bwait(4)
                        b.bwait(5)
                    for half in range(2):
                        ii = mm3(b, PS[4 + half][:, :],
                                 pth[:, sc * 128:(sc + 1) * 128],
                                 ptl[:, sc * 128:(sc + 1) * 128],
                                 vbh[:, par + half * 512: par + (half + 1) * 512],
                                 vbl[:, par + half * 512: par + (half + 1) * 512],
                                 start=(sc == 0), stop=(sc == i))
                    vb_marks[sc % 2] = b.t(ii)
                b.wait(b.se, b.t_te)
                o_t = p_f32
                for half in range(2):
                    b.bmark(4 + half, b.t_se,
                            b.s(b.se.activation(out=o_t[:, half * 512:(half + 1) * 512],
                                                in_=PS[4 + half][:, :], func=AF.Identity,
                                                bias=0.0, scale=1.0)))
                ot_mark = b.t_se.n
                for c in range(8):
                    bank = 2 + c % 2
                    b.bwait(bank)
                    b.wait(b.te, b.t_se, ot_mark)
                    tp_pe_split(b, C, bank, 0,
                                ogh[:, c * 512 + iq * 128: c * 512 + iq * 128 + 128],
                                ogl[:, c * 512 + iq * 128: c * 512 + iq * 128 + 128],
                                o_t[:, c * 128:(c + 1) * 128])
            og_mark_se = b.t_se.n
            og_mark_ve = b.t_ve.n
            for tt in range(4):
                tok = 4 * g + tt
                b.wait(b.se, b.t_ve)
                b.inc(b.se.dma_start(out=xr[:], in_=rows3(src, tok)), s_xr, 16)
                b.wait(b.te, b.t_se, og_mark_se)
                b.wait(b.te, b.t_ve, og_mark_ve)
                for sem, val in wsems:
                    b.wait(b.te, sem, val)
                b.bwait(6)
                b.bwait(7)
                for half in range(2):
                    ps = PS[6 + half]
                    for c in range(8):
                        ii = mm3(b, ps[:, :],
                                 ogh[:, c * 512 + tt * 128: c * 512 + tt * 128 + 128],
                                 ogl[:, c * 512 + tt * 128: c * 512 + tt * 128 + 128],
                                 woh[:, c * D + half * 512: c * D + (half + 1) * 512],
                                 wol[:, c * D + half * 512: c * D + (half + 1) * 512],
                                 start=(c == 0), stop=(c == 7))
                    b.t(ii)
                b.wait(b.ve, s_xr)
                b.wait(b.ve, s_bias)
                b.wait(b.ve, b.t_ve)
                b.v(b.ve.tensor_tensor(out=xpb[:], in0=xr[:], in1=bo_rep[:],
                                       op=ALU.add))
                b.wait(b.ve, b.t_te)
                b.wait(b.ve, b.t_ve)
                for half in range(2):
                    b.bmark(6 + half, b.t_ve,
                            b.v(b.ve.tensor_tensor(
                                out=resid[:, half * 512:(half + 1) * 512],
                                in0=PS[6 + half][:, :],
                                in1=xpb[:, half * 512:(half + 1) * 512], op=ALU.add)))
                b.wait(b.ve, s_out)
                layernorm_tile(b, outt[:], resid[:], xpb[:], stats[:, 3:8])
                b.wait(b.ge, b.t_ve)
                b.inc(b.ge.dma_start(out=rows3(dst, tok), in_=outt[:]), s_out, 16)
        b.barrier()
        b.rmchan(s_bias, s_qg, s_xr, s_out, *s_kb, *s_vb)



def load_shifted(b, sem, eng, dst, src, t0, ntiles):
    d3 = dst.rearrange("p (tt d) -> p tt d", d=D)
    base = src.offset
    t = src.tensor
    b.inc(eng.dma_start(
        out=d3[:, :, 0:512],
        in_=bass.AP(t, base + t0 * D, [[D, 128], [128 * D, ntiles], [1, 512]])),
        sem, 16)
    if t0 > 0:
        b.inc(eng.dma_start(
            out=d3[:, :, 512:1024],
            in_=bass.AP(t, base + (t0 - 1) * D + 512,
                        [[D, 128], [128 * D, ntiles], [1, 512]])), sem, 16)
    else:
        b.v(b.ve.memset(d3[0:1, 0, 512:1024], 0.0))
        b.inc(eng.dma_start(
            out=d3[1:128, 0, 512:1024],
            in_=bass.AP(t, base + 512, [[D, 127], [1, 512]])), sem, 16)
        if ntiles > 1:
            b.inc(eng.dma_start(
                out=d3[:, 1:ntiles, 512:1024],
                in_=bass.AP(t, base + 127 * D + 512,
                            [[D, 128], [128 * D, ntiles - 1], [1, 512]])), sem, 16)


def slot2_wprep(b, C):
    nc, W = C["nc"], C["W"]
    with ExitStack() as ph:
        def sb(name, cols, dt=F32):
            return ph.enter_context(nc.sbuf_tensor(f"{name}W2{Sem._uid[0]}", [128, cols], dt))
        stage = sb("stage", 2 * 5460)
        sh = sb("sh", 2 * 5460, BF16)
        sl = sb("sl", 2 * 5460, BF16)
        s_ld = [b.chan(ph, "w2lda"), b.chan(ph, "w2ldb", sw=True)]
        s_st = [b.chan(ph, "w2sta"), b.chan(ph, "w2stb")]
        s_sl2 = [b.chan(ph, "w2sla", sw=True), b.chan(ph, "w2slb")]
        lm = [0, 0]
        cn = [0]
        jobs = []
        for l in range(2):
            jobs.append((W["ff_w1"].rearrange("l (c p) n -> l c p n", p=128)[l],
                         C["scr_w1h"], C["scr_w1l"], l * 1024, 8, 5460, 128))
            jobs.append((W["ff_wqkv"].rearrange("l (c p) n -> l c p n", p=128)[l],
                         C["scr_wqh"], C["scr_wql"], l * 1024, 8, 1536, 128))
            jobs.append((W["ff_wo"].rearrange("l (c p) n -> l c p n", p=128)[l],
                         C["scr_woh"], C["scr_wol"], l * 512, 4, 1024, 128))
        for l in range(2):
            jobs.append((W["ff_w2"].rearrange("l m n -> l m n")[l],
                         C["scr_w2h"], C["scr_w2l"], l * 2730, 22, 1024, None))
        for (hbm, sch, scl_, row0, nch, wid, chunked) in jobs:
            for cji in range(nch):
                rows = 128 if (chunked is not None or cji < 21) else 42
                r0 = cji * 128
                p_ = cn[0] % 2
                off = p_ * 5460
                ld_e = (b.se, b.ge)[p_]
                lo_e = (b.ge, b.se)[p_]
                b.wait(ld_e, b.t_ve, lm[p_])
                b.wait(ld_e, s_st[p_])
                b.wait(ld_e, s_sl2[p_])
                if chunked is not None:
                    b.inc(ld_e.dma_start(out=stage[:rows, off:off + wid], in_=hbm[cji]),
                          s_ld[p_], 16)
                else:
                    b.inc(ld_e.dma_start(out=stage[:rows, off:off + wid],
                                         in_=hbm[r0:r0 + rows, :]), s_ld[p_], 16)
                b.wait(b.ve, s_ld[p_])
                b.wait(b.ve, s_st[p_])
                b.wait(b.ve, s_sl2[p_])
                split_hilo(b, sh[:rows, off:off + wid], sl[:rows, off:off + wid],
                           stage[:rows, off:off + wid])
                lm[p_] = b.t_ve.n
                b.wait(b.sy, b.t_ve, lm[p_])
                b.wait(lo_e, b.t_ve, lm[p_])
                for (scr, buf, eng, sem) in ((sch, sh, b.sy, s_st[p_]),
                                             (scl_, sl, lo_e, s_sl2[p_])):
                    b.inc(eng.dma_start(
                        out=scr.rearrange("(u p) n -> p u n", p=128)[:rows, (row0 // 128) + cji]
                        if chunked is not None else scr[row0 + r0: row0 + r0 + rows, :],
                        in_=buf[:rows, off:off + wid]), sem, 16)
                cn[0] += 1
        b.barrier()
        b.rmchan(*s_ld, *s_st, *s_sl2)


def slot2_attn(b, C, l, src, dst):
    nc, W, PS = C["nc"], C["W"], C["PS"]
    t_len, nt = C["t_len"], C["nt"]
    nw = t_len // 512
    maskU, maskL = C["maskU"], C["maskL"]
    with ExitStack() as pa:
        def sb(name, cols, dt=F32):
            return pa.enter_context(nc.sbuf_tensor(f"{name}2A{Sem._uid[0]}", [128, cols], dt))
        rot = {}
        s_rt = b.chan(pa, "rt2")
        histage = sb("histage", 4 * 64)
        for nm in ("rc1", "rs1", "rc2", "rs2"):
            rot[nm] = sb(nm, 4 * 64)

        def load_rot(w):
            for nm in ("rc1", "rs1", "rc2", "rs2"):
                b.wait(b.se, b.t_ve)
                b.inc(b.se.dma_start(
                    out=histage[:, :],
                    in_=W[nm + "_hi"].rearrange("(tt p) c -> p tt c", p=128)[:, 4 * w:4 * w + 4]),
                    s_rt, 16)
                b.inc(b.se.dma_start(
                    out=rot[nm][:, :],
                    in_=W[nm + "_lo"].rearrange("(tt p) c -> p tt c", p=128)[:, 4 * w:4 * w + 4]),
                    s_rt, 16)
                b.wait(b.ve, s_rt)
                b.wait(b.ve, b.t_ve)
                b.v(b.ve.tensor_tensor(out=rot[nm][:, :], in0=histage[:, :],
                                       in1=rot[nm][:, :], op=ALU.add))
        qs_rep = sb("qs_rep", 64)
        ks_rep = sb("ks_rep", 64)
        s_sc = b.chan(pa, "sc2")
        b.inc(b.se.dma_start(out=qs_rep[:],
                             in_=bass.AP(W["ff_qs"].tensor, W["ff_qs"].offset + l * 64,
                                         [[0, 128], [1, 64]])), s_sc, 16)
        b.inc(b.se.dma_start(out=ks_rep[:],
                             in_=bass.AP(W["ff_ks"].tensor, W["ff_ks"].offset + l * 64,
                                         [[0, 128], [1, 64]])), s_sc, 16)
        wkvh = sb("wkvh", 2 * 8 * 512, BF16)
        wkvl = sb("wkvl", 2 * 8 * 512, BF16)
        woh = sb("woh", 4 * D, BF16)
        wol = sb("wol", 4 * D, BF16)
        s_wk = [b.chan(pa, "wk2a"), b.chan(pa, "wk2b")]
        s_wo = b.chan(pa, "wo2")
        wk_marks = [0, 0]
        wqh3 = C["scr_wqh"].rearrange("(u p) n -> p u n", p=128)
        wql3 = C["scr_wql"].rearrange("(u p) n -> p u n", p=128)
        b.inc(b.se.dma_start(out=woh.rearrange("p (c q) -> p c q", q=D),
                             in_=C["scr_woh"].rearrange("(u p) n -> p u n", p=128)[:, 4 * l:4 * l + 4]),
              s_wo, 16)
        b.inc(b.se.dma_start(out=wol.rearrange("p (c q) -> p c q", q=D),
                             in_=C["scr_wol"].rearrange("(u p) n -> p u n", p=128)[:, 4 * l:4 * l + 4]),
              s_wo, 16)
        wsems = [(s_wo, s_wo.n)]

        hwin = sb("hwin", 2 * D)
        hwin2 = sb("hwin2", 2 * D)
        hfh = sb("hfh", 8 * 512, BF16)
        hfl = sb("hfl", 8 * 512, BF16)
        qkvG = sb("qkvG", 4 * 1536)
        sqs = sb("sqs", 2048)
        ss8 = sb("ss8", 96)
        qn = sb("qn", 2048)
        qwfh = sb("qwfh", 4 * 512, BF16)
        qwfl = sb("qwfl", 4 * 512, BF16)
        kwfh = sb("kwfh", 2 * 4 * 512, BF16)
        kwfl = sb("kwfl", 2 * 4 * 512, BF16)
        vwh = sb("vwh", 2 * 4 * 512, BF16)
        vwl = sb("vwl", 2 * 4 * 512, BF16)
        s_sb2 = sb("s_sb2", 2 * 1280)
        p_f32 = sb("p_f32", 2 * 1280)
        o_t1 = sb("o_t1", 128)
        pth = sb("pth", 2 * 640, BF16)
        ptl = sb("ptl", 2 * 640, BF16)
        o_fmh = sb("o_fmh", 4 * 512, BF16)
        o_fml = sb("o_fml", 4 * 512, BF16)
        stats = sb("stats", 16)
        xr = sb("xr", D)
        s_hw = b.chan(pa, "hw2")
        s_xr = b.chan(pa, "xr2")
        s_out = b.chan(pa, "out2", sw=True)

        for w in range(nw):
            cur = w % 2
            load_rot(w)
            b.wait(b.se, b.t_te)
            b.wait(b.se, b.t_ve)
            load_shifted(b, s_hw, b.se, hwin, src, w * 512, 2)
            load_shifted(b, s_hw, b.se, hwin2, src, w * 512 + 256, 2)
            b.wait(b.te, s_hw)
            for sub, hw_ in ((0, hwin), (1, hwin2)):
                for t2 in range(2):
                    tt = sub * 2 + t2
                    for c in range(8):
                        bank = 6 + ((t2 * 8 + c) % 2)
                        b.bwait(bank)
                        tp_pe_split(b, C, bank, 0,
                                    hfh[:, c * 512 + tt * 128: c * 512 + tt * 128 + 128],
                                    hfl[:, c * 512 + tt * 128: c * 512 + tt * 128 + 128],
                                    hw_[:, t2 * D + c * 128: t2 * D + c * 128 + 128])
            h_mark_se = b.t_se.n
            h_mark_ve = b.t_ve.n
            for cc in range(3):
                wkp = (cc % 2) * 8 * 512
                b.wait(b.se, b.t_te, wk_marks[cc % 2])
                b.inc(b.se.dma_start(
                    out=wkvh[:, wkp:wkp + 8 * 512].rearrange("p (c q) -> p c q", q=512),
                    in_=wqh3[:, 8 * l:8 * l + 8, cc * 512:(cc + 1) * 512]), s_wk[cc % 2], 16)
                b.inc(b.se.dma_start(
                    out=wkvl[:, wkp:wkp + 8 * 512].rearrange("p (c q) -> p c q", q=512),
                    in_=wql3[:, 8 * l:8 * l + 8, cc * 512:(cc + 1) * 512]), s_wk[cc % 2], 16)
                for tt in range(4):
                    ps = PS[tt % 2]
                    b.wait(b.te, b.t_se, h_mark_se)
                    b.wait(b.te, b.t_ve, h_mark_ve)
                    b.wait(b.te, s_wk[cc % 2])
                    b.bwait(tt % 2)
                    for ci in range(8):
                        ii = mm3(b, ps[:, :],
                                 hfh[:, ci * 512 + tt * 128: ci * 512 + tt * 128 + 128],
                                 hfl[:, ci * 512 + tt * 128: ci * 512 + tt * 128 + 128],
                                 wkvh[:, wkp + ci * 512: wkp + (ci + 1) * 512],
                                 wkvl[:, wkp + ci * 512: wkp + (ci + 1) * 512],
                                 start=(ci == 0), stop=(ci == 7))
                    wk_marks[cc % 2] = b.t(ii)
                    b.wait(b.ve, b.t_te)
                    b.wait(b.ve, b.t_ve)
                    b.bmark(tt % 2, b.t_ve,
                            b.v(b.ve.tensor_copy(
                                qkvG[:, cc * 2048 + tt * 512: cc * 2048 + tt * 512 + 512],
                                ps[:, :])))
            for (part, srep, ctab, stab) in ((0, qs_rep, "rc1", "rs1"),
                                             (1, ks_rep, "rc2", "rs2")):
                reg = qkvG[:, part * 2048:(part + 1) * 2048]
                b.wait(b.se, b.t_ve)
                b.wait(b.se, b.t_se)
                b.s(b.se.activation(out=sqs[:], in_=reg, func=AF.Square,
                                    bias=0.0, scale=1.0))
                b.wait(b.ve, b.t_se)
                b.v(b.ve.reduce_sum(ss8[:, 0:32],
                                    sqs.rearrange("p (h c) -> p h c", c=64),
                                    axis=AX.X))
                b.wait(b.se, b.t_ve)
                b.s(b.se.activation(out=ss8[:, 32:64], in_=ss8[:, 0:32],
                                    func=AF.Sqrt, bias=0.0, scale=1.0))
                b.wait(b.ve, b.t_se)
                b.v(b.ve.tensor_scalar_max(ss8[:, 0:32], ss8[:, 32:64], 1e-12))
                b.wait(b.ve, b.t_ve)
                b.v(b.ve.reciprocal(ss8[:, 64:96], ss8[:, 0:32]))
                nrm_b = ss8[:, 64:96].rearrange("p h -> p h ()").broadcast_to([128, 32, 64])
                b.wait(b.ve, b.t_ve)
                b.wait(b.ve, b.t_te)
                b.v(b.ve.tensor_tensor(out=qn.rearrange("p (h c) -> p h c", c=64),
                                       in0=reg.rearrange("p (h c) -> p h c", c=64),
                                       in1=nrm_b, op=ALU.mult))
                qs_b = srep.rearrange("p c -> p () c").broadcast_to([128, 32, 64])
                b.wait(b.ve, b.t_ve)
                b.wait(b.ve, s_sc)
                b.v(b.ve.tensor_tensor(out=qn.rearrange("p (h c) -> p h c", c=64),
                                       in0=qn.rearrange("p (h c) -> p h c", c=64),
                                       in1=qs_b, op=ALU.mult))
                c_b = rot[ctab].rearrange("p (g c) -> p g () c", c=64).broadcast_to(
                    [128, 4, 8, 64])
                b.wait(b.ve, b.t_ve)
                b.v(b.ve.tensor_tensor(
                    out=sqs.rearrange("p (g h c) -> p g h c", h=8, c=64),
                    in0=qn.rearrange("p (g h c) -> p g h c", h=8, c=64),
                    in1=c_b, op=ALU.mult))
                for hf in range(2):
                    s_b = bass.AP(rot[stab], hf * 32,
                                  [[256, 128], [64, 4], [0, 8], [1, 32]])
                    shuf = bass.AP(qn, (1 - hf) * 32,
                                   [[2048, 128], [512, 4], [64, 8], [1, 32]])
                    dst4 = bass.AP(hwin, hf * 32,
                                   [[2048, 128], [512, 4], [64, 8], [1, 32]])
                    b.wait(b.ve, b.t_ve)
                    b.v(b.ve.tensor_tensor(out=dst4, in0=shuf, in1=s_b,
                                           op=ALU.mult))
                b.wait(b.ve, b.t_ve)
                b.v(b.ve.tensor_tensor(out=qn[:], in0=sqs[:], in1=hwin[:, :2048],
                                       op=ALU.add))
                tr_mark = b.t_ve.n
                b.wait(b.te, b.t_ve, tr_mark)
                for tt in range(4):
                    for c in range(4):
                        if part == 0:
                            dh_ = qwfh[:, c * 512 + tt * 128: c * 512 + tt * 128 + 128]
                            dl_ = qwfl[:, c * 512 + tt * 128: c * 512 + tt * 128 + 128]
                        else:
                            base_ = cur * 4 * 512
                            dh_ = kwfh[:, base_ + c * 512 + tt * 128: base_ + c * 512 + tt * 128 + 128]
                            dl_ = kwfl[:, base_ + c * 512 + tt * 128: base_ + c * 512 + tt * 128 + 128]
                        bank = 4 + (tt * 4 + c) % 2
                        b.bwait(bank)
                        tp_pe_split(b, C, bank, 0, dh_, dl_,
                                    qn[:, tt * 512 + c * 128: tt * 512 + (c + 1) * 128])
            b.wait(b.ve, b.t_te)
            split_hilo(b, vwh[:, cur * 2048: cur * 2048 + 2048],
                       vwl[:, cur * 2048: cur * 2048 + 2048],
                       qkvG[:, 4096:6144])
            vw_mark = b.t_ve.n
            qk_mark_se = b.t_se.n
            qk_mark_ve = b.t_ve.n
            pn_marks = [0, 0]

            def s2_stage1(n):
                pr, iq = divmod(n, 4)
                buf = n % 2
                wA = (512 - iq * 128) if w > 0 else 0
                wB = iq * 128 + 128
                L = wA + wB
                for hh in range(2):
                    hrow = slice(64 * hh, 64 * hh + 64)
                    bnkA, bnkB = (2, 3) if hh == 0 else (6, 7)
                    b.wait(b.te, b.t_se, qk_mark_se)
                    b.wait(b.te, b.t_ve, qk_mark_ve)
                    b.bwait(bnkA)
                    b.bwait(bnkB)
                    qap_h = qwfh[hrow, pr * 512 + iq * 128: pr * 512 + iq * 128 + 128]
                    qap_l = qwfl[hrow, pr * 512 + iq * 128: pr * 512 + iq * 128 + 128]
                    if wA:
                        prev = (1 - cur) * 2048
                        mm3(b, PS[bnkA][:, :wA],
                            qap_h, qap_l,
                            kwfh[hrow, prev + pr * 512 + iq * 128: prev + pr * 512 + 512],
                            kwfl[hrow, prev + pr * 512 + iq * 128: prev + pr * 512 + 512],
                            start=True, stop=True)
                    ii = mm3(b, PS[bnkB][:, :wB],
                             qap_h, qap_l,
                             kwfh[hrow, cur * 2048 + pr * 512: cur * 2048 + pr * 512 + wB],
                             kwfl[hrow, cur * 2048 + pr * 512: cur * 2048 + pr * 512 + wB],
                             start=True, stop=True)
                    b.t(ii)
                    sreg = s_sb2[:, buf * 1280 + hh * 640: buf * 1280 + hh * 640 + L]
                    b.wait(b.ve, b.t_te)
                    b.wait(b.ve, b.t_se)
                    b.wait(b.ve, b.t_ve)
                    if wA:
                        b.bmark(bnkA, b.t_ve,
                                b.v(b.ve.tensor_scalar_mul(sreg[:, :wA], PS[bnkA][:, :wA], 8.0)))
                    b.bmark(bnkB, b.t_ve,
                            b.v(b.ve.tensor_scalar_mul(sreg[:, wA:L], PS[bnkB][:, :wB], 8.0)))
                    b.wait(b.ve, b.t_ve)
                    if wA:
                        b.v(b.ve.tensor_tensor(out=sreg[:, 0:128], in0=sreg[:, 0:128],
                                               in1=maskL[:], op=ALU.add))
                        b.wait(b.ve, b.t_ve)
                    b.v(b.ve.tensor_tensor(out=sreg[:, L - 128:L], in0=sreg[:, L - 128:L],
                                           in1=maskU[:], op=ALU.add))
                    b.wait(b.ve, b.t_ve)
                    st = stats[:, buf * 4 + hh * 2: buf * 4 + hh * 2 + 2]
                    b.v(b.ve.reduce_max(st[:, 0:1], sreg, axis=AX.X, negate=True))
                    b.wait(b.se, b.t_ve)
                    b.wait(b.se, b.t_se)
                    b.wait(b.se, b.t_te)
                    preg = p_f32[:, buf * 1280 + hh * 640: buf * 1280 + hh * 640 + L]
                    b.s(b.se.activation(out=preg, in_=sreg, func=AF.Exp,
                                        bias=st[:, 0:1], scale=1.0,
                                        accum_out=st[:, 1:2]))
                    b.wait(b.ve, b.t_se)
                    b.v(b.ve.reciprocal(stats[:, 8 + buf * 2 + hh: 9 + buf * 2 + hh],
                                        st[:, 1:2]))
                    b.wait(b.ve, b.t_ve)
                    b.v(b.ve.tensor_scalar(out=preg, in0=preg,
                                           scalar1=stats[:, 8 + buf * 2 + hh: 9 + buf * 2 + hh],
                                           scalar2=None, op0=ALU.mult))
                pn_marks[buf] = b.t_ve.n

            def s2_stage2(n):
                pr, iq = divmod(n, 4)
                buf = n % 2
                wA = (512 - iq * 128) if w > 0 else 0
                L = wA + iq * 128 + 128
                nsc = L // 128
                for hh in range(2):
                    for sc in range(nsc):
                        bank = sc % 2
                        b.bwait(bank)
                        b.wait(b.te, b.t_ve, pn_marks[buf])
                        tp_pe_split(b, C, bank, 0,
                                    pth[:, hh * 640 + sc * 128: hh * 640 + sc * 128 + 128],
                                    ptl[:, hh * 640 + sc * 128: hh * 640 + sc * 128 + 128],
                                    p_f32[:, buf * 1280 + hh * 640 + sc * 128:
                                          buf * 1280 + hh * 640 + sc * 128 + 128])
                pt_se = b.t_se.n
                pt_ve = b.t_ve.n
                ob = PS[4 + n % 2]
                b.wait(b.te, b.t_se, pt_se)
                b.wait(b.te, b.t_ve, pt_ve)
                b.bwait(4 + n % 2)
                first = True
                for sc in range(nsc):
                    if sc < wA // 128:
                        vwin, vtile = 1 - cur, iq + sc
                    else:
                        vwin, vtile = cur, sc - wA // 128
                    vbase = vwin * 2048 + vtile * 512
                    for hh in range(2):
                        h = 2 * pr + hh
                        b.te.matmul(
                            ob[:, 64 * hh:64 * hh + 64],
                            pth[:, hh * 640 + sc * 128: hh * 640 + sc * 128 + 128],
                            vwh[:, vbase + h * 64: vbase + h * 64 + 64],
                            start=first, stop=False)
                        first = False
                        b.te.matmul(
                            ob[:, 64 * hh:64 * hh + 64],
                            pth[:, hh * 640 + sc * 128: hh * 640 + sc * 128 + 128],
                            vwl[:, vbase + h * 64: vbase + h * 64 + 64],
                            start=False, stop=False)
                        ii = b.te.matmul(
                            ob[:, 64 * hh:64 * hh + 64],
                            ptl[:, hh * 640 + sc * 128: hh * 640 + sc * 128 + 128],
                            vwh[:, vbase + h * 64: vbase + h * 64 + 64],
                            start=False,
                            stop=(sc == nsc - 1 and hh == 1))
                b.t(ii)
                b.wait(b.se, b.t_te)
                b.bmark(4 + n % 2, b.t_se,
                        b.s(b.se.activation(out=o_t1[:, 0:128], in_=ob[:, 0:128],
                                            func=AF.Identity, bias=0.0, scale=1.0)))
                ot_mark = b.t_se.n
                obank = n % 2
                b.bwait(obank)
                b.wait(b.te, b.t_se, ot_mark)
                tp_pe_split(b, C, obank, 0,
                            o_fmh[:, pr * 512 + iq * 128: pr * 512 + iq * 128 + 128],
                            o_fml[:, pr * 512 + iq * 128: pr * 512 + iq * 128 + 128],
                            o_t1[:, 0:128])

            for n in range(16):
                s2_stage1(n)
                if n > 0:
                    s2_stage2(n - 1)
            s2_stage2(15)
            for tt in range(4):
                tok = w * 4 + tt
                b.wait(b.se, b.t_ve)
                b.wait(b.se, s_out)
                b.inc(b.se.dma_start(out=xr[:], in_=rows3(src, tok)), s_xr, 16)
                b.bwait(6)
                b.bwait(7)
                b.wait(b.te, b.t_se)
                b.wait(b.te, b.t_ve)
                for sem, val in wsems:
                    b.wait(b.te, sem, val)
                for half in range(2):
                    ps = PS[6 + half]
                    for c in range(4):
                        ii = mm3(b, ps[:, :],
                                 o_fmh[:, c * 512 + tt * 128: c * 512 + tt * 128 + 128],
                                 o_fml[:, c * 512 + tt * 128: c * 512 + tt * 128 + 128],
                                 woh[:, c * D + half * 512: c * D + (half + 1) * 512],
                                 wol[:, c * D + half * 512: c * D + (half + 1) * 512],
                                 start=(c == 0), stop=(c == 3))
                    b.t(ii)
                b.wait(b.ve, s_xr)
                b.wait(b.ve, b.t_te)
                b.wait(b.ve, b.t_ve)
                b.wait(b.ve, s_out)
                for half in range(2):
                    b.bmark(6 + half, b.t_ve,
                            b.v(b.ve.tensor_tensor(
                                out=xr[:, half * 512:(half + 1) * 512],
                                in0=PS[6 + half][:, :],
                                in1=xr[:, half * 512:(half + 1) * 512], op=ALU.add)))
                b.wait(b.ge, b.t_ve)
                b.inc(b.ge.dma_start(out=rows3(dst, tok), in_=xr[:]), s_out, 16)
        b.barrier()
        b.rmchan(s_rt, s_sc, s_hw, s_xr, s_out, s_wo, *s_wk)


def slot2_ffn(b, C, l, src, dst):
    nc, W, PS = C["nc"], C["W"], C["PS"]
    t_len, nt = C["t_len"], C["nt"]
    GRP = min(1024, t_len)
    TPG = GRP // 128
    n_grp = t_len // GRP
    JCH = [(j * 128, min(128, 2730 - j * 128)) for j in range(22)]
    with ExitStack() as ph:
        def sb(name, cols, dt=F32):
            return ph.enter_context(nc.sbuf_tensor(f"{name}2F{Sem._uid[0]}", [128, cols], dt))
        lg_rep = sb("lg_rep", D)
        lb_rep = sb("lb_rep", D)
        b2_rep = sb("b2_rep", D)
        b1a = sb("b1a", 22)
        b1g = sb("b1g", 22)
        s_cst = b.chan(ph, "cst2f")
        for (dstv, name, off) in ((lg_rep, "ff_lg", l * D), (lb_rep, "ff_lb", l * D),
                                  (b2_rep, "ff_b2", l * D)):
            b.inc(b.se.dma_start(out=dstv[:],
                                 in_=bass.AP(W[name].tensor, W[name].offset + off,
                                             [[0, 128], [1, D]])), s_cst, 16)
        bt = W["ff_b1"].tensor
        boff = W["ff_b1"].offset + l * 5460
        for (dstv, o2) in ((b1a, 0), (b1g, 2730)):
            b.inc(b.se.dma_start(out=dstv[:, 0:21],
                                 in_=bass.AP(bt, boff + o2, [[1, 128], [128, 21]]),
                                 allow_slow_non_contiguous=True), s_cst, 16)
            b.inc(b.se.dma_start(out=dstv[0:42, 21:22],
                                 in_=bass.AP(bt, boff + o2 + 2688, [[1, 42], [1, 1]]),
                                 allow_slow_non_contiguous=True), s_cst, 16)

        h2 = sb("h2", TPG * D)
        h2f_h = sb("h2fh", 8 * GRP, BF16)
        h2f_l = sb("h2fl", 8 * GRP, BF16)
        w1ah = sb("w1ah", 2 * 8 * 128, BF16)
        w1al = sb("w1al", 2 * 8 * 128, BF16)
        w1gh = sb("w1gh", 2 * 8 * 128, BF16)
        w1gl = sb("w1gl", 2 * 8 * 128, BF16)
        w2h_ = sb("w2h_", 2 * 512, BF16)
        w2l_ = sb("w2l_", 2 * 512, BF16)
        gel = sb("gel", 2 * 512)
        h3c = sb("h3c", 2 * 512)
        h3h = sb("h3h", 22 * GRP, BF16)
        h3l = sb("h3l", 22 * GRP, BF16)
        xab = sb("xab", D)
        outt = sb("outt", D)
        stats = sb("stats", 24)
        s_h2 = b.chan(ph, "h2f")
        s_w1 = [b.chan(ph, "w12fa"), b.chan(ph, "w12fb")]
        s_w2 = [b.chan(ph, "w22fa"), b.chan(ph, "w22fb")]
        s_xab = b.chan(ph, "xab2f")
        s_out = b.chan(ph, "out2f", sw=True)

        w1h3 = C["scr_w1h"].rearrange("(u p) n -> p u n", p=128)
        w1l3 = C["scr_w1l"].rearrange("(u p) n -> p u n", p=128)

        for g in range(n_grp):
            t0g = g * GRP
            b.wait(b.se, b.t_ve)
            b.wait(b.se, b.t_te)
            load_shifted(b, s_h2, b.se, h2, src, t0g, TPG)
            g3 = lambda t: t.rearrange("p (g c) -> p g c", c=D)
            b.wait(b.ve, s_h2)
            b.wait(b.ve, b.t_ve)
            b.v(b.ve.reduce_sum(stats[:, 0:TPG], g3(h2[:]), axis=AX.X))
            b.wait(b.ve, b.t_ve)
            b.v(b.ve.tensor_scalar_mul(stats[:, TPG:2 * TPG], stats[:, 0:TPG],
                                       -1.0 / D))
            nb_ = stats[:, TPG:2 * TPG].rearrange("p g -> p g ()").broadcast_to(
                [128, TPG, D])
            b.wait(b.ve, b.t_ve)
            b.wait(b.ve, b.t_te)
            b.v(b.ve.tensor_tensor(out=g3(h2[:]), in0=g3(h2[:]), in1=nb_,
                                   op=ALU.add))
            for tt in range(TPG):
                b.wait(b.se, b.t_ve)
                b.wait(b.se, b.t_se)
                b.s(b.se.activation(out=xab[:], in_=h2[:, tt * D:(tt + 1) * D],
                                    func=AF.Square, bias=0.0, scale=1.0,
                                    accum_out=stats[:, 2 * TPG + tt: 2 * TPG + tt + 1]))
            b.wait(b.se, b.t_se)
            b.s(b.se.activation(out=stats[:, 0:TPG], in_=stats[:, 2 * TPG:3 * TPG],
                                func=AF.Sqrt, bias=1e-5, scale=1.0 / D))
            b.wait(b.ve, b.t_se)
            b.v(b.ve.reciprocal(stats[:, TPG:2 * TPG], stats[:, 0:TPG]))
            rb_ = stats[:, TPG:2 * TPG].rearrange("p g -> p g ()").broadcast_to(
                [128, TPG, D])
            b.wait(b.ve, b.t_ve)
            b.v(b.ve.tensor_tensor(out=g3(h2[:]), in0=g3(h2[:]), in1=rb_,
                                   op=ALU.mult))
            lgb_ = lg_rep.rearrange("p c -> p () c").broadcast_to([128, TPG, D])
            lbb_ = lb_rep.rearrange("p c -> p () c").broadcast_to([128, TPG, D])
            b.wait(b.ve, b.t_ve)
            b.wait(b.ve, s_cst)
            b.v(b.ve.tensor_tensor(out=g3(h2[:]), in0=g3(h2[:]), in1=lgb_,
                                   op=ALU.mult))
            b.wait(b.ve, b.t_ve)
            b.v(b.ve.tensor_tensor(out=g3(h2[:]), in0=g3(h2[:]), in1=lbb_,
                                   op=ALU.add))
            h2_mark = b.t_ve.n
            for tt in range(TPG):
                reg = h2[:, tt * D:(tt + 1) * D]
                for c in range(8):
                    bank = 2 + (tt * 8 + c) % 2
                    b.bwait(bank)
                    b.wait(b.te, b.t_ve, h2_mark)
                    tp_pe_split(b, C, bank, 0,
                                h2f_h[:, c * GRP + tt * 128: c * GRP + tt * 128 + 128],
                                h2f_l[:, c * GRP + tt * 128: c * GRP + tt * 128 + 128],
                                reg[:, c * 128:(c + 1) * 128])
            h2f_mark_se = b.t_se.n
            h2f_mark_ve = b.t_ve.n
            w1_marks = [0, 0]
            for jj, (j0, jw) in enumerate(JCH):
                w1p = (jj % 2) * 8 * 128
                b.wait(b.se, b.t_te, w1_marks[jj % 2])
                b.inc(b.se.dma_start(
                    out=w1ah[:, w1p:w1p + 8 * jw].rearrange("p (c q) -> p c q", q=jw),
                    in_=w1h3[:, 8 * l:8 * l + 8, j0:j0 + jw]), s_w1[jj % 2], 16)
                b.inc(b.se.dma_start(
                    out=w1al[:, w1p:w1p + 8 * jw].rearrange("p (c q) -> p c q", q=jw),
                    in_=w1l3[:, 8 * l:8 * l + 8, j0:j0 + jw]), s_w1[jj % 2], 16)
                b.inc(b.se.dma_start(
                    out=w1gh[:, w1p:w1p + 8 * jw].rearrange("p (c q) -> p c q", q=jw),
                    in_=w1h3[:, 8 * l:8 * l + 8, 2730 + j0:2730 + j0 + jw]), s_w1[jj % 2], 16)
                b.inc(b.se.dma_start(
                    out=w1gl[:, w1p:w1p + 8 * jw].rearrange("p (c q) -> p c q", q=jw),
                    in_=w1l3[:, 8 * l:8 * l + 8, 2730 + j0:2730 + j0 + jw]), s_w1[jj % 2], 16)
                pa_, pg_ = (0, 1) if jj % 2 == 0 else (2, 3)
                par2 = (jj % 2) * 512
                for tg in range(GRP // 512):
                    b.wait(b.te, s_w1[jj % 2])
                    b.wait(b.te, b.t_se, h2f_mark_se)
                    b.wait(b.te, b.t_ve, h2f_mark_ve)
                    b.bwait(pa_)
                    b.bwait(pg_)
                    for ci in range(8):
                        mm3(b, PS[pa_][:jw, :],
                            w1ah[:, w1p + ci * jw: w1p + (ci + 1) * jw],
                            w1al[:, w1p + ci * jw: w1p + (ci + 1) * jw],
                            h2f_h[:, ci * GRP + tg * 512: ci * GRP + (tg + 1) * 512],
                            h2f_l[:, ci * GRP + tg * 512: ci * GRP + (tg + 1) * 512],
                            start=(ci == 0), stop=(ci == 7))
                        ii = mm3(b, PS[pg_][:jw, :],
                                 w1gh[:, w1p + ci * jw: w1p + (ci + 1) * jw],
                                 w1gl[:, w1p + ci * jw: w1p + (ci + 1) * jw],
                                 h2f_h[:, ci * GRP + tg * 512: ci * GRP + (tg + 1) * 512],
                                 h2f_l[:, ci * GRP + tg * 512: ci * GRP + (tg + 1) * 512],
                                 start=(ci == 0), stop=(ci == 7))
                    w1_marks[jj % 2] = b.t(ii)
                    b.wait(b.se, b.t_te)
                    b.wait(b.se, s_cst)
                    b.wait(b.se, b.t_ve)
                    b.bmark(pg_, b.t_se,
                            b.s(b.se.activation(out=gel[:jw, par2:par2 + 512],
                                                in_=PS[pg_][:jw, :],
                                                func=AF.Gelu, bias=b1g[:jw, jj:jj + 1],
                                                scale=1.0)))
                    b.wait(b.ve, b.t_se)
                    b.wait(b.ve, b.t_te)
                    b.bmark(pa_, b.t_ve,
                            b.v(b.ve.scalar_tensor_tensor(
                                out=h3c[:jw, par2:par2 + 512], in0=PS[pa_][:jw, :],
                                scalar=b1a[:jw, jj:jj + 1],
                                in1=gel[:jw, par2:par2 + 512], op0=ALU.add, op1=ALU.mult)))
                    b.wait(b.ve, b.t_ve)
                    b.wait(b.ve, b.t_te)
                    b.v(b.ve.tensor_copy(h3h[:jw, jj * GRP + tg * 512: jj * GRP + (tg + 1) * 512],
                                         h3c[:jw, par2:par2 + 512]))
                    b.wait(b.ve, b.t_ve)
                    b.v(b.ve.tensor_tensor(out=h3l[:jw, jj * GRP + tg * 512: jj * GRP + (tg + 1) * 512],
                                           in0=h3c[:jw, par2:par2 + 512],
                                           in1=h3h[:jw, jj * GRP + tg * 512: jj * GRP + (tg + 1) * 512],
                                           op=ALU.subtract))
            h3_mark = b.t_ve.n
            for half in range(2):
                w2_marks = [0, 0]
                for jj, (j0, jw) in enumerate(JCH):
                    w2p = (jj % 2) * 512
                    b.wait(b.se, b.t_te, w2_marks[jj % 2])
                    b.inc(b.se.dma_start(
                        out=w2h_[:jw, w2p:w2p + 512],
                        in_=C["scr_w2h"][l * 2730 + j0: l * 2730 + j0 + jw,
                                         half * 512:(half + 1) * 512]), s_w2[jj % 2], 16)
                    b.inc(b.se.dma_start(
                        out=w2l_[:jw, w2p:w2p + 512],
                        in_=C["scr_w2l"][l * 2730 + j0: l * 2730 + j0 + jw,
                                         half * 512:(half + 1) * 512]), s_w2[jj % 2], 16)
                    b.wait(b.te, s_w2[jj % 2])
                    b.wait(b.te, b.t_ve, h3_mark)
                    if jj == 0:
                        for tt2 in range(TPG):
                            b.bwait(tt2)
                    for tt in range(TPG):
                        ii = mm3(b, PS[tt][:, :],
                                 h3h[:jw, jj * GRP + tt * 128: jj * GRP + tt * 128 + 128],
                                 h3l[:jw, jj * GRP + tt * 128: jj * GRP + tt * 128 + 128],
                                 w2h_[:jw, w2p:w2p + 512],
                                 w2l_[:jw, w2p:w2p + 512],
                                 start=(jj == 0), stop=(jj == 21))
                    w2_marks[jj % 2] = b.t(ii)
                for tt in range(TPG):
                    tok = t0g // 128 + tt
                    b.wait(b.se, b.t_ve)
                    b.inc(b.se.dma_start(out=xab[:, half * 512:(half + 1) * 512],
                                         in_=bass.AP(src.tensor,
                                                     src.offset + (tok * 128) * D + half * 512,
                                                     [[D, 128], [1, 512]])), s_xab, 16)
                    b.wait(b.ve, s_xab)
                    b.wait(b.ve, b.t_te)
                    b.wait(b.ve, b.t_ve)
                    b.wait(b.ve, s_cst)
                    b.v(b.ve.tensor_tensor(out=xab[:, half * 512:(half + 1) * 512],
                                           in0=xab[:, half * 512:(half + 1) * 512],
                                           in1=b2_rep[:, half * 512:(half + 1) * 512],
                                           op=ALU.add))
                    b.wait(b.ve, b.t_ve)
                    b.wait(b.ve, s_out)
                    b.bmark(tt, b.t_ve,
                            b.v(b.ve.tensor_tensor(
                                out=outt[:, half * 512:(half + 1) * 512],
                                in0=PS[tt][:, :],
                                in1=xab[:, half * 512:(half + 1) * 512], op=ALU.add)))
                    b.wait(b.ge, b.t_ve)
                    b.inc(b.ge.dma_start(
                        out=bass.AP(dst.tensor, dst.offset + (tok * 128) * D + half * 512,
                                    [[D, 128], [1, 512]]),
                        in_=outt[:, half * 512:(half + 1) * 512]), s_out, 16)
        b.barrier()
        b.rmchan(s_cst, s_h2, s_xab, s_out, *s_w1, *s_w2)


def slot2(b, C, src, dst):
    nc, W = C["nc"], C["W"]
    t_len, nt = C["t_len"], C["nt"]
    if not C.get("w12_done"):
        slot2_wprep(b, C)
        C["w12_done"] = True
    xa, y = C["scr2_xa"], C["scr2_y"]
    cur = src
    for l in range(2):
        slot2_attn(b, C, l, cur, xa)
        slot2_ffn(b, C, l, xa, y)
        cur = y
    with ExitStack() as ph:
        def sb(name, cols, dt=F32):
            return ph.enter_context(nc.sbuf_tensor(f"{name}2L{Sem._uid[0]}", [128, cols], dt))
        x0 = sb("x0", 2 * 4 * D)
        y2 = sb("y2", 2 * 4 * D)
        sq = sb("sq", 4 * D)
        outt = sb("outt", 4 * D)
        stats = sb("stats", 16)
        s_ld = [b.chan(ph, "ld2La"), b.chan(ph, "ld2Lb")]
        s_out = b.chan(ph, "out2L", sw=True)
        lm4 = [0, 0]
        for g4 in range(nt // 4):
            p_ = g4 % 2
            off = p_ * 4 * D
            b.wait(b.se, b.t_ve, lm4[p_])
            b.inc(b.se.dma_start(
                out=x0[:, off:off + 4 * D].rearrange("p (tt d) -> p tt d", d=D),
                in_=src.rearrange("(tt p) d -> p tt d", p=128)[:, 4 * g4:4 * g4 + 4]),
                s_ld[p_], 16)
            b.inc(b.se.dma_start(
                out=y2[:, off:off + 4 * D].rearrange("p (tt d) -> p tt d", d=D),
                in_=y.rearrange("(tt p) d -> p tt d", p=128)[:, 4 * g4:4 * g4 + 4]),
                s_ld[p_], 16)
            b.wait(b.ve, s_ld[p_])
            b.wait(b.ve, b.t_ve)
            lm4[p_] = b.v(b.ve.tensor_tensor(out=y2[:, off:off + 4 * D],
                                             in0=x0[:, off:off + 4 * D],
                                             in1=y2[:, off:off + 4 * D], op=ALU.add))
            b.wait(b.ve, s_out)
            layernorm_quad(b, outt[:], y2[:, off:off + 4 * D], x0[:, off:off + 4 * D],
                           sq[:], stats[:, 0:12])
            lm4[p_] = b.t_ve.n
            b.wait(b.ge, b.t_ve)
            b.inc(b.ge.dma_start(
                out=dst.rearrange("(tt p) d -> p tt d", p=128)[:, 4 * g4:4 * g4 + 4],
                in_=outt.rearrange("p (tt d) -> p tt d", d=D)), s_out, 16)
        b.barrier()
        b.rmchan(s_out, *s_ld)



def slot3(b, C, src, dst):
    nc, W, PS = C["nc"], C["W"], C["PS"]
    t_len, nt = C["t_len"], C["nt"]
    ng = nt // 4
    scr_k = C["scr_vt"]
    scr_v3 = C["scr_v3"]

    with ExitStack() as pa:
        def sba(name, cols, dt=F32):
            return pa.enter_context(nc.sbuf_tensor(f"{name}3A{Sem._uid[0]}", [128, cols], dt))
        wf = sba("wf", 2 * D)
        wkh = sba("wkh", 8 * D, BF16); wkl = sba("wkl", 8 * D, BF16)
        wvh = sba("wvh", 8 * D, BF16); wvl = sba("wvl", 8 * D, BF16)
        wsems = list(load_w_hilo(b, pa, wf, wkh, wkl, W["l1a_wk"], 8, D))
        wsems += list(load_w_hilo(b, pa, wf, wvh, wvl, W["l1a_wv"], 8, D))
        bk_rep = sba("bk_rep", D)
        bv_rep = sba("bv_rep", D)
        s_bias = b.chan(pa, "bias3")
        bias_rep_load(b, s_bias, bk_rep[:], W["l1a_bk"], D)
        bias_rep_load(b, s_bias, bv_rep[:], W["l1a_bv"], D)

        xg = sba("xg", 4 * D)
        xfh = sba("xfh", 8 * 512, BF16)
        xfl = sba("xfl", 8 * 512, BF16)
        kt1 = sba("kt1", 512)
        kt2 = sba("kt2", 512)
        kte = sba("kte", 512)
        ktileh = sba("ktileh", D, BF16)
        ktilel = sba("ktilel", D, BF16)
        vt1 = sba("vt1", 512)
        vtileh = sba("vtileh", 8 * 130, BF16)
        vtilel = sba("vtilel", 8 * 130, BF16)
        s_xg = b.chan(pa, "xg3")
        s_spk = b.chan(pa, "spk3", sw=True)
        s_spv = b.chan(pa, "spv3", sw=True)

        vth4 = vtileh.rearrange("p (pr hh c) -> p pr hh c", hh=2, c=65)
        vtl4 = vtilel.rearrange("p (pr hh c) -> p pr hh c", hh=2, c=65)
        b.v(b.ve.memset(vth4[:, :, :, 64:65], 1.0))
        b.v(b.ve.memset(vtl4[:, :, :, 64:65], 0.0))

        for g in range(ng):
            b.wait(b.se, b.t_te)
            b.inc(b.se.dma_start(
                out=xg.rearrange("p (tt d) -> p tt d", d=D),
                in_=src.rearrange("(tt p) d -> p tt d", p=128)[:, 4 * g:4 * g + 4]),
                s_xg, 16)
            b.wait(b.te, s_xg)
            for tt in range(4):
                for c in range(8):
                    bank = 6 + ((tt * 8 + c) % 2)
                    b.bwait(bank)
                    tp_pe_split(b, C, bank, 0,
                                xfh[:, c * 512 + tt * 128: c * 512 + tt * 128 + 128],
                                xfl[:, c * 512 + tt * 128: c * 512 + tt * 128 + 128],
                                xg[:, tt * D + c * 128: tt * D + c * 128 + 128])
            b.wait(b.te, b.t_se)
            b.wait(b.te, b.t_ve)
            for tt in range(4):
                tok = 4 * g + tt
                for m in "kv":
                    for half in range(2):
                        ps = PS[half]
                        for sem, val in wsems:
                            b.wait(b.te, sem, val)
                        b.bwait(half)
                        wmh = wkh if m == "k" else wvh
                        wml = wkl if m == "k" else wvl
                        for ci in range(8):
                            ii = mm3(b, ps[:, :],
                                     xfh[:, ci * 512 + tt * 128: ci * 512 + tt * 128 + 128],
                                     xfl[:, ci * 512 + tt * 128: ci * 512 + tt * 128 + 128],
                                     wmh[:, ci * D + half * 512: ci * D + (half + 1) * 512],
                                     wml[:, ci * D + half * 512: ci * D + (half + 1) * 512],
                                     start=(ci == 0), stop=(ci == 7))
                        b.t(ii)
                    if m == "k":
                        b.wait(b.ve, b.t_te)
                        b.wait(b.ve, s_bias)
                        b.wait(b.ve, s_spk)
                        for half in range(2):
                            ps = PS[half]
                            b.wait(b.ve, b.t_ve)
                            b.bmark(half, b.t_ve,
                                    b.v(b.ve.tensor_tensor(out=kt1[:], in0=ps[:, :],
                                                           in1=bk_rep[:, half * 512:(half + 1) * 512],
                                                           op=ALU.add)))
                            b.wait(b.ve, b.t_ve)
                            b.v(b.ve.tensor_scalar_min(kt2[:], kt1[:], 0.0))
                            b.wait(b.se, b.t_ve)
                            b.wait(b.se, b.t_se)
                            b.s(b.se.activation(out=kte[:], in_=kt2[:], func=AF.Exp,
                                                bias=0.0, scale=1.0))
                            b.wait(b.ve, b.t_se)
                            b.v(b.ve.scalar_tensor_tensor(
                                out=kt2[:], in0=kt1[:], scalar=0.0,
                                in1=kte[:], op0=ALU.max, op1=ALU.add))
                            split_hilo(b, ktileh[:, half * 512:(half + 1) * 512],
                                       ktilel[:, half * 512:(half + 1) * 512], kt2[:])
                        b.wait(b.ge, b.t_ve)
                        b.inc(b.ge.dma_start(out=plane(scr_k, 2 * tok),
                                             in_=ktileh[:]), s_spk, 16)
                        b.inc(b.ge.dma_start(out=plane(scr_k, 2 * tok + 1),
                                             in_=ktilel[:]), s_spk, 16)
                    else:
                        b.wait(b.ve, b.t_te)
                        b.wait(b.ve, s_bias)
                        b.wait(b.ve, s_spv)
                        for half in range(2):
                            ps = PS[half]
                            b.wait(b.ve, b.t_ve)
                            b.bmark(half, b.t_ve,
                                    b.v(b.ve.tensor_tensor(out=vt1[:], in0=ps[:, :],
                                                           in1=bv_rep[:, half * 512:(half + 1) * 512],
                                                           op=ALU.add)))
                            dsth = vth4[:, 4 * half:4 * half + 4, :, 0:64]
                            dstl = vtl4[:, 4 * half:4 * half + 4, :, 0:64]
                            b.wait(b.ve, b.t_ve)
                            b.v(b.ve.tensor_copy(dsth, vt1.rearrange(
                                "p (pr hh c) -> p pr hh c", hh=2, c=64)))
                            b.wait(b.ve, b.t_ve)
                            b.v(b.ve.tensor_tensor(
                                out=dstl, in0=vt1.rearrange(
                                    "p (pr hh c) -> p pr hh c", hh=2, c=64),
                                in1=dsth, op=ALU.subtract))
                        b.wait(b.ge, b.t_ve)
                        b.inc(b.ge.dma_start(out=plane(scr_v3, 2 * tok),
                                             in_=vtileh[:]), s_spv, 16)
                        b.inc(b.ge.dma_start(out=plane(scr_v3, 2 * tok + 1),
                                             in_=vtilel[:]), s_spv, 16)
        b.barrier()
        b.rmchan(s_bias, s_xg, s_spk, s_spv)

    with ExitStack() as pb:
        def sb(name, cols, dt=F32):
            return pb.enter_context(nc.sbuf_tensor(f"{name}3B{Sem._uid[0]}", [128, cols], dt))
        kv2h = sb("kv2h", 8 * 128, BF16)
        kv2l = sb("kv2l", 8 * 128, BF16)
        ksumh = sb("ksumh", 8 * 16, BF16)
        ksuml = sb("ksuml", 8 * 16, BF16)
        b.v(b.ve.memset(kv2h[:], 0.0))
        b.v(b.ve.memset(kv2l[:], 0.0))
        b.v(b.ve.memset(ksumh[:], 0.0))
        b.v(b.ve.memset(ksuml[:], 0.0))
        with ExitStack() as p1:
            def sb1(name, cols, dt=F32):
                return p1.enter_context(nc.sbuf_tensor(f"{name}31{Sem._uid[0]}", [128, cols], dt))
            ktbh = sb1("ktbh", 2 * D, BF16)
            ktbl = sb1("ktbl", 2 * D, BF16)
            vtbh = sb1("vtbh", 2 * 1040, BF16)
            vtbl = sb1("vtbl", 2 * 1040, BF16)
            s_kt = [b.chan(p1, "kt31a"), b.chan(p1, "kt31b")]
            kt_marks = [0, 0]
            for tok in range(nt):
                par = tok % 2
                b.wait(b.se, b.t_te, kt_marks[par])
                b.inc(b.se.dma_start(out=ktbh[:, par * D:(par + 1) * D],
                                     in_=plane(scr_k, 2 * tok)), s_kt[par], 16)
                b.inc(b.se.dma_start(out=ktbl[:, par * D:(par + 1) * D],
                                     in_=plane(scr_k, 2 * tok + 1)), s_kt[par], 16)
                b.inc(b.se.dma_start(out=vtbh[:, par * 1040:(par + 1) * 1040],
                                     in_=plane(scr_v3, 2 * tok)), s_kt[par], 16)
                b.inc(b.se.dma_start(out=vtbl[:, par * 1040:(par + 1) * 1040],
                                     in_=plane(scr_v3, 2 * tok + 1)), s_kt[par], 16)
                b.wait(b.te, s_kt[par])
                if tok == 0:
                    for p2 in range(8):
                        b.bwait(p2)
                for p in range(8):
                    ps = PS[p]
                    off = 0
                    ii = mm3(b, ps[:, off:off + 130],
                             ktbh[:, par * D + 128 * p: par * D + 128 * p + 128],
                             ktbl[:, par * D + 128 * p: par * D + 128 * p + 128],
                             vtbh[:, par * 1040 + 130 * p: par * 1040 + 130 * p + 130],
                             vtbl[:, par * 1040 + 130 * p: par * 1040 + 130 * p + 130],
                             start=(tok == 0), stop=(tok == nt - 1))
                kt_marks[par] = b.t(ii)
            for p in range(8):
                ps = PS[p]
                off = 0
                for (rows, csrc, cdst) in (
                        (slice(0, 64), slice(off, off + 64), slice(128 * p, 128 * p + 64)),
                        (slice(64, 128), slice(off + 65, off + 129),
                         slice(128 * p + 64, 128 * p + 128))):
                    b.wait(b.se, b.t_te)
                    b.wait(b.se, b.t_ve)
                    b.bmark(p, b.t_se,
                            b.s(b.se.activation(out=kv2h[rows, cdst], in_=ps[rows, csrc],
                                                func=AF.Identity, bias=0.0, scale=1.0)))
                    b.wait(b.ve, b.t_se)
                    b.bmark(p, b.t_ve,
                            b.v(b.ve.scalar_tensor_tensor(
                                out=kv2l[rows, cdst], in0=ps[rows, csrc], scalar=0.0,
                                in1=kv2h[rows, cdst], op0=ALU.add, op1=ALU.subtract)))
                for (rows, csrc, jj) in ((slice(0, 64), off + 64, 0),
                                         (slice(64, 128), off + 129, 1)):
                    cdst = slice(16 * p + 2 * p + jj, 16 * p + 2 * p + jj + 1)
                    b.wait(b.se, b.t_te)
                    b.wait(b.se, b.t_ve)
                    b.bmark(p, b.t_se,
                            b.s(b.se.activation(out=ksumh[rows, cdst],
                                                in_=ps[rows, csrc:csrc + 1],
                                                func=AF.Identity, bias=0.0, scale=1.0)))
                    b.wait(b.ve, b.t_se)
                    b.bmark(p, b.t_ve,
                            b.v(b.ve.scalar_tensor_tensor(
                                out=ksuml[rows, cdst], in0=ps[rows, csrc:csrc + 1],
                                scalar=0.0, in1=ksumh[rows, cdst],
                                op0=ALU.add, op1=ALU.subtract)))
            b.barrier()
            b.rmchan(*s_kt)

        wf = sb("wf", 2 * D)
        wqh = sb("wqh", 8 * D, BF16); wql = sb("wql", 8 * D, BF16)
        woh = sb("woh", 8 * D, BF16); wol = sb("wol", 8 * D, BF16)
        wsems = list(load_w_hilo(b, pb, wf, wqh, wql, W["l1a_wq"], 8, D))
        wsems += list(load_w_hilo(b, pb, wf, woh, wol, W["l1a_wo"], 8, D))
        bq8 = sb("bq8", 8)
        bo_rep = sb("bo_rep", D)
        s_bias = b.chan(pb, "bias3b")
        b.inc(b.se.dma_start(
            out=bq8[:], in_=W["l1a_bq"].rearrange("(cc p) -> p cc", p=128),
            allow_slow_non_contiguous=True), s_bias, 16)
        bias_rep_load(b, s_bias, bo_rep[:], W["l1a_bo"], D)

        xg = sb("xg", 4 * D)
        xfh = sb("xfh", 8 * 512, BF16)
        xfl = sb("xfl", 8 * 512, BF16)
        fmq = xg
        t_a = sb("t_a", 512)
        t_b = sb("t_b", 512)
        fmqh = sb("fmqh", 8 * 512, BF16)
        fmql = sb("fmql", 8 * 512, BF16)
        z_sb = sb("z_sb", 512)
        z_b = sb("z_b", 512)
        qzc = sb("qzc", 512)
        qzh = sb("qzh", 8 * 512, BF16)
        qzl = sb("qzl", 8 * 512, BF16)
        sch = sb("sch", 8 * 512, BF16)
        scl = sb("scl", 8 * 512, BF16)
        xr = sb("xr", D)
        xpb = sb("xpb", D)
        resid = sb("resid", D)
        outt = sb("outt", D)
        stats = sb("stats", 8)
        s_xg = b.chan(pb, "xg3b")
        s_zb = b.chan(pb, "zb3")
        s_zw = b.chan(pb, "zw3", sw=True)
        s_xr = b.chan(pb, "xr3")
        s_out = b.chan(pb, "out3", sw=True)

        for g in range(ng):
            b.wait(b.se, b.t_te)
            b.wait(b.se, b.t_ve)
            b.inc(b.se.dma_start(
                out=xg.rearrange("p (tt d) -> p tt d", d=D),
                in_=src.rearrange("(tt p) d -> p tt d", p=128)[:, 4 * g:4 * g + 4]),
                s_xg, 16)
            b.wait(b.te, s_xg)
            for tt in range(4):
                for c in range(8):
                    bank = 6 + ((tt * 8 + c) % 2)
                    b.bwait(bank)
                    tp_pe_split(b, C, bank, 0,
                                xfh[:, c * 512 + tt * 128: c * 512 + tt * 128 + 128],
                                xfl[:, c * 512 + tt * 128: c * 512 + tt * 128 + 128],
                                xg[:, tt * D + c * 128: tt * D + c * 128 + 128])
            b.wait(b.te, b.t_se)
            b.wait(b.te, b.t_ve)
            for cc in range(8):
                ps = PS[cc % 2]
                for sem, val in wsems:
                    b.wait(b.te, sem, val)
                b.bwait(cc % 2)
                for ci in range(8):
                    ii = mm3(b, ps[:, :],
                             wqh[:, ci * D + cc * 128: ci * D + cc * 128 + 128],
                             wql[:, ci * D + cc * 128: ci * D + cc * 128 + 128],
                             xfh[:, ci * 512:(ci + 1) * 512],
                             xfl[:, ci * 512:(ci + 1) * 512],
                             start=(ci == 0), stop=(ci == 7))
                b.t(ii)
                b.wait(b.ve, b.t_te)
                b.wait(b.ve, s_bias)
                b.wait(b.ve, b.t_ve)
                b.v(b.ve.tensor_scalar(out=t_a[:], in0=ps[:, :],
                                       scalar1=bq8[:, cc:cc + 1], scalar2=0.0,
                                       op0=ALU.add, op1=ALU.min))
                b.wait(b.se, b.t_ve)
                b.wait(b.se, b.t_se)
                b.s(b.se.activation(out=t_b[:], in_=t_a[:], func=AF.Exp,
                                    bias=0.0, scale=1.0))
                b.wait(b.ve, b.t_se)
                b.bmark(cc % 2, b.t_ve,
                        b.v(b.ve.tensor_scalar(out=t_a[:], in0=ps[:, :],
                                               scalar1=bq8[:, cc:cc + 1], scalar2=0.0,
                                               op0=ALU.add, op1=ALU.max)))
                b.wait(b.ve, b.t_ve)
                b.v(b.ve.tensor_tensor(out=fmq[:, cc * 512:(cc + 1) * 512],
                                       in0=t_a[:], in1=t_b[:], op=ALU.add))
            split_hilo(b, fmqh[:], fmql[:], fmq[:])
            fmq_mark = b.t_ve.n
            b.wait(b.te, b.t_ve, fmq_mark)
            b.bwait(2)
            for c in range(8):
                ii = mm3(b, PS[2][0:16, :], ksumh[:, 16 * c:16 * c + 16],
                         ksuml[:, 16 * c:16 * c + 16],
                         fmqh[:, c * 512:(c + 1) * 512],
                         fmql[:, c * 512:(c + 1) * 512],
                         start=(c == 0), stop=(c == 7))
            b.t(ii)
            b.wait(b.ve, b.t_te)
            b.wait(b.ve, s_zb)
            b.bmark(2, b.t_ve,
                    b.v(b.ve.tensor_scalar_add(z_sb[0:16, :], PS[2][0:16, :], 1e-6)))
            b.wait(b.ve, b.t_ve)
            b.wait(b.ve, s_zw)
            b.v(b.ve.reciprocal(qzc[0:16, :], z_sb[0:16, :]))
            b.wait(b.ge, b.t_ve)
            b.wait(b.ge, s_zb)
            b.inc(b.ge.dma_start(out=C["scr_z"][:], in_=qzc[0:16, :]), s_zw, 16)
            zt = C["scr_z"].tensor
            for c in range(8):
                b.wait(b.se, s_zw)
                b.wait(b.se, b.t_ve)
                b.inc(b.se.dma_start(
                    out=z_b[0:64, :],
                    in_=bass.AP(zt, 2 * c * 512, [[0, 64], [1, 512]])), s_zb, 16)
                b.inc(b.se.dma_start(
                    out=z_b[64:128, :],
                    in_=bass.AP(zt, (2 * c + 1) * 512, [[0, 64], [1, 512]])), s_zb, 16)
                b.wait(b.ve, s_zb)
                b.wait(b.ve, s_zw)
                b.wait(b.ve, b.t_ve)
                b.v(b.ve.tensor_tensor(out=qzc[:], in0=fmq[:, c * 512:(c + 1) * 512],
                                       in1=z_b[:], op=ALU.mult))
                b.wait(b.ve, b.t_ve)
                b.wait(b.ve, b.t_te)
                b.v(b.ve.tensor_copy(qzh[:, c * 512:(c + 1) * 512], qzc[:]))
                b.wait(b.ve, b.t_ve)
                b.v(b.ve.tensor_tensor(out=qzl[:, c * 512:(c + 1) * 512],
                                       in0=qzc[:],
                                       in1=qzh[:, c * 512:(c + 1) * 512],
                                       op=ALU.subtract))
            qz_mark = b.t_ve.n
            for p in range(8):
                ps = PS[3 + p % 2]
                b.wait(b.te, b.t_ve, qz_mark)
                b.bwait(3 + p % 2)
                ii = mm3(b, ps[:, :], kv2h[:, 128 * p:128 * p + 128],
                         kv2l[:, 128 * p:128 * p + 128],
                         qzh[:, p * 512:(p + 1) * 512],
                         qzl[:, p * 512:(p + 1) * 512],
                         start=True, stop=True)
                b.t(ii)
                b.wait(b.se, b.t_te)
                b.wait(b.se, b.t_se)
                b.bmark(3 + p % 2, b.t_se,
                        b.s(b.se.activation(out=sch[:, p * 512:(p + 1) * 512],
                                            in_=ps[:, :], func=AF.Identity,
                                            bias=0.0, scale=1.0)))
                b.wait(b.ve, b.t_se)
                b.wait(b.ve, b.t_te)
                b.bmark(3 + p % 2, b.t_ve,
                        b.v(b.ve.scalar_tensor_tensor(
                            out=scl[:, p * 512:(p + 1) * 512], in0=ps[:, :], scalar=0.0,
                            in1=sch[:, p * 512:(p + 1) * 512],
                            op0=ALU.add, op1=ALU.subtract)))
            sc_mark_ve = b.t_ve.n
            sc_mark_se = b.t_se.n
            for tt in range(4):
                tok = 4 * g + tt
                b.wait(b.se, b.t_ve)
                b.inc(b.se.dma_start(out=xr[:], in_=rows3(src, tok)), s_xr, 16)
                b.wait(b.te, b.t_ve, sc_mark_ve)
                b.wait(b.te, b.t_se, sc_mark_se)
                b.bwait(6)
                b.bwait(7)
                for half in range(2):
                    ps = PS[6 + half]
                    for c in range(8):
                        ii = mm3(b, ps[:, :],
                                 sch[:, c * 512 + tt * 128: c * 512 + tt * 128 + 128],
                                 scl[:, c * 512 + tt * 128: c * 512 + tt * 128 + 128],
                                 woh[:, c * D + half * 512: c * D + (half + 1) * 512],
                                 wol[:, c * D + half * 512: c * D + (half + 1) * 512],
                                 start=(c == 0), stop=(c == 7))
                    b.t(ii)
                b.wait(b.ve, s_xr)
                b.wait(b.ve, s_bias)
                b.wait(b.ve, b.t_ve)
                b.v(b.ve.tensor_tensor(out=xpb[:], in0=xr[:], in1=bo_rep[:],
                                       op=ALU.add))
                b.wait(b.ve, b.t_te)
                b.wait(b.ve, b.t_ve)
                for half in range(2):
                    b.bmark(6 + half, b.t_ve,
                            b.v(b.ve.tensor_tensor(
                                out=resid[:, half * 512:(half + 1) * 512],
                                in0=PS[6 + half][:, :],
                                in1=xpb[:, half * 512:(half + 1) * 512], op=ALU.add)))
                b.wait(b.ve, s_out)
                layernorm_tile(b, outt[:], resid[:], xpb[:], stats[:, 0:5])
                b.wait(b.ge, b.t_ve)
                b.inc(b.ge.dma_start(out=rows3(dst, tok), in_=outt[:]), s_out, 16)
        b.barrier()
        b.rmchan(s_bias, s_xg, s_zb, s_zw, s_xr, s_out)



def _quant_scr(sb, width):
    return dict(xn=sb("q_xn", width), absx=sb("q_absx", width),
                acc=sb("q_acc", width), tmpb=sb("q_tmpb", width),
                bmax=sb("q_bmax", width // 16), scale=sb("q_scale", width // 16),
                hsc=sb("q_hsc", width // 16))


def slot4_wprep(b, C):
    nc, W = C["nc"], C["W"]
    with ExitStack() as ph:
        def sb(name, cols, dt=F32):
            return ph.enter_context(nc.sbuf_tensor(f"{name}P{Sem._uid[0]}", [128, cols], dt))
        wchunk = sb("wchunk", 2 * 2816)
        wq = sb("wq", 2816)
        tpsh = sb("tpsh", 22 * 128, BF16)
        tpsl = sb("tpsl", 22 * 128, BF16)
        scr = _quant_scr(sb, 2816)
        s_ld = [b.chan(ph, "wp_lda"), b.chan(ph, "wp_ldb")]
        s_st = b.chan(ph, "wp_st", sw=True)
        ld_marks = [0, 0]

        jobs = [("fp4_wg", C["scr_wgT"], 22, 1024, 8),
                ("fp4_wu", C["scr_wuT"], 22, 1024, 8),
                ("fp4_wd", C["scr_wdT"], 8, 2816, 22)]
        jn = 0
        for wname, scrT, nrow, wid, nkc in jobs:
            hbm = W[wname]
            for j in range(nrow):
                par = (jn % 2) * 2816
                b.wait(b.se, b.t_ve, ld_marks[jn % 2])
                b.inc(b.se.dma_start(
                    out=wchunk[:, par:par + wid],
                    in_=hbm.rearrange("(j p) k -> p j k", p=128)[:, j]),
                    s_ld[jn % 2], 16)
                b.wait(b.ve, s_ld[jn % 2])
                b.wait(b.ve, b.t_te)
                quant_tile(b, C, wq[:, :wid], wchunk[:, par:par + wid], scr, wid)
                ld_marks[jn % 2] = b.t_ve.n
                wq_mark = b.t_ve.n
                b.wait(b.se, s_st)
                b.wait(b.ve, s_st)
                for c in range(nkc):
                    bank = c % 4
                    b.bwait(bank)
                    b.wait(b.te, b.t_ve, wq_mark)
                    tp_pe_split(b, C, bank, 0,
                                tpsh[:, c * 128:(c + 1) * 128],
                                tpsl[:, c * 128:(c + 1) * 128],
                                wq[:, c * 128:(c + 1) * 128])
                b.wait(b.ge, b.t_se)
                b.wait(b.ge, b.t_ve)
                v3 = scrT.rearrange("(u p) n -> p u n", p=128)
                b.inc(b.ge.dma_start(
                    out=v3[:, 0:nkc, j * 128:(j + 1) * 128],
                    in_=tpsh[:, :nkc * 128].rearrange("p (c q) -> p c q", q=128)),
                    s_st, 16)
                b.inc(b.ge.dma_start(
                    out=v3[:, nkc:2 * nkc, j * 128:(j + 1) * 128],
                    in_=tpsl[:, :nkc * 128].rearrange("p (c q) -> p c q", q=128)),
                    s_st, 16)
                jn += 1
        b.barrier()
        b.rmchan(s_st, *s_ld)


def slot4(b, C, src, dst):
    nc, W, PS = C["nc"], C["W"], C["PS"]
    t_len, nt = C["t_len"], C["nt"]
    if not C.get("wq_done"):
        slot4_wprep(b, C)
        C["wq_done"] = True

    GRP = min(512, t_len)
    TPG = GRP // 128
    n_grp = t_len // GRP
    NSPANS = [512] * 5 + [256]

    with ExitStack() as ph:
        def sb(name, cols, dt=F32):
            return ph.enter_context(nc.sbuf_tensor(f"{name}F{Sem._uid[0]}", [128, cols], dt))
        xg = sb("xg", TPG * D)
        xqt = sb("xqt", D)
        xq_fmh = sb("xqfh", 8 * GRP, BF16)
        xq_fml = sb("xqfl", 8 * GRP, BF16)
        scr = _quant_scr(sb, D)
        wgbh = sb("wgbh", 2 * 8 * 512, BF16)
        wgbl = sb("wgbl", 2 * 8 * 512, BF16)
        wubh = sb("wubh", 2 * 8 * 512, BF16)
        wubl = sb("wubl", 2 * 8 * 512, BF16)
        sg = sb("sg", 2 * 512)
        hbuf = sb("hbuf", 512)
        hqc = sb("hqc", 2 * 512)
        hq_fmh = sb("hqfh", 22 * GRP, BF16)
        hq_fml = sb("hqfl", 22 * GRP, BF16)
        wdbh = sb("wdbh", 2 * 512, BF16)
        wdbl = sb("wdbl", 2 * 512, BF16)
        stats = sb("stats", 16)
        outt = sb("outt", TPG * D)
        s_xg = b.chan(ph, "xg4")
        s_wg = [b.chan(ph, "wg4a"), b.chan(ph, "wg4b")]
        s_wd = [b.chan(ph, "wd4a"), b.chan(ph, "wd4b")]
        s_out = b.chan(ph, "out4", sw=True)

        wgT3 = C["scr_wgT"].rearrange("(u p) n -> p u n", p=128)
        wuT3 = C["scr_wuT"].rearrange("(u p) n -> p u n", p=128)
        wdT3 = C["scr_wdT"].rearrange("(u p) n -> p u n", p=128)

        for grp in range(n_grp):
            t0 = grp * TPG
            b.wait(b.se, b.t_ve)
            b.wait(b.se, b.t_se)
            b.inc(b.se.dma_start(
                out=xg.rearrange("p (tt d) -> p tt d", d=D),
                in_=src.rearrange("(tt p) d -> p tt d", p=128)[:, t0:t0 + TPG]),
                s_xg, 16)
            b.wait(b.ve, s_xg)
            for tt in range(TPG):
                b.wait(b.ve, b.t_ve)
                b.wait(b.ve, b.t_te)
                quant_tile(b, C, xqt[:], xg[:, tt * D:(tt + 1) * D], scr, D)
                xq_mark = b.t_ve.n
                for c in range(8):
                    bank = 2 + c % 2
                    b.bwait(bank)
                    b.wait(b.te, b.t_ve, xq_mark)
                    tp_pe_split(b, C, bank, 0,
                                xq_fmh[:, c * GRP + tt * 128: c * GRP + tt * 128 + 128],
                                xq_fml[:, c * GRP + tt * 128: c * GRP + tt * 128 + 128],
                                xqt[:, c * 128:(c + 1) * 128])
            xqf_mark_se = b.t_se.n
            xqf_mark_ve = b.t_ve.n
            kk = 0
            mm_marks = [0, 0]
            sg_marks = [0, 0]
            hq_marks = [0, 0]
            hqtp_marks = [0, 0]
            wg_marks = [0, 0]
            for nch, span in enumerate(NSPANS):
                n0 = nch * 512
                wpar = (nch % 2) * 8 * 512
                b.wait(b.se, b.t_te, wg_marks[nch % 2])
                for (buf_h, buf_l, w3) in ((wgbh, wgbl, wgT3), (wubh, wubl, wuT3)):
                    b.inc(b.se.dma_start(
                        out=buf_h[:, wpar:wpar + 8 * span].rearrange("p (c q) -> p c q", q=span),
                        in_=w3[:, 0:8, n0:n0 + span]), s_wg[nch % 2], 16)
                    b.inc(b.se.dma_start(
                        out=buf_l[:, wpar:wpar + 8 * span].rearrange("p (c q) -> p c q", q=span),
                        in_=w3[:, 8:16, n0:n0 + span]), s_wg[nch % 2], 16)
                def s4_mm(tt):
                    pa_, pg_ = (0, 1) if tt % 2 == 0 else (4, 5)
                    b.wait(b.te, s_wg[nch % 2])
                    b.wait(b.te, b.t_se, xqf_mark_se)
                    b.wait(b.te, b.t_ve, xqf_mark_ve)
                    b.bwait(pa_)
                    b.bwait(pg_)
                    for ci in range(8):
                        mm3(b, PS[pa_][:, :span],
                            xq_fmh[:, ci * GRP + tt * 128: ci * GRP + tt * 128 + 128],
                            xq_fml[:, ci * GRP + tt * 128: ci * GRP + tt * 128 + 128],
                            wgbh[:, wpar + ci * span: wpar + (ci + 1) * span],
                            wgbl[:, wpar + ci * span: wpar + (ci + 1) * span],
                            start=(ci == 0), stop=(ci == 7))
                        ii = mm3(b, PS[pg_][:, :span],
                                 xq_fmh[:, ci * GRP + tt * 128: ci * GRP + tt * 128 + 128],
                                 xq_fml[:, ci * GRP + tt * 128: ci * GRP + tt * 128 + 128],
                                 wubh[:, wpar + ci * span: wpar + (ci + 1) * span],
                                 wubl[:, wpar + ci * span: wpar + (ci + 1) * span],
                                 start=(ci == 0), stop=(ci == 7))
                    mm_marks[tt % 2] = b.t(ii)
                    wg_marks[nch % 2] = mm_marks[tt % 2]

                def s4_quant_dve(tt):
                    pa_, pg_ = (0, 1) if tt % 2 == 0 else (4, 5)
                    par4 = (tt % 2) * 512
                    b.wait(b.se, b.t_te, mm_marks[tt % 2])
                    b.wait(b.se, b.t_ve, sg_marks[tt % 2])
                    b.bmark(pa_, b.t_se,
                            b.s(b.se.activation(out=sg[:, par4:par4 + span], in_=PS[pa_][:, :span],
                                                func=AF.Silu, bias=0.0, scale=1.0)))
                    b.wait(b.ve, b.t_se)
                    b.wait(b.ve, b.t_ve)
                    sg_marks[tt % 2] = b.v(b.ve.tensor_tensor(
                        out=hbuf[:, :span], in0=sg[:, par4:par4 + span],
                        in1=PS[pg_][:, :span], op=ALU.mult))
                    b.bmark(pg_, b.t_ve, sg_marks[tt % 2])
                    b.wait(b.ve, b.t_te, hqtp_marks[tt % 2])
                    quant_tile(b, C, hqc[:, par4:par4 + span], hbuf[:, :span],
                               scr, span)
                    hq_marks[tt % 2] = b.t_ve.n

                def s4_quant_tp(tt):
                    par4 = (tt % 2) * 512
                    b.wait(b.te, b.t_ve, hq_marks[tt % 2])
                    for c in range(span // 128):
                        bank = 6 + c % 2
                        b.bwait(bank)
                        tp_pe_split(b, C, bank, 0,
                                    hq_fmh[:, (kk + c) * GRP + tt * 128:
                                           (kk + c) * GRP + tt * 128 + 128],
                                    hq_fml[:, (kk + c) * GRP + tt * 128:
                                           (kk + c) * GRP + tt * 128 + 128],
                                    hqc[:, par4 + c * 128: par4 + (c + 1) * 128])
                    hqtp_marks[tt % 2] = b.t_te.n

                for tt in range(TPG):
                    s4_mm(tt)
                    if tt > 0:
                        s4_quant_dve(tt - 1)
                    if tt > 1:
                        s4_quant_tp(tt - 2)
                s4_quant_dve(TPG - 1)
                s4_quant_tp(TPG - 2)
                s4_quant_tp(TPG - 1)
                kk += span // 128
            hqf_se, hqf_ve = b.t_se.n, b.t_ve.n
            wd_marks = [0, 0]
            for co in range(2):
                for kch in range(22):
                    par = (kch % 2) * 512
                    b.wait(b.se, b.t_te, wd_marks[kch % 2])
                    b.inc(b.se.dma_start(out=wdbh[:, par:par + 512],
                                         in_=wdT3[:, kch, co * 512:(co + 1) * 512]),
                          s_wd[kch % 2], 16)
                    b.inc(b.se.dma_start(out=wdbl[:, par:par + 512],
                                         in_=wdT3[:, 22 + kch, co * 512:(co + 1) * 512]),
                          s_wd[kch % 2], 16)
                    b.wait(b.te, s_wd[kch % 2])
                    b.wait(b.te, b.t_se, hqf_se)
                    b.wait(b.te, b.t_ve, hqf_ve)
                    if kch == 0:
                        for tt2 in range(TPG):
                            b.bwait(4 + tt2)
                    for tt in range(TPG):
                        ii = mm3(b, PS[4 + tt][:, :],
                                 hq_fmh[:, kch * GRP + tt * 128: kch * GRP + tt * 128 + 128],
                                 hq_fml[:, kch * GRP + tt * 128: kch * GRP + tt * 128 + 128],
                                 wdbh[:, par:par + 512], wdbl[:, par:par + 512],
                                 start=(kch == 0), stop=(kch == 21))
                    wd_marks[kch % 2] = b.t(ii)
                for tt in range(TPG):
                    b.wait(b.ve, b.t_te)
                    b.wait(b.ve, s_xg)
                    b.bmark(4 + tt, b.t_ve,
                            b.v(b.ve.tensor_tensor(
                                out=xg[:, tt * D + co * 512: tt * D + co * 512 + 512],
                                in0=PS[4 + tt][:, :],
                                in1=xg[:, tt * D + co * 512: tt * D + co * 512 + 512],
                                op=ALU.add)))
            b.wait(b.ve, s_out)
            layernorm_quad(b, outt[:], xg[:], outt[:], xg[:],
                           stats[:, 0:12], ng=TPG)
            b.wait(b.ge, b.t_ve)
            b.inc(b.ge.dma_start(
                out=dst.rearrange("(tt p) d -> p tt d", p=128)[:, t0:t0 + TPG],
                in_=outt.rearrange("p (tt d) -> p tt d", d=D)), s_out, 16)
        b.barrier()
        b.rmchan(s_xg, s_out, *s_wg, *s_wd)



SLOT_FNS = {}


def build_program():
    t_len = _t()
    n_seq = _nseq()
    nt = t_len // 128

    nc = bass.Bass("TRN2", target_bir_lowering=False)

    x_in = nc.dram_tensor("x_in", [n_seq * t_len, D], F32, kind="ExternalInput").ap()
    y_out = nc.dram_tensor("y_out", [n_seq * t_len, D], F32, kind="ExternalOutput").ap()

    W = {}

    def w_in(name, shape):
        W[name] = nc.dram_tensor(name, shape, F32, kind="ExternalInput").ap()

    for m in "qkvo":
        w_in("l0a_w" + m, [D, D]); w_in("l0a_b" + m, [D])
        w_in("l1a_w" + m, [D, D]); w_in("l1a_b" + m, [D])
    w_in("ff_wqkv", [2, D, 3 * FF_INNER])
    w_in("ff_qs", [2, FF_DH]); w_in("ff_ks", [2, FF_DH])
    w_in("ff_wo", [2, FF_INNER, D])
    w_in("ff_lg", [2, D]); w_in("ff_lb", [2, D])
    w_in("ff_w1", [2, D, 2 * FF_HID]); w_in("ff_b1", [2, 2 * FF_HID])
    w_in("ff_w2", [2, FF_HID, D]); w_in("ff_b2", [2, D])
    w_in("fp4_wg", [2816, D]); w_in("fp4_wu", [2816, D]); w_in("fp4_wd", [D, 2816])
    w_in("mask_u", [128, 128]); w_in("mask_l", [128, 128]); w_in("ident", [128, 128])
    for nm in ("rc1", "rs1", "rc2", "rs2"):
        w_in(nm + "_hi", [t_len, FF_DH]); w_in(nm + "_lo", [t_len, FF_DH])

    ng = nt // 4
    scr_qk = nc.dram_tensor("scr_qk", [4 * ng * 128, 8 * 512], BF16, kind="Internal").ap()
    scr_vt = nc.dram_tensor("scr_vt", [2 * nt * 128, D], BF16, kind="Internal").ap()
    scr_a = nc.dram_tensor("scr_a", [t_len, D], F32, kind="Internal").ap()
    scr2_xa = nc.dram_tensor("scr2_xa", [t_len, D], F32, kind="Internal").ap()
    scr2_y = nc.dram_tensor("scr2_y", [t_len, D], F32, kind="Internal").ap()
    scr_w1h = nc.dram_tensor("scr_w1h", [2 * 1024, 5460], BF16, kind="Internal").ap()
    scr_wqh = nc.dram_tensor("scr_wqh", [2 * 1024, 1536], BF16, kind="Internal").ap()
    scr_wql = nc.dram_tensor("scr_wql", [2 * 1024, 1536], BF16, kind="Internal").ap()
    scr_woh = nc.dram_tensor("scr_woh", [2 * 512, 1024], BF16, kind="Internal").ap()
    scr_wol = nc.dram_tensor("scr_wol", [2 * 512, 1024], BF16, kind="Internal").ap()
    scr_w1l = nc.dram_tensor("scr_w1l", [2 * 1024, 5460], BF16, kind="Internal").ap()
    scr_w2h = nc.dram_tensor("scr_w2h", [2 * 2730, 1024], BF16, kind="Internal").ap()
    scr_w2l = nc.dram_tensor("scr_w2l", [2 * 2730, 1024], BF16, kind="Internal").ap()
    scr_v3 = nc.dram_tensor("scr_v3", [2 * nt * 128, 1040], BF16, kind="Internal").ap()
    scr_z = nc.dram_tensor("scr_z", [16, 512], F32, kind="Internal").ap()
    scr_wgT = nc.dram_tensor("scr_wgT", [16 * 128, 2816], BF16, kind="Internal").ap()
    scr_wuT = nc.dram_tensor("scr_wuT", [16 * 128, 2816], BF16, kind="Internal").ap()
    scr_wdT = nc.dram_tensor("scr_wdT", [44 * 128, 1024], BF16, kind="Internal").ap()
    scr_b2 = nc.dram_tensor("scr_b", [t_len, D], F32, kind="Internal").ap()

    b = Bld(nc)
    stk = b.stk

    PS = [stk.enter_context(nc.psum_tensor(f"ps{i}", [128, 512], F32))
          for i in range(8)]
    cpool = stk.enter_context(nc.sbuf_tensor("cpool", [128, 8], F32))
    for j, v in enumerate((1e-5, 1.0 / 1024.0, 1e-6, 1e-12)):
        b.v(b.ve.memset(cpool[:, j:j + 1], v))
        nc.const_aps.register(F32, v, cpool[:, j:j + 1])
    b.v(b.ve.memset(cpool[:, 6:7], 6.0))
    b.wait(b.ve, b.t_ve)
    b.v(b.ve.reciprocal(cpool[:, 7:8], cpool[:, 6:7]))
    maskU = stk.enter_context(nc.sbuf_tensor("maskU", [128, 128], F32))
    maskL = stk.enter_context(nc.sbuf_tensor("maskL", [128, 128], F32))
    ident = stk.enter_context(nc.sbuf_tensor("ident_sb", [128, 128], F32))
    b.inc(b.sy.dma_start(out=maskU[:], in_=W["mask_u"][:]), b.q_sy, 16)
    b.inc(b.sy.dma_start(out=maskL[:], in_=W["mask_l"][:]), b.q_sy, 16)
    b.inc(b.sy.dma_start(out=ident[:], in_=W["ident"][:]), b.q_sy, 16)

    C = dict(nc=nc, W=W, t_len=t_len, nt=nt, PS=PS, maskU=maskU, maskL=maskL,
             ident=ident,
             rsix=cpool[:, 7:8],
             scr_qk=scr_qk, scr_vt=scr_vt,
             scr_wgT=scr_wgT, scr_wuT=scr_wuT, scr_wdT=scr_wdT, scr_v3=scr_v3, scr_z=scr_z,
             scr2_xa=scr2_xa, scr2_y=scr2_y, scr_w1h=scr_w1h, scr_w1l=scr_w1l,
             scr_wqh=scr_wqh, scr_wql=scr_wql, scr_woh=scr_woh, scr_wol=scr_wol,
             scr_w2h=scr_w2h, scr_w2l=scr_w2l)

    slot_fns = {1: slot1, 2: slot2, 3: slot3, 4: slot4}
    slot_fns.update(SLOT_FNS)
    for s in range(n_seq):
        src = x_in.rearrange("(s2 t) d -> s2 t d", s2=n_seq)[s] if n_seq > 1 else x_in
        y_rows = y_out.rearrange("(s2 t) d -> s2 t d", s2=n_seq)[s] if n_seq > 1 else y_out
        active = [k for k in (1, 2, 3, 4) if k in DEV_SLOTS and k in slot_fns]
        cur = src
        for j, sl in enumerate(active):
            nxt = y_rows if j == len(active) - 1 else (scr_a if j % 2 == 0 else scr_b2)
            slot_fns[sl](b, C, cur, nxt)
            cur = nxt
        b.barrier()

    b.ge.wait_ge(b.q_ge.h, b.q_ge.n)
    return nc



def _prep_inputs(inputs, t_len, n_seq, n_cores):
    rot = _rotary_tables(t_len)
    mu, ml = _masks()
    base = {
        "ff_wqkv": np.ascontiguousarray(inputs["L0_ffn_wqkv"], np.float32),
        "ff_qs": np.ascontiguousarray(inputs["L0_ffn_q_scale"], np.float32),
        "ff_ks": np.ascontiguousarray(inputs["L0_ffn_k_scale"], np.float32),
        "ff_wo": np.ascontiguousarray(inputs["L0_ffn_wo"], np.float32),
        "ff_lg": np.ascontiguousarray(inputs["L0_ffn_ln_g"], np.float32),
        "ff_lb": np.ascontiguousarray(inputs["L0_ffn_ln_b"], np.float32),
        "ff_w1": np.ascontiguousarray(inputs["L0_ffn_w1"], np.float32),
        "ff_b1": np.ascontiguousarray(inputs["L0_ffn_b1"], np.float32),
        "ff_w2": np.ascontiguousarray(inputs["L0_ffn_w2"], np.float32),
        "ff_b2": np.ascontiguousarray(inputs["L0_ffn_b2"], np.float32),
        "fp4_wg": np.ascontiguousarray(inputs["L1_ffn_w_gate"], np.float32),
        "fp4_wu": np.ascontiguousarray(inputs["L1_ffn_w_up"], np.float32),
        "fp4_wd": np.ascontiguousarray(inputs["L1_ffn_w_down"], np.float32),
        "mask_u": mu, "mask_l": ml,
        "ident": np.eye(128, dtype=np.float32),
    }
    for m in "qkvo":
        base["l0a_w" + m] = np.ascontiguousarray(inputs["L0_attn_W" + m], np.float32)
        base["l0a_b" + m] = np.ascontiguousarray(inputs["L0_attn_b" + m], np.float32)
        base["l1a_w" + m] = np.ascontiguousarray(inputs["L1_attn_w" + m], np.float32)
        base["l1a_b" + m] = np.ascontiguousarray(inputs["L1_attn_b" + m], np.float32)
    base.update(rot)

    x = np.asarray(inputs["L0_attn_x"], dtype=np.float32)
    in_maps = []
    for c in range(n_cores):
        m = dict(base)
        xs = x[c * n_seq:(c + 1) * n_seq, :t_len, :]
        m["x_in"] = np.ascontiguousarray(xs.reshape(n_seq * t_len, D))
        in_maps.append(m)
    return in_maps


def kernel(**inputs):
    t_len = _t()
    n_seq = _nseq()
    n_cores = DEV_NCORES or N_CORES
    nc = build_program()
    in_maps = _prep_inputs(inputs, t_len, n_seq, n_cores)
    res = run_bass_kernel_spmd(nc, in_maps, core_ids=list(range(n_cores)))
    outs = [r["y_out"].reshape(n_seq, t_len, D) for r in res.results]
    return np.concatenate(outs, axis=0)
